# version 01
# baseline (speedup 1.0000x reference)
import sys
sys.path.insert(0, "/opt/trn_rl_repo")
import numpy as np
import concourse.bass as bass
import concourse.mybir as mybir

F32 = mybir.dt.float32
BF16 = mybir.dt.bfloat16
AF = mybir.ActivationFunctionType
ALU = mybir.AluOpType
AX = mybir.AxisListType

ENGINES = ("tensor", "vector", "scalar", "gpsimd", "sync")


class Chan:
    def __init__(self, nc, name):
        self.sem = nc.alloc_semaphore(name)
        self.name = name
        self.n = 0

    def tick(self, k=1):
        self.n += k
        return self.n


class Ctx:
    def __init__(self, nc):
        self.nc = nc
        self.q = {e: [] for e in ENGINES}
        self.chans = {}

    def chan(self, name):
        if name not in self.chans:
            self.chans[name] = Chan(self.nc, name)
        return self.chans[name]

    def on(self, eng, fn):
        self.q[eng].append(fn)

    def emit(self, eng, maker, ch=None, k=1):
        if ch is None:
            self.on(eng, maker)
            return None
        tgt = ch.tick(k)
        def fn(e, maker=maker, ch=ch, k=k):
            maker(e).then_inc(ch.sem, k)
        self.on(eng, fn)
        return tgt

    def wait(self, eng, ch, val=None):
        v = ch.n if val is None else val
        if v < 0:
            return
        self.on(eng, lambda e, s=ch.sem, v=v: e.wait_ge(s, v))

    def build(self):
        nc = self.nc
        with nc.Block() as blk:
            for name in ENGINES:
                fns = self.q[name]
                def body(e, fns=fns):
                    for f in fns:
                        f(e)
                getattr(blk, name)(body)
        self.q = {e: [] for e in ENGINES}



def pack_lhsT(w, d_in=None):
    d_in, f = w.shape
    assert d_in % 128 == 0
    nd = d_in // 128
    nf = (f + 127) // 128
    if f % 128:
        w = np.concatenate([w, np.zeros((d_in, nf * 128 - f), w.dtype)], axis=1)
    a = w.reshape(nd, 128, nf, 128).transpose(2, 1, 0, 3)
    return np.ascontiguousarray(a.reshape(nf, 128, nd * 128))


def pack_rhs(w):
    f, n = w.shape
    nf = (f + 127) // 128
    if f % 128:
        w = np.concatenate([w, np.zeros((nf * 128 - f, n), w.dtype)], axis=0)
    return np.ascontiguousarray(w.reshape(nf, 128, n))


def tm_view(x):
    T, d = x.shape
    assert T % 128 == 0
    return np.ascontiguousarray(x.reshape(T // 128, 128, d))



"""Reusable emitters: LN prologue (stage->LN->xacc/xnfm_d/fp32-routing),
top-2 softmax, and the dense-MoE expert loop (relu_bias | swiglu).

Buffer container B holds all SBUF/PSUM handles + channels so emitters can be
chained into one program (and later fused across layers).
"""


class B:
    pass


def alloc_common(nc, ctx, NT, D=1024, n_e_max=8, nf_max=32, TB=1024):
    nd = D // 128
    b = B()
    b.nc, b.ctx = nc, ctx
    b.NT, b.D, b.nd, b.TB = NT, D, nd, TB
    b.NWR = 3
    b.xacc = nc.alloc_sbuf_tensor("xacc", [128, NT * D], BF16)
    b.hid = nc.alloc_sbuf_tensor("hid", [128, 32 * 1024], BF16)
    b.sqscr = b.hid
    b.stage = [nc.alloc_sbuf_tensor(f"stage{i}", [128, D], F32) for i in range(2)]
    b.xtr = [nc.alloc_sbuf_tensor(f"xtr{i}", [128, nd * 128], BF16) for i in range(2)]
    b.xtr32 = nc.alloc_sbuf_tensor("xtr32", [128, nd * 128], F32)
    b.identsb = nc.alloc_sbuf_tensor("identsb", [128, 128], F32)
    b.ones1 = nc.alloc_sbuf_tensor("ones1", [1, 128], BF16)
    b.epsc = nc.alloc_sbuf_tensor("epsc", [128, 1], F32)
    
    b.stats = nc.alloc_sbuf_tensor("stats", [128, max(6 * NT, 192)], F32)
    b.lg = nc.alloc_sbuf_tensor("lg", [128, NT * n_e_max], F32)
    b.combine = nc.alloc_sbuf_tensor("combine", [128, NT * n_e_max], F32)
    b.sm1 = nc.alloc_sbuf_tensor("sm1", [128, NT], F32)
    b.sm2 = nc.alloc_sbuf_tensor("sm2", [128, NT], F32)
    b.sden = nc.alloc_sbuf_tensor("sden", [128, NT], F32)
    b.scr8 = nc.alloc_sbuf_tensor("scr8", [128, NT * n_e_max], F32)
    b.pool = [nc.alloc_psum_tensor(f"pool{i}", [128, 512], F32) for i in range(7)]
    b.poolB = nc.alloc_psum_tensor("poolB", [128, 1024], BF16)
    b.pe = ctx.chan("pe"); b.ve = ctx.chan("ve"); b.se = ctx.chan("se")
    b.gst = ctx.chan("gst")
    b.ch_stage = [ctx.chan(f"c_stg{i}") for i in range(2)]
    b.ch_xst = ctx.chan("c_xst")
    b.ch_ident = ctx.chan("c_ident")
    b.ch_cst = [ctx.chan(f"c_cst{i}") for i in range(4)]
    b.ch_cstg = [ctx.chan(f"c_cstg{i}") for i in range(2)]
    return b


def emit_consts(b, id_d):
    ctx = b.ctx
    b.ones_ve = ctx.emit("vector", lambda v: v.memset(b.ones1[:, :], 1.0), b.ve)
    ctx.emit("vector", lambda v: v.memset(b.epsc[:, :], 1e-5), b.ve)
    ctx.emit("sync", lambda s: s.dma_start(b.identsb[:, :], id_d), b.ch_ident, 16)


def prologue(b, x_d, xnfm_d, gw32=None, gw_E=0, ln=True):
    ctx = b.ctx
    NT, nd, D, TPB = b.NT, b.nd, b.D, b.TB // 128
    stage, pool, xtr, xtr32 = b.stage, b.pool, b.xtr, b.xtr32
    stage_free_pe = {}
    xst_tgt = {}
    tr_pool_free = {0: b.ve.n, 1: b.ve.n}
    route_pe = b.pe.n
    for t in range(NT):
        if t >= 2:
            ctx.wait("sync", b.pe, stage_free_pe[t - 2])
        st_tgt = ctx.emit("sync", lambda s, t=t: s.dma_start(
            stage[t % 2][:, :], x_d[t]), b.ch_stage[t % 2], 16)
        ctx.wait("vector", b.ch_stage[t % 2], st_tgt)
        def mkstat(t):
            def fn(v):
                v.tensor_copy(b.xacc[:, t * D:(t + 1) * D], stage[t % 2][:, :])
                if ln:
                    inst = v.tensor_reduce(b.stats[:, t:t + 1], stage[t % 2][:, :],
                                           AX.X, ALU.add, negate=True)
                else:
                    inst = v.engine_nop()
                return inst
            return fn
        ctx.emit("vector", mkstat(t), b.ve)
        if ln:
            ctx.wait("scalar", b.ch_stage[t % 2], st_tgt)
            ctx.wait("scalar", b.ve)
            def mksq(t):
                def fn(sc):
                    sc.activation(b.sqscr[:, 0:D], stage[t % 2][:, :], AF.Square,
                                  accum_out=b.stats[:, NT + t:NT + t + 1])
                    sc.activation(b.stats[:, 2 * NT + t:2 * NT + t + 1],
                                  b.stats[:, t:t + 1], AF.Copy, scale=1.0 / D)
                    return sc.activation(b.stats[:, 3 * NT + t:3 * NT + t + 1],
                                         b.stats[:, t:t + 1], AF.Square,
                                         scale=1.0 / D)
                return fn
            ctx.emit("scalar", mksq(t), b.se)
            ctx.wait("vector", b.se)
            ctx.emit("vector", lambda v, t=t: v.scalar_tensor_tensor(
                b.stats[:, 4 * NT + t:4 * NT + t + 1],
                b.stats[:, NT + t:NT + t + 1], 1.0 / D,
                b.stats[:, 3 * NT + t:3 * NT + t + 1],
                ALU.mult, ALU.subtract), b.ve)
            ctx.wait("scalar", b.ve)
            def mkrstd(t):
                def fn(sc):
                    sc.activation(b.stats[:, 5 * NT + t:5 * NT + t + 1],
                                  b.stats[:, 4 * NT + t:4 * NT + t + 1],
                                  AF.Ln, bias=b.epsc[:, 0:1])
                    sc.activation(b.sqscr[:, 0:512], stage[t % 2][:, 0:512], AF.Copy)
                    return sc.activation(b.stats[:, 5 * NT + t:5 * NT + t + 1],
                                         b.stats[:, 5 * NT + t:5 * NT + t + 1],
                                         AF.Exp, scale=-0.5)
                return fn
            ctx.emit("scalar", mkrstd(t), b.se)
            ctx.wait("vector", b.se)
            ctx.emit("vector", lambda v, t=t: v.tensor_scalar(
                stage[t % 2][:, :], stage[t % 2][:, :],
                b.stats[:, 2 * NT + t:2 * NT + t + 1],
                b.stats[:, 5 * NT + t:5 * NT + t + 1],
                ALU.add, ALU.mult), b.ve)
        apply_ve = b.ve.n
        for grp in range(2):
            ctx.wait("tensor", b.ve, apply_ve)
            if t == 0 and grp == 0:
                ctx.wait("tensor", b.ch_ident)
            ctx.wait("tensor", b.ve, tr_pool_free[grp])
            def mk(tt, gg):
                def f(te):
                    for c in range(4):
                        inst = te.transpose(
                            pool[gg][:, c * 128:(c + 1) * 128],
                            stage[tt % 2][:, (gg * 4 + c) * 128:(gg * 4 + c + 1) * 128],
                            b.identsb[:, :])
                    return inst
                return f
            ctx.emit("tensor", mk(t, grp), b.pe)
            if grp == 1:
                stage_free_pe[t] = b.pe.n
            ctx.wait("vector", b.pe)
            if grp == 0:
                if t >= 2:
                    ctx.wait("vector", b.ch_xst, xst_tgt[t - 2])
                if t >= 1:
                    ctx.wait("vector", b.pe, route_pe)
            def mkevtr(t, grp):
                def fn(v):
                    v.tensor_copy(xtr32[:, grp * 512:(grp + 1) * 512], pool[grp][:, :])
                    return v.tensor_copy(
                        xtr[t % 2][:, grp * 512:(grp + 1) * 512], pool[grp][:, :])
                return fn
            tr_pool_free[grp] = ctx.emit("vector", mkevtr(t, grp), b.ve)
        xtr_ve = b.ve.n
        if gw32 is not None:
            E = gw_E
            ctx.wait("tensor", b.ve, xtr_ve)
            if t == 0:
                ctx.wait("tensor", b.ch_cst[0])
            def mkroute(t):
                def fn(te):
                    for c in range(nd):
                        inst = te.matmul(pool[2][:, t * E:(t + 1) * E],
                                         xtr32[:, c * 128:(c + 1) * 128],
                                         gw32[:, c * E:(c + 1) * E],
                                         start=(c == 0), stop=(c == nd - 1))
                    return inst
                return fn
            route_pe = ctx.emit("tensor", mkroute(t), b.pe)
        tb, toff = t // TPB, t % TPB
        ctx.wait("sync", b.ve, xtr_ve)
        if t >= 1:
            ctx.wait("sync", b.ch_xst, xst_tgt[t - 1])
        xst_tgt[t] = ctx.emit("sync", lambda s, tb=tb, toff=toff, t=t: s.dma_start(
            xnfm_d[tb][:, :, toff * 128:(toff + 1) * 128].rearrange("c p j -> p c j"),
            xtr[t % 2][:, :].rearrange("p (c j) -> p c j", j=128)), b.ch_xst, 16)
    b.xst_total = b.ch_xst.n


def softmax_top2(b, E, renorm=True):
    ctx, NT = b.ctx, b.NT
    lg, combine, scr8 = b.lg, b.combine, b.scr8
    ctx.wait("vector", b.pe)
    ctx.emit("vector", lambda v: v.tensor_copy(lg[:, 0:NT * E], b.pool[2][:, 0:NT * E]), b.ve)

    def sm_a(v):
        v.tensor_reduce(b.sm1[:, 0:NT], lg[:, 0:NT * E].rearrange("p (t e) -> p t e", e=E),
                        AX.X, ALU.max, negate=True)
        for t in range(NT):
            sl = slice(t * E, (t + 1) * E)
            v.tensor_scalar(lg[:, sl], lg[:, sl], b.sm1[:, t:t + 1], None, ALU.add)
        for _ in range(4):
            v.engine_nop()
        v.tensor_single_scalar(scr8[:, 0:NT * E], lg[:, 0:NT * E], 0.0, ALU.is_ge)
        for _ in range(4):
            v.engine_nop()
        v.scalar_tensor_tensor(scr8[:, 0:NT * E], scr8[:, 0:NT * E], -1e30,
                               lg[:, 0:NT * E], ALU.mult, ALU.add)
        v.tensor_reduce(b.sm2[:, 0:NT],
                        scr8[:, 0:NT * E].rearrange("p (t e) -> p t e", e=E),
                        AX.X, ALU.max)
        for t in range(NT):
            sl = slice(t * E, (t + 1) * E)
            inst = v.tensor_scalar(scr8[:, sl], lg[:, sl], b.sm2[:, t:t + 1], None, ALU.is_ge)
        return inst
    ctx.emit("vector", sm_a, b.ve)
    ctx.wait("scalar", b.ve)
    ctx.emit("scalar", lambda sc: sc.activation(
        combine[:, 0:NT * E], lg[:, 0:NT * E], AF.Exp), b.se)
    ctx.wait("vector", b.se)
    def sm_c1(v):
        if renorm:
            v.tensor_mul(combine[:, 0:NT * E], combine[:, 0:NT * E], scr8[:, 0:NT * E])
        return v.tensor_reduce(b.sden[:, 0:NT],
                               combine[:, 0:NT * E].rearrange("p (t e) -> p t e", e=E),
                               AX.X, ALU.add)
    ctx.emit("vector", sm_c1, b.ve)
    ctx.wait("scalar", b.ve)
    def sm_recip(sc):
        sc.activation(b.sm2[:, 0:NT], b.sden[:, 0:NT], AF.Ln)
        return sc.activation(b.sden[:, 0:NT], b.sm2[:, 0:NT], AF.Exp, scale=-1.0)
    ctx.emit("scalar", sm_recip, b.se)
    ctx.wait("vector", b.se)
    def sm_c2(v):
        if not renorm:
            v.tensor_mul(combine[:, 0:NT * E], combine[:, 0:NT * E], scr8[:, 0:NT * E])
        for t in range(NT):
            sl = slice(t * E, (t + 1) * E)
            inst = v.tensor_scalar(combine[:, sl], combine[:, sl],
                                   b.sden[:, t:t + 1], None, ALU.mult)
        return inst
    ctx.emit("vector", sm_c2, b.ve)
    b.combine_ve = b.ve.n


def moe_loop(b, cfg):
    ctx, nc = b.ctx, b.nc
    E, H, TB, D, nd = cfg["E"], cfg["H"], b.TB, b.D, b.nd
    NT, TPB = b.NT, b.TB // 128
    NTB = NT // TPB
    nf = H // 128
    P1W = min(512, TB)
    nhalves = TB // P1W
    T4G = 2
    nhalfg = (TPB + T4G - 1) // T4G
    act = cfg["act"]
    pool = b.pool
    if not hasattr(b, "xnfm"):
        b.xnfm = [nc.alloc_sbuf_tensor(f"xnfm{i}", [128, nd * TB], BF16) for i in range(2)]
        b.w1r = [nc.alloc_sbuf_tensor(f"w1r{i}", [128, nd * 128], BF16) for i in range(b.NWR)]
        b.wur = [nc.alloc_sbuf_tensor(f"wur{i}", [128, nd * 128], BF16) for i in range(b.NWR)]
        b.w2r = [nc.alloc_sbuf_tensor(f"w2r{i}", [128, D], BF16) for i in range(b.NWR)]
        b.b2r = [nc.alloc_sbuf_tensor("b2r0", [1, D], BF16)]
        b.ch_xnfm = [ctx.chan(f"c_xnf{i}") for i in range(2)]
        b.ch_w1 = [ctx.chan(f"c_w1_{i}") for i in range(b.NWR)]
        b.ch_wu = [ctx.chan(f"c_wu_{i}") for i in range(b.NWR)]
        b.ch_w2 = [ctx.chan(f"c_w2_{i}") for i in range(b.NWR)]
        b.ch_b2 = [ctx.chan("c_b2_0")]
    assert nf * TB <= 32 * 1024
    hid, xnfm = b.hid, b.xnfm

    NWR = b.NWR
    w1_last = [b.pe.n] * NWR
    wu_last = [b.pe.n] * NWR
    w2_last = [b.pe.n] * NWR
    b2_last = [b.pe.n]
    xnfm_last = [b.pe.n] * 2
    pool_free = [b.ve.n] * 8
    pool01_free_se = b.se.n
    ph2_done = b.pe.n
    first_xnfm = True

    for e in range(E):
        if act == "relu_bias":
            ctx.wait("gpsimd", b.pe, b2_last[0])
            b2_tgt = ctx.emit("gpsimd", lambda g, e=e: g.dma_start(
                b.b2r[0][:, :], cfg["b2_d"][0:1, e * D:(e + 1) * D]), b.ch_b2[0], 16)
        for tb in range(NTB):
            buf = (e * NTB + tb) % 2
            ctx.wait("sync", b.pe, xnfm_last[buf])
            if first_xnfm:
                ctx.wait("sync", b.ch_xst, b.xst_total)
                first_xnfm = False
            xnfm_tgt = ctx.emit("sync", lambda s, buf=buf, tb=tb: s.dma_start(
                xnfm[buf][:, :].rearrange("p (c t) -> p c t", c=nd),
                cfg["xnfm_d"][tb].rearrange("c p t -> p c t")), b.ch_xnfm[buf], 16)
            first_evac = True
            for f in range(nf):
                slot = f % NWR
                if act == "relu_bias":
                    ctx.wait("gpsimd", b.pe, w1_last[slot])
                    w1_tgt = ctx.emit("gpsimd", lambda g, e=e, f=f, slot=slot: g.dma_start(
                        b.w1r[slot][:, :], cfg["w1_d"][e, f]), b.ch_w1[slot], 16)
                    for half in range(nhalves):
                        gi = f * nhalves + half
                        pi = 5 + gi % 2
                        ctx.wait("tensor", b.ch_w1[slot], w1_tgt)
                        ctx.wait("tensor", b.ch_xnfm[buf], xnfm_tgt)
                        ctx.wait("tensor", b.ve, pool_free[pi])
                        def mk1(f, half, pi, buf, slot):
                            def fn(te):
                                for c in range(nd):
                                    inst = te.matmul(
                                        pool[pi][:, 0:P1W], b.w1r[slot][:, c * 128:(c + 1) * 128],
                                        xnfm[buf][:, c * TB + half * P1W:c * TB + (half + 1) * P1W],
                                        start=(c == 0), stop=(c == nd - 1))
                                return inst
                            return fn
                        ctx.emit("tensor", mk1(f, half, pi, buf, slot), b.pe)
                        w1_last[slot] = b.pe.n
                        xnfm_last[buf] = b.pe.n
                        ctx.wait("vector", b.pe)
                        if first_evac:
                            ctx.wait("vector", b.pe, ph2_done)
                            ctx.wait("vector", cfg["ch_b1"])
                            first_evac = False
                        pool_free[pi] = ctx.emit(
                            "vector", lambda v, f=f, half=half, pi=pi, e=e: v.tensor_scalar(
                                hid[:, f * TB + half * P1W:f * TB + (half + 1) * P1W],
                                pool[pi][:, 0:P1W],
                                cfg["b1sb"][:, e * nf + f:e * nf + f + 1], 0.0,
                                ALU.add, ALU.max), b.ve)
                else:
                    ctx.wait("gpsimd", b.pe, w1_last[slot])
                    wg_tgt = ctx.emit("gpsimd", lambda g, e=e, f=f, slot=slot: g.dma_start(
                        b.w1r[slot][:, :], cfg["g_d"][e, f]), b.ch_w1[slot], 16)
                    ctx.wait("gpsimd", b.pe, wu_last[slot])
                    wu_tgt = ctx.emit("gpsimd", lambda g, e=e, f=f, slot=slot: g.dma_start(
                        b.wur[slot][:, :], cfg["u_d"][e, f]), b.ch_wu[slot], 16)
                    for half in range(nhalves):
                        ctx.wait("tensor", b.ch_w1[slot], wg_tgt)
                        ctx.wait("tensor", b.ch_wu[slot], wu_tgt)
                        ctx.wait("tensor", b.ch_xnfm[buf], xnfm_tgt)
                        ctx.wait("tensor", b.se, pool01_free_se)
                        ctx.wait("tensor", b.ve, pool_free[6])
                        def mkgu(f, half, buf, slot):
                            def fn(te):
                                for c in range(nd):
                                    te.matmul(
                                        pool[5][:, 0:P1W], b.w1r[slot][:, c * 128:(c + 1) * 128],
                                        xnfm[buf][:, c * TB + half * P1W:c * TB + (half + 1) * P1W],
                                        start=(c == 0), stop=(c == nd - 1))
                                for c in range(nd):
                                    inst = te.matmul(
                                        pool[6][:, 0:P1W], b.wur[slot][:, c * 128:(c + 1) * 128],
                                        xnfm[buf][:, c * TB + half * P1W:c * TB + (half + 1) * P1W],
                                        start=(c == 0), stop=(c == nd - 1))
                                return inst
                            return fn
                        ctx.emit("tensor", mkgu(f, half, buf, slot), b.pe)
                        w1_last[slot] = b.pe.n
                        wu_last[slot] = b.pe.n
                        xnfm_last[buf] = b.pe.n
                        sl_h = slice(f * TB + half * P1W, f * TB + (half + 1) * P1W)
                        ctx.wait("scalar", b.pe)
                        if first_evac:
                            ctx.wait("scalar", b.pe, ph2_done)
                            first_evac = False
                        pool01_free_se = ctx.emit("scalar", lambda sc, sl_h=sl_h: sc.activation(
                            hid[:, sl_h], pool[5][:, 0:P1W], AF.Silu), b.se)
                        ctx.wait("vector", b.se, pool01_free_se)
                        pool_free[6] = ctx.emit("vector", lambda v, sl_h=sl_h: v.tensor_tensor(
                            hid[:, sl_h], pool[6][:, 0:P1W], hid[:, sl_h], ALU.mult), b.ve)
            hid_ve = b.ve.n
            for halfg in range(nhalfg):
                t4 = min(T4G, TPB - halfg * T4G)
                for f in range(nf):
                    slot = f % NWR
                    ctx.wait("gpsimd", b.pe, w2_last[slot])
                    w2_tgt = ctx.emit("gpsimd", lambda g, e=e, f=f, slot=slot: g.dma_start(
                        b.w2r[slot][:, :], cfg["w2_d"][e, f]), b.ch_w2[slot], 16)
                    ctx.wait("tensor", b.ch_w2[slot], w2_tgt)
                    if f == 0:
                        ctx.wait("tensor", b.ve, hid_ve)
                        if act == "swiglu":
                            ctx.wait("tensor", b.ve, pool_free[6])
                        if act == "relu_bias":
                            ctx.wait("tensor", b.ch_b2[0], b2_tgt)
                            ctx.wait("tensor", b.ve, b.ones_ve)
                        for q in range(2 * t4):
                            ctx.wait("tensor", b.ve, pool_free[q])
                    def mk2(e, f, halfg, t4, slot):
                        def fn(te):
                            for t4i in range(t4):
                                tsub = halfg * T4G + t4i
                                for nh in range(2):
                                    if f == 0 and act == "relu_bias":
                                        te.matmul(
                                            pool[t4i * 2 + nh][:, :], b.ones1[0:1, :],
                                            b.b2r[0][0:1, nh * 512:(nh + 1) * 512],
                                            start=True, stop=False)
                                    inst = te.matmul(
                                        pool[t4i * 2 + nh][:, :],
                                        hid[:, f * TB + tsub * 128:f * TB + (tsub + 1) * 128],
                                        b.w2r[slot][:, nh * 512:(nh + 1) * 512],
                                        start=(f == 0 and act != "relu_bias"),
                                        stop=(f == nf - 1))
                            return inst
                        return fn
                    ctx.emit("tensor", mk2(e, f, halfg, t4, slot), b.pe)
                    w2_last[slot] = b.pe.n
                    if f == 0 and act == "relu_bias":
                        b2_last[0] = b.pe.n
                ph2_done = b.pe.n
                ctx.wait("vector", b.pe)
                if e == 0 and tb == 0 and halfg == 0:
                    ctx.wait("vector", b.ve, b.combine_ve)
                def mkev(e, tb, halfg, t4):
                    def fn(v):
                        for t4i in range(t4):
                            tsub = halfg * T4G + t4i
                            gt = tb * TPB + tsub
                            for nh in range(2):
                                dst = b.xacc[:, gt * D + nh * 512:gt * D + (nh + 1) * 512]
                                cap = b.combine[:, gt * E + e:gt * E + e + 1]
                                if e == 0 and cfg.get("first_acc", False):
                                    inst = v.tensor_scalar(
                                        dst, pool[t4i * 2 + nh][:, :], cap, None, ALU.mult)
                                else:
                                    inst = v.scalar_tensor_tensor(
                                        dst, pool[t4i * 2 + nh][:, :], cap, dst,
                                        ALU.mult, ALU.add)
                        return inst
                    return fn
                ctx.emit("vector", mkev(e, tb, halfg, t4), b.ve)
                for q in range(2 * t4):
                    pool_free[q] = b.ve.n



"""Full-row attention emitter for L1_attn (alibi causal MHA) and L2_attn
(forgetting attention fwd + anti-causal bwd). See design notes in memory.

att_s (128,6144 bf16): Q[0:2048] | K[2048:4096] | P[4096:6144]
xtr[0]=PT_sb (128,1024 bf16); xtr[1]=V (128,1024 bf16, 16 tiles x 64)
hid[0:64, h*SL+col]: attn_fm per seq; stage rows: jrow/gcrow/urow
stats cols: mparts 0..9 | mtree 10..15 | lparts 16..25 | ltree 26..31 |
            rl 32 | bcol 33 | ebias 34
psum: pool[0..3] scores, pool[4] PV acc, pool[5,6] proj/outproj, poolB P^T.
"""


def get_alibi_slopes(n):
    import math
    def pow2_slopes(n):
        start = 2 ** (-2 ** (-(math.log2(n) - 3)))
        return [start * start ** i for i in range(n)]
    if math.log2(n).is_integer():
        return pow2_slopes(n)
    c = 2 ** math.floor(math.log2(n))
    return pow2_slopes(c) + get_alibi_slopes(2 * c)[0::2][:n - c]


def host_pack_headproj(wq_cols, wk_cols):
    D, H, dh = wq_cols.shape
    qk = np.concatenate([wq_cols, wk_cols], axis=2)
    a = qk.reshape(8, 128, H, 2 * dh).transpose(2, 1, 0, 3)
    return np.ascontiguousarray(a.reshape(H, 128, 8 * 2 * dh))


def host_pack_vslab(wv_cols):
    D, H, dh = wv_cols.shape
    v = wv_cols.reshape(8, 128, H, dh).transpose(2, 1, 0, 3)
    return np.ascontiguousarray(v.reshape(H, 128, 8 * dh))


def host_pack_wo(w_out):
    w = w_out.reshape(16, 64, 8, 128).transpose(2, 0, 1, 3)
    w = w.reshape(8, 2, 8, 64, 128).transpose(0, 1, 3, 2, 4)
    return np.ascontiguousarray(w.reshape(8, 2, 64, 8 * 128))


def host_masks():
    i = np.arange(128)[:, None]
    j = np.arange(128)[None, :]
    diag = np.where(j > i, -1e30, 0.0).astype(np.float32)
    diagT = np.where(j < i, -1e30, 0.0).astype(np.float32)
    return np.concatenate([diag, diagT], axis=1)


def alloc_attn(b, id_d):
    nc, ctx = b.nc, b.ctx
    if hasattr(b, "att_s"):
        return
    b.att_s = nc.alloc_sbuf_tensor("att_s", [128, 2048], BF16)
    b.masks = nc.alloc_sbuf_tensor("masks", [128, 256], BF16)
    b.ones32 = nc.alloc_sbuf_tensor("ones32", [1, 128], F32)
    b.negones32 = nc.alloc_sbuf_tensor("negones32", [1, 128], F32)
    b.onecol = nc.alloc_sbuf_tensor("onecol", [128, 1], F32)
    b.iotap = nc.alloc_sbuf_tensor("iotap", [128, 1], F32)
    b.identbf = nc.alloc_sbuf_tensor("identbf", [128, 128], BF16)
    b.bqk = nc.alloc_sbuf_tensor("bqk", [128, 32], F32)
    b.rowsW = nc.alloc_sbuf_tensor("rowsW", [128, 128], F32)
    b.bo_c2 = nc.alloc_sbuf_tensor("bo_c2", [128, 8], F32)
    b.ch_masks = ctx.chan("c_masks")
    b.ch_bqk = ctx.chan("c_bqk")
    b.ch_bo2 = ctx.chan("c_bo2")
    b.ch_gcrow = [ctx.chan(f"c_gcr{i}") for i in range(2)]
    b.ch_gcst = ctx.chan("c_gcst")
    b.ch_identbf = ctx.chan("c_identbf")
    b.gpe = ctx.chan("gpe")
    ctx.emit("vector", lambda v: v.memset(b.ones32[:, :], 1.0), b.ve)
    ctx.emit("vector", lambda v: v.memset(b.negones32[:, :], -1.0), b.ve)
    ctx.emit("vector", lambda v: v.memset(b.onecol[:, :], 1.0), b.ve)
    ctx.emit("vector", lambda v: v.memset(b.rowsW[64:128, 0:128], 1.0), b.ve)
    b.attconst_ve = b.ve.n
    ctx.emit("gpsimd", lambda g: g.iota(b.iotap[:, :], [[0, 1]], base=0,
                                        channel_multiplier=1,
                                        allow_small_or_imprecise_dtypes=True), b.gpe)
    ctx.emit("gpsimd", lambda g: g.dma_start(b.identbf[:, :], id_d), b.ch_identbf, 16)


def _tree(v, sums, srcs, scratch0, op):
    dst = scratch0
    cur = list(srcs)
    for _ in range(5):
        v.engine_nop()
    while len(cur) > 1:
        nxt = []
        for j in range(0, len(cur) - 1, 2):
            if op == "max":
                v.tensor_max(sums[:, dst:dst + 1], sums[:, cur[j]:cur[j] + 1],
                             sums[:, cur[j + 1]:cur[j + 1] + 1])
            else:
                v.tensor_add(sums[:, dst:dst + 1], sums[:, cur[j]:cur[j] + 1],
                             sums[:, cur[j + 1]:cur[j + 1] + 1])
            nxt.append(dst)
            dst += 1
        if len(cur) % 2 == 1:
            nxt.append(cur[-1])
        cur = nxt
        for _ in range(5):
            v.engine_nop()
    return cur[0]


def make_blocks(direction, qt, SL):
    blocks = []
    if direction == 0:
        ext0, ext1 = 0, qt * 128 + 128
        k0 = 0
        while k0 + 512 <= ext1 - 128:
            blocks.append((k0, 512, None))
            k0 += 512
        while k0 < ext1 - 128:
            blocks.append((k0, 128, None))
            k0 += 128
        blocks.append((k0, 128, "diag"))
    else:
        ext0, ext1 = qt * 128, SL
        blocks.append((ext0, 128, "diagT"))
        k0 = ext0 + 128
        while k0 < ext1 and (k0 - ext0) % 512 != 0:
            blocks.append((k0, 128, None))
            k0 += 128
        while k0 + 512 <= ext1:
            blocks.append((k0, 512, None))
            k0 += 512
        while k0 < ext1:
            blocks.append((k0, 128, None))
            k0 += 128
    return ext0, ext1, blocks


def emit_attn(b, cfg):
    ctx, nc = b.ctx, b.nc
    D, nd, TB = b.D, b.nd, b.TB
    H, dh = 16, 64
    mode = cfg["mode"]
    n_seq, seq_tb = cfg["n_seq"], cfg["seq_tb"]
    SL = seq_tb * TB
    NQT = SL // 128
    pool, poolB = b.pool, b.poolB
    att_s, stage, xtr32 = b.att_s, b.stage, b.xtr32
    hid = b.hid
    QO, KO, PO = 0, 2048, 0
    sums = b.stats
    slopes = get_alibi_slopes(H) if mode == "alibi" else None

    ctx.emit("sync", lambda s: s.dma_start(b.masks[:, :], cfg["masks_d"]), b.ch_masks, 16)
    if mode == "alibi":
        ctx.emit("gpsimd", lambda g: g.dma_start(b.bqk[:, :], cfg["bqk_d"]), b.ch_bqk, 16)
        ctx.emit("gpsimd", lambda g: g.dma_start(b.b2r[0][:, :], cfg["bv_d"]), b.ch_bqk, 16)
        ctx.emit("gpsimd", lambda g: g.dma_start(b.bo_c2[:, :], cfg["bo_d"]), b.ch_bo2, 16)
    else:
        ctx.emit("gpsimd", lambda g: g.dma_start(b.bqk[0:16, 16:17], cfg["bf_d"]), b.ch_bqk, 16)
        if not hasattr(b, "wfsb"):
            b.wfsb = nc.alloc_sbuf_tensor("wfsb", [128, 128], BF16)
            b.ch_wf = ctx.chan("c_wf")
        ctx.emit("gpsimd", lambda g: g.dma_start(b.wfsb[:, :], cfg["wf_d"]), b.ch_wf, 16)

    qk_last = [b.pe.n] * b.NWR
    v_last = [b.pe.n] * b.NWR
    wo_last = [b.pe.n] * b.NWR
    xnfm_last = [b.pe.n] * 2
    pool_free = [b.ve.n] * 7
    poolB_free = b.ve.n
    stage_pe = b.pe.n
    prev_pass_pe = b.pe.n
    jrow_ve = b.ve.n
    first = True
    xnfm_tgts = {}
    qk_tgt = {}
    v_tgt = {}
    row_tgts = {}

    def emit_fpass(s):
        nonlocal pool_free, stage_pe
        zs = [stage[0][0:16, 0:1024], stage[1][0:16, 0:1024]]
        ctx.wait("tensor", b.ch_wf)
        for piece in range(2):
            for hw in range(2):
                toff = piece * 1024 + hw * 512
                stb, off = toff // TB, toff % TB
                ctx.wait("tensor", b.ch_xnfm[stb], xnfm_tgts[stb])
                ctx.wait("tensor", b.ve, pool_free[5])
                if piece == 0 and hw == 0:
                    ctx.wait("tensor", b.pe, prev_pass_pe)
                def mkfg(stb, off):
                    def fn(te):
                        for c in range(nd):
                            inst = te.matmul(
                                pool[5][0:16, 0:512], b.wfsb[:, c * 16:(c + 1) * 16],
                                b.xnfm[stb][:, c * TB + off:c * TB + off + 512],
                                start=(c == 0), stop=(c == nd - 1))
                        return inst
                    return fn
                ctx.emit("tensor", mkfg(stb, off), b.pe)
                ctx.wait("vector", b.pe)
                if piece == 0 and hw == 0:
                    ctx.wait("vector", b.ch_bqk)
                pool_free[5] = ctx.emit("vector", lambda v, piece=piece, hw=hw: v.tensor_scalar(
                    stage[piece][0:16, hw * 512:(hw + 1) * 512], pool[5][0:16, 0:512],
                    b.bqk[0:16, 16:17], None, ALU.add), b.ve)
        for piece in range(2):
            ctx.wait("scalar", b.ve)
            def mkls(piece):
                def fn(sc):
                    sc.activation(xtr32[0:16, 0:1024], zs[piece], AF.Abs)
                    sc.activation(xtr32[0:16, 0:1024], xtr32[0:16, 0:1024],
                                  AF.Exp, scale=-1.0)
                    return sc.activation(xtr32[0:16, 0:1024], xtr32[0:16, 0:1024],
                                         AF.Ln, bias=b.onecol[0:16, 0:1])
                return fn
            ctx.emit("scalar", mkls(piece), b.se)
            ctx.wait("vector", b.se)
            def mkls2(piece):
                def fn(v):
                    v.tensor_single_scalar(zs[piece], zs[piece], 0.0, ALU.min)
                    return v.tensor_sub(zs[piece], zs[piece], xtr32[0:16, 0:1024])
                return fn
            ctx.emit("vector", mkls2(piece), b.ve)
            ctx.wait("scalar", b.ve)
        def mkscan_a(v):
            return v.tensor_tensor_scan(xtr32[0:16, 0:1024], zs[0], zs[0], 0.0,
                                        ALU.add, ALU.bypass)
        ctx.emit("vector", mkscan_a, b.ve)
        ctx.emit("vector", lambda v: v.tensor_sub(zs[0], xtr32[0:16, 0:1024], zs[0]), b.ve)
        gca_ve = b.ve.n
        ctx.wait("gpsimd", b.ve, gca_ve)
        ctx.emit("gpsimd", lambda g, s=s: g.dma_start(
            cfg["u_dram"][s, :, 0:1024], zs[0]), b.ch_gcst, 16)
        gst_a = b.ch_gcst.n
        ctx.wait("vector", b.ch_gcst, gst_a)
        def mkscan_b(v):
            return v.tensor_tensor_scan(zs[0], zs[1], zs[1],
                                        xtr32[0:16, 1023:1024], ALU.add, ALU.bypass)
        ctx.emit("vector", mkscan_b, b.ve)
        ctx.emit("vector", lambda v: v.tensor_sub(zs[1], zs[0], zs[1]), b.ve)
        ctx.emit("vector", lambda v: v.tensor_scalar(
            xtr32[0:16, 0:1024], xtr32[0:16, 0:1024], -1.0, None, ALU.mult), b.ve)
        ctx.emit("vector", lambda v: v.tensor_scalar(
            zs[0], zs[0], -1.0, None, ALU.mult), b.ve)
        gcb_ve = b.ve.n
        ctx.wait("gpsimd", b.ve, gcb_ve)
        ctx.emit("gpsimd", lambda g, s=s: g.dma_start(
            cfg["gc_dram"][s, :, 0:1024], xtr32[0:16, 0:1024]), b.ch_gcst, 16)
        ctx.emit("gpsimd", lambda g, s=s: g.dma_start(
            cfg["gc_dram"][s, :, 1024:2048], zs[0]), b.ch_gcst, 16)
        ctx.emit("gpsimd", lambda g, s=s: g.dma_start(
            cfg["u_dram"][s, :, 1024:2048], zs[1]), b.ch_gcst, 16)
        stage_pe = b.pe.n

    def emit_proj(s, h, slot):
        for part in range(2):
            for half in range(SL // 512):
                stb, off = (half * 512) // TB, (half * 512) % TB
                pi = 5 + (part * (SL // 512) + half) % 2
                ctx.wait("tensor", b.ch_w1[slot], qk_tgt[h])
                ctx.wait("tensor", b.ch_xnfm[stb], xnfm_tgts[stb])
                ctx.wait("tensor", b.ve, pool_free[pi])
                def mkqk(part, stb, off, pi, slot):
                    def fn(te):
                        for c in range(nd):
                            inst = te.matmul(
                                pool[pi][64:128, :],
                                b.w1r[slot][:, c * 128 + part * 64:c * 128 + (part + 1) * 64],
                                b.xnfm[stb][:, c * TB + off:c * TB + off + 512],
                                start=(c == 0), stop=(c == nd - 1),
                                tile_position=(0, 64))
                        return inst
                    return fn
                ctx.emit("tensor", mkqk(part, stb, off, pi, slot), b.pe)
                qk_last[slot] = b.pe.n
                xnfm_last[stb] = b.pe.n
                ctx.wait("vector", b.pe)
                if mode == "alibi" and h == 0 and part == 0 and half == 0:
                    ctx.wait("vector", b.ch_bqk)
                dst = hid[64:128, (QO if part == 0 else KO) + half * 512:
                          (QO if part == 0 else KO) + (half + 1) * 512]
                if mode == "alibi":
                    if part == 0:
                        pool_free[pi] = ctx.emit("vector",
                            lambda v, dst=dst, pi=pi, h=h: v.tensor_scalar(
                                dst, pool[pi][64:128, :], b.bqk[64:128, h:h + 1], 0.125,
                                ALU.add, ALU.mult), b.ve)
                    else:
                        pool_free[pi] = ctx.emit("vector",
                            lambda v, dst=dst, pi=pi, h=h: v.tensor_scalar(
                                dst, pool[pi][64:128, :], b.bqk[64:128, 16 + h:16 + h + 1],
                                None, ALU.add), b.ve)
                else:
                    if part == 0:
                        pool_free[pi] = ctx.emit("vector",
                            lambda v, dst=dst, pi=pi: v.tensor_scalar(
                                dst, pool[pi][64:128, :], 0.125, None, ALU.mult), b.ve)
                    else:
                        pool_free[pi] = ctx.emit("vector",
                            lambda v, dst=dst, pi=pi: v.tensor_copy(
                                dst, pool[pi][64:128, :]), b.ve)
        for g8 in range((SL + 1023) // 1024):
            ntt = min(8, SL // 128 - g8 * 8)
            pi = 5 + g8 % 2
            ctx.wait("tensor", b.ch_wu[slot], v_tgt[h])
            ctx.wait("tensor", b.ve, pool_free[pi])
            if mode == "alibi":
                ctx.wait("tensor", b.ch_bqk)
                ctx.wait("tensor", b.ve, b.ones_ve)
            def mkv(g8, pi, slot, h, ntt=ntt):
                def fn(te):
                    inst = None
                    for tt8 in range(ntt):
                        tt = g8 * 8 + tt8
                        stb, off = (tt * 128) // TB, (tt * 128) % TB
                        if mode == "alibi":
                            te.matmul(pool[pi][:, tt8 * 64:(tt8 + 1) * 64],
                                      b.ones1[0:1, :],
                                      b.b2r[0][0:1, h * 64:(h + 1) * 64],
                                      start=True, stop=False)
                        for c in range(nd):
                            inst = te.matmul(
                                pool[pi][:, tt8 * 64:(tt8 + 1) * 64],
                                b.xnfm[stb][:, c * TB + off:c * TB + off + 128],
                                b.wur[slot][:, c * 64:(c + 1) * 64],
                                start=(c == 0 and mode != "alibi"), stop=(c == nd - 1))
                    return inst
                return fn
            ctx.emit("tensor", mkv(g8, pi, slot, h), b.pe)
            v_last[slot] = b.pe.n
            xnfm_last[0] = b.pe.n
            xnfm_last[1] = b.pe.n
            ctx.wait("vector", b.pe)
            pool_free[pi] = ctx.emit("vector", lambda v, g8=g8, pi=pi, ntt=ntt: v.tensor_copy(
                b.xtr[1][:, g8 * 512:g8 * 512 + ntt * 64], pool[pi][:, 0:ntt * 64]), b.ve)

    def emit_head_attn(s, h, direction):
        nonlocal poolB_free, stage_pe, jrow_ve
        for qt in range(NQT):
            ext0, ext1, blocks = make_blocks(direction, qt, SL)
            ext = ext1 - ext0
            nb = len(blocks)
            assert nb <= 10
            for i, (k0, w, msk) in enumerate(blocks):
                r = k0 - ext0
                pj, pcol = r // 512, r % 512
                ctx.wait("tensor", b.ve, pool_free[pj])
                if mode == "fgate":
                    for piece in set([k0 // 1024, (k0 + w - 1) // 1024]):
                        ctx.wait("tensor", b.ch_gcrow[piece], row_tgts[piece])
                else:
                    ctx.wait("tensor", b.ve, jrow_ve)
                def mksc(h, qt, k0, w, pj, pcol, direction):
                    def fn(te):
                        te.matmul(pool[pj][:, pcol:pcol + w],
                                  hid[64:128, QO + qt * 128:QO + (qt + 1) * 128],
                                  hid[64:128, KO + k0:KO + k0 + w],
                                  start=True, stop=False)
                        lt = b.rowsW[64:65, 0:128]
                        segs = []
                        a0 = k0
                        while a0 < k0 + w:
                            a1 = min(k0 + w, (a0 // 1024 + 1) * 1024)
                            segs.append((a0, a1))
                            a0 = a1
                        for si, (a0, a1) in enumerate(segs):
                            inst = te.matmul(
                                pool[pj][:, pcol + (a0 - k0):pcol + (a1 - k0)],
                                lt,
                                stage[a0 // 1024][64:65, a0 % 1024:a0 % 1024 + (a1 - a0)],
                                start=False, stop=(si == len(segs) - 1))
                        return inst
                    return fn
                ctx.emit("tensor", mksc(h, qt, k0, w, pj, pcol, direction), b.pe)
                stage_pe = b.pe.n
            ctx.wait("vector", b.pe)
            ctx.wait("vector", b.ch_masks)
            def mkmask(blocks, ext0):
                def fn(v):
                    inst = v.engine_nop()
                    for (k0, w, msk) in blocks:
                        if msk is None:
                            continue
                        r = k0 - ext0
                        moff = 0 if msk == "diag" else 128
                        inst = v.tensor_tensor(
                            pool[r // 512][:, r % 512:r % 512 + 128],
                            pool[r // 512][:, r % 512:r % 512 + 128],
                            b.masks[:, moff:moff + 128], ALU.add)
                    for _ in range(4):
                        v.engine_nop()
                    return inst
                return fn
            ctx.emit("vector", mkmask(blocks, ext0), b.ve)
            def mkmp(blocks, ext0):
                def fn(v):
                    inst = None
                    for i, (k0, w, msk) in enumerate(blocks):
                        r = k0 - ext0
                        inst = v.tensor_reduce(sums[:, i:i + 1],
                                               pool[r // 512][:, r % 512:r % 512 + w],
                                               AX.X, ALU.max)
                    return inst
                return fn
            ctx.emit("vector", mkmp(blocks, ext0), b.ve)
            def mkmt(nb, h, qt):
                def fn(v):
                    mfin = _tree(v, sums, list(range(nb)), 10, "max")
                    return v.tensor_copy(sums[:, 15:16], sums[:, mfin:mfin + 1])
                return fn
            ctx.emit("vector", mkmt(nb, h, qt), b.ve)
            mt_ve = b.ve.n
            ctx.wait("scalar", b.ve, mt_ve)
            ctx.emit("scalar", lambda sc: sc.activation(
                sums[:, 34:35], sums[:, 15:16], AF.Copy, scale=-1.0), b.se)
            def mkexp(blocks, ext0):
                def fn(sc):
                    inst = None
                    for i, (k0, w, msk) in enumerate(blocks):
                        r = k0 - ext0
                        inst = sc.activation(
                            att_s[:, PO + r:PO + r + w],
                            pool[r // 512][:, r % 512:r % 512 + w], AF.Exp,
                            bias=sums[:, 34:35],
                            accum_out=sums[:, 16 + i:16 + i + 1])
                    return inst
                return fn
            ctx.emit("scalar", lambda sc: sc.activation(
                xtr32[:, 0:256], b.masks[:, 0:256], AF.Copy), b.se)
            ctx.emit("scalar", mkexp(blocks, ext0), b.se)
            exp_se = b.se.n
            ctx.wait("vector", b.se, exp_se)
            def mklt(nb, ext):
                def fn(v):
                    lfin = _tree(v, sums, [16 + i for i in range(nb)], 26, "add")
                    v.reciprocal(sums[:, 32:33], sums[:, lfin:lfin + 1])
                    for _ in range(6):
                        v.engine_nop()
                    return v.tensor_scalar(att_s[:, PO:PO + ext], att_s[:, PO:PO + ext],
                                           sums[:, 32:33], None, ALU.mult)
                return fn
            ctx.emit("vector", mklt(nb, ext), b.ve)
            pnorm_ve = b.ve.n
            for pj in range(4):
                pool_free[pj] = pnorm_ve
            nkt = ext // 128
            ctx.wait("tensor", b.ve, pnorm_ve)
            ctx.wait("tensor", b.ve, pool_free[4])
            ctx.wait("tensor", b.ch_identbf)
            first_pv = True
            nrnd = (nkt + 7) // 8
            for rnd in range(nrnd):
                r0 = rnd * 8
                rn = min(8, nkt - r0)
                ctx.wait("tensor", b.ve, poolB_free)
                def mktr(r0, rn):
                    def fn(te):
                        for j in range(rn):
                            inst = te.transpose(
                                poolB[:, j * 128:(j + 1) * 128],
                                att_s[:, PO + (r0 + j) * 128:PO + (r0 + j + 1) * 128],
                                b.identbf[:, :])
                        return inst
                    return fn
                ctx.emit("tensor", mktr(r0, rn), b.pe)
                ctx.wait("vector", b.pe)
                evb_ve = ctx.emit("vector", lambda v, rn=rn: v.tensor_copy(
                    b.xtr[0][:, 0:rn * 128], poolB[:, 0:rn * 128]), b.ve)
                poolB_free = evb_ve
                ctx.wait("tensor", b.ve, evb_ve)
                def mkpv(r0, rn, ext0, first_pv, last):
                    def fn(te):
                        inst = None
                        for j in range(rn):
                            tt = ext0 // 128 + r0 + j
                            inst = te.matmul(
                                pool[4][0:64, 0:128],
                                b.xtr[1][:, tt * 64:(tt + 1) * 64],
                                b.xtr[0][:, j * 128:(j + 1) * 128],
                                start=(first_pv and j == 0),
                                stop=(last and j == rn - 1))
                        return inst
                    return fn
                ctx.emit("tensor", mkpv(r0, rn, ext0, first_pv, rnd == nrnd - 1), b.pe)
                first_pv = False
            ctx.wait("vector", b.pe)
            pool_free[4] = ctx.emit("vector", lambda v, h=h, qt=qt: v.tensor_copy(
                hid[0:64, h * SL + qt * 128:h * SL + (qt + 1) * 128],
                pool[4][0:64, 0:128]), b.ve)

    def emit_outproj(s, direction):
        nonlocal pool_free
        wo_d = cfg["wo_d"] if mode == "alibi" else (
            cfg["wo_f_d"] if direction == 0 else cfg["wo_b_d"])
        attn_done_ve = b.ve.n
        for cchunk in range(8):
            slot0 = (cchunk * 2) % b.NWR
            slot1 = (cchunk * 2 + 1) % b.NWR
            ctx.wait("gpsimd", b.pe, wo_last[slot0])
            wo_t0 = ctx.emit("gpsimd", lambda g, cchunk=cchunk, slot=slot0: g.dma_start(
                b.w2r[slot][0:64, :], wo_d[cchunk, 0]), b.ch_w2[slot0], 16)
            ctx.wait("gpsimd", b.pe, wo_last[slot1])
            wo_t1 = ctx.emit("gpsimd", lambda g, cchunk=cchunk, slot=slot1: g.dma_start(
                b.w2r[slot][0:64, :], wo_d[cchunk, 1]), b.ch_w2[slot1], 16)
            for th in range(SL // 512):
                pi = 5 + (th % 2)
                ctx.wait("tensor", b.ch_w2[slot0], wo_t0)
                ctx.wait("tensor", b.ch_w2[slot1], wo_t1)
                ctx.wait("tensor", b.ve, attn_done_ve)
                ctx.wait("tensor", b.ve, pool_free[pi])
                def mkoc(th, slot0, slot1, pi):
                    def fn(te):
                        for h in range(16):
                            sl = slot0 if h < 8 else slot1
                            inst = te.matmul(
                                pool[pi][:, :],
                                b.w2r[sl][0:64, (h % 8) * 128:(h % 8 + 1) * 128],
                                hid[0:64, h * SL + th * 512:h * SL + (th + 1) * 512],
                                start=(h == 0), stop=(h == 15))
                        return inst
                    return fn
                ctx.emit("tensor", mkoc(th, slot0, slot1, pi), b.pe)
                wo_last[slot0] = b.pe.n
                wo_last[slot1] = b.pe.n
                ctx.wait("vector", b.pe)
                if mode == "alibi":
                    ctx.wait("vector", b.ch_bo2)
                def mkev(cchunk, pi):
                    def fn(v):
                        if mode == "alibi":
                            return v.tensor_scalar(xtr32[:, 0:512], pool[pi][:, :],
                                                   b.bo_c2[:, cchunk:cchunk + 1],
                                                   None, ALU.add)
                        return v.tensor_copy(xtr32[:, 0:512], pool[pi][:, :])
                    return fn
                u_ve = ctx.emit("vector", mkev(cchunk, pi), b.ve)
                pool_free[pi] = u_ve
                pit = th % 2
                ctx.wait("tensor", b.ve, u_ve)
                ctx.wait("tensor", b.ve, pool_free[pit])
                def mktr2(pit):
                    def fn(te):
                        for j in range(4):
                            inst = te.transpose(pool[pit][:, j * 128:(j + 1) * 128],
                                                xtr32[:, j * 128:(j + 1) * 128],
                                                b.identsb[:, :])
                        return inst
                    return fn
                ctx.emit("tensor", mktr2(pit), b.pe)
                ctx.wait("vector", b.pe)
                def mkacc(th, s, pit, cchunk=cchunk):
                    def fn(v):
                        inst = None
                        for j in range(4):
                            gt = s * (SL // 128) + th * 4 + j
                            dst = b.xacc[:, gt * D + cchunk * 128:gt * D + (cchunk + 1) * 128]
                            inst = v.scalar_tensor_tensor(
                                dst, pool[pit][:, j * 128:(j + 1) * 128], 1.0, dst,
                                ALU.mult, ALU.add)
                        return inst
                    return fn
                ctx.emit("vector", mkacc(th, s, pit), b.ve)
                pool_free[pit] = b.ve.n

    for s in range(n_seq):
        for direction in range([1, 2][mode == "fgate"]):
            if direction == 0:
                for stb in range(seq_tb):
                    tb = s * seq_tb + stb
                    ctx.wait("sync", b.pe, xnfm_last[stb])
                    if first:
                        ctx.wait("sync", b.ch_xst, b.xst_total)
                        first = False
                    xnfm_tgts[stb] = ctx.emit("sync", lambda sy, stb=stb, tb=tb: sy.dma_start(
                        b.xnfm[stb][:, :].rearrange("p (c t) -> p c t", c=nd),
                        cfg["xnfm_d"][tb].rearrange("c p t -> p c t")), b.ch_xnfm[stb], 16)
                if mode == "fgate":
                    emit_fpass(s)
            for h in range(H):
                slot = h % b.NWR
                ctx.wait("gpsimd", b.pe, qk_last[slot])
                qk_tgt[h] = ctx.emit("gpsimd", lambda g, h=h, slot=slot: g.dma_start(
                    b.w1r[slot][:, :], cfg["qk_d"][h]), b.ch_w1[slot], 16)
                ctx.wait("gpsimd", b.pe, v_last[slot])
                v_tgt[h] = ctx.emit("gpsimd", lambda g, h=h, slot=slot: g.dma_start(
                    b.wur[slot][:, 0:512], cfg["v_d"][h]), b.ch_wu[slot], 16)
                emit_proj(s, h, slot)
                if mode == "fgate":
                    dsrc = cfg["gc_dram"] if direction == 0 else cfg["u_dram"]
                    for piece in range(2):
                        ctx.wait("sync", b.pe, stage_pe)
                        ctx.wait("sync", b.ch_gcst)
                        row_tgts[piece] = ctx.emit("sync",
                            lambda sy, s=s, h=h, piece=piece, dsrc=dsrc: sy.dma_start(
                                stage[piece][64:65, 0:1024],
                                dsrc[s, h:h + 1, piece * 1024:(piece + 1) * 1024]),
                            b.ch_gcrow[piece], 16)
                else:
                    ctx.wait("gpsimd", b.pe, stage_pe)
                    for piece in range(2):
                        ctx.emit("gpsimd", lambda g, piece=piece: g.iota(
                            stage[piece][64:65, 0:1024], [[1, 1024]], base=piece * 1024,
                            channel_multiplier=0,
                            allow_small_or_imprecise_dtypes=True), b.gpe)
                    ctx.wait("vector", b.gpe)
                    ctx.wait("vector", b.pe, stage_pe)
                    def mkjrow(h):
                        def fn(v):
                            v.tensor_scalar(stage[0][64:65, 0:1024], stage[0][64:65, 0:1024],
                                            slopes[h], None, ALU.mult)
                            return v.tensor_scalar(stage[1][64:65, 0:1024],
                                                   stage[1][64:65, 0:1024],
                                                   slopes[h], None, ALU.mult)
                        return fn
                    jrow_ve = ctx.emit("vector", mkjrow(h), b.ve)
                emit_head_attn(s, h, direction)
            emit_outproj(s, direction)
            prev_pass_pe = b.pe.n



"""L3_attn: ELU-feature linear attention (diagonal KV), base-0 only.

Q stored per-seq in hid[0:64, h*SL + col]; V never stored (only diag sums).
Out-proj feature-major (wo as lhsT), gelu-tanh in fm, PE transpose, += xacc.
Weight streams reuse moe rings: qk->w1r/ch_w1, v->wur/ch_wu, wo->w2r/ch_w2.
Scratch aliases stage/xtr32 (dead after prologue); sums alias stats (pitch 12).
"""


def host_pack_l3attn(wq, bq, wk, bk, wv, bv, wo, bo):
    D, H, dh = wq.shape
    qk = np.concatenate([wq, wk], axis=2)
    a = qk.reshape(8, 128, H, 2 * dh).transpose(2, 1, 0, 3)
    qk_p = np.ascontiguousarray(a.reshape(H, 128, 8 * 2 * dh))
    v = wv.reshape(8, 128, H, dh).transpose(2, 1, 0, 3)
    v_p = np.ascontiguousarray(v.reshape(H, 128, 8 * dh))
    w = wo.reshape(H, dh, 8, 128).transpose(2, 0, 1, 3)
    w = w.reshape(8, 2, 8, dh, 128).transpose(0, 1, 3, 2, 4)
    wo_p = np.ascontiguousarray(w.reshape(8, 2, dh, 8 * 128))
    bqkv = np.stack([bq, bk, bv], axis=0)
    bqkv_p = np.ascontiguousarray(bqkv.transpose(2, 0, 1).reshape(dh, 3 * H))
    bo_p = np.ascontiguousarray(bo.reshape(8, 128).T)
    return {"qk_p": qk_p, "v_p": v_p, "wo_p": wo_p, "bqkv_p": bqkv_p, "bo_p": bo_p}


def emit_l3attn(b, cfg):
    ctx, nc = b.ctx, b.nc
    D, nd, TB = b.D, b.nd, b.TB
    H, dh = 16, 64
    pool = b.pool
    n_seq, seq_tb = cfg["n_seq"], cfg["seq_tb"]
    SL = seq_tb * TB
    NH = TB // 512
    npart = seq_tb * NH
    assert npart <= 4 and H * 12 <= b.stats.shape[1]
    hid, stage, xtr32 = b.hid, b.stage, b.xtr32
    assert H * SL <= 32 * 1024

    if not hasattr(b, "l3a"):
        b.l3a = True
        b.bqkv = nc.alloc_sbuf_tensor("bqkv", [64, 3 * H], F32)
        b.bo_c = nc.alloc_sbuf_tensor("bo_c", [128, 8], F32)
        b.ch_bqkv = ctx.chan("c_bqkv")
        b.ch_bo = ctx.chan("c_bo")
    ctx.emit("gpsimd", lambda g: g.dma_start(b.bqkv[:, :], cfg["bqkv_d"]), b.ch_bqkv, 16)
    ctx.emit("gpsimd", lambda g: g.dma_start(b.bo_c[:, :], cfg["bo_d"]), b.ch_bo, 16)

    fq = stage[0][0:64, 0:512]
    fk = stage[0][0:64, 512:1024]
    fv = stage[1][0:64, 0:512]
    mq = stage[1][0:64, 512:1024]
    mk = xtr32[0:64, 0:512]
    kvs = xtr32[0:64, 512:1024]
    sums = b.stats

    qk_last = [b.pe.n] * b.NWR
    v_last = [b.pe.n] * b.NWR
    wo_last = [b.pe.n] * b.NWR
    xnfm_last = [b.pe.n] * 2
    pool_free = [b.ve.n] * 8
    prev_seq_pe = b.pe.n
    first = True

    for s in range(n_seq):
        for stb in range(seq_tb):
            tb = s * seq_tb + stb
            buf = tb % 2
            ctx.wait("sync", b.pe, xnfm_last[buf])
            if first:
                ctx.wait("sync", b.ch_xst, b.xst_total)
                first = False
            xnfm_tgt = ctx.emit("sync", lambda sy, buf=buf, tb=tb: sy.dma_start(
                b.xnfm[buf][:, :].rearrange("p (c t) -> p c t", c=nd),
                cfg["xnfm_d"][tb].rearrange("c p t -> p c t")), b.ch_xnfm[buf], 16)
            for h in range(H):
                slot = h % b.NWR
                ctx.wait("gpsimd", b.pe, qk_last[slot])
                qk_tgt = ctx.emit("gpsimd", lambda g, h=h, slot=slot: g.dma_start(
                    b.w1r[slot][:, :], cfg["qk_d"][h]), b.ch_w1[slot], 16)
                ctx.wait("gpsimd", b.pe, v_last[slot])
                v_tgt = ctx.emit("gpsimd", lambda g, h=h, slot=slot: g.dma_start(
                    b.wur[slot][:, 0:512], cfg["v_d"][h]), b.ch_wu[slot], 16)
                for half in range(NH):
                    trio = ((tb * H + h) * NH + half) % 2
                    ps = [pool[3 * trio + i] for i in range(3)]
                    ctx.wait("tensor", b.ch_w1[slot], qk_tgt)
                    ctx.wait("tensor", b.ch_wu[slot], v_tgt)
                    ctx.wait("tensor", b.ch_xnfm[buf], xnfm_tgt)
                    for i in range(3):
                        ctx.wait("tensor", b.ve, pool_free[3 * trio + i])
                    def mkproj(h, half, buf, slot, ps):
                        def fn(te):
                            for part in range(3):
                                for c in range(nd):
                                    if part < 2:
                                        lhs = b.w1r[slot][:, c * 128 + part * 64:c * 128 + (part + 1) * 64]
                                    else:
                                        lhs = b.wur[slot][:, c * 64:(c + 1) * 64]
                                    inst = te.matmul(
                                        ps[part][0:64, :], lhs,
                                        b.xnfm[buf][:, c * TB + half * 512:c * TB + (half + 1) * 512],
                                        start=(c == 0), stop=(c == nd - 1))
                            return inst
                        return fn
                    ctx.emit("tensor", mkproj(h, half, buf, slot, ps), b.pe)
                    qk_last[slot] = b.pe.n
                    v_last[slot] = b.pe.n
                    xnfm_last[buf] = b.pe.n
                    ctx.wait("vector", b.pe)
                    if tb == s * seq_tb and h == 0 and half == 0:
                        ctx.wait("vector", b.ch_bqkv)
                        ctx.wait("vector", b.pe, prev_seq_pe)
                    def mkfeat(h, ps):
                        def fn(v):
                            v.tensor_scalar(fq, ps[0][0:64, :], b.bqkv[:, h:h + 1], None, ALU.add)
                            v.tensor_scalar(fk, ps[1][0:64, :], b.bqkv[:, H + h:H + h + 1], None, ALU.add)
                            v.tensor_scalar(fv, ps[2][0:64, :], b.bqkv[:, 2 * H + h:2 * H + h + 1], None, ALU.add)
                            v.tensor_single_scalar(mq, fq, 0.0, ALU.min)
                            return v.tensor_single_scalar(mk, fk, 0.0, ALU.min)
                        return fn
                    ctx.emit("vector", mkfeat(h, ps), b.ve)
                    ctx.wait("scalar", b.ve)
                    ctx.emit("scalar", lambda sc: sc.activation(mq, mq, AF.Exp), b.se)
                    ctx.emit("scalar", lambda sc: sc.activation(mk, mk, AF.Exp), b.se)
                    ctx.wait("vector", b.se)
                    def mkfeat2(h, half, stb):
                        qcol = h * SL + stb * TB + half * 512
                        cc = h * 12 + (stb * NH + half) * 2
                        def fn(v):
                            v.tensor_single_scalar(fk, fk, 0.0, ALU.max)
                            v.tensor_single_scalar(fq, fq, 0.0, ALU.max)
                            v.tensor_add(fk, fk, mk)
                            v.tensor_add(fq, fq, mq)
                            v.tensor_mul(kvs, fk, fv)
                            v.tensor_copy(hid[0:64, qcol:qcol + 512], fq)
                            v.tensor_reduce(sums[0:64, cc:cc + 1], kvs, AX.X, ALU.add)
                            return v.tensor_reduce(sums[0:64, cc + 1:cc + 2], fk, AX.X, ALU.add)
                        return fn
                    pf = ctx.emit("vector", mkfeat2(h, half, stb), b.ve)
                    for i in range(3):
                        pool_free[3 * trio + i] = pf
        def mkcomb():
            def fn(v):
                for h in range(H):
                    c = h * 12
                    if npart >= 3:
                        v.tensor_add(sums[0:64, c + 8:c + 9], sums[0:64, c:c + 1],
                                     sums[0:64, c + 4:c + 5])
                    else:
                        v.tensor_copy(sums[0:64, c + 8:c + 9], sums[0:64, c:c + 1])
                for h in range(H):
                    c = h * 12
                    if npart == 4:
                        v.tensor_add(sums[0:64, c + 9:c + 10], sums[0:64, c + 2:c + 3],
                                     sums[0:64, c + 6:c + 7])
                    elif npart >= 2:
                        v.tensor_copy(sums[0:64, c + 9:c + 10], sums[0:64, c + 2:c + 3])
                    else:
                        v.memset(sums[0:64, c + 9:c + 10], 0.0)
                for h in range(H):
                    c = h * 12
                    if npart >= 3:
                        v.tensor_add(sums[0:64, c + 10:c + 11], sums[0:64, c + 1:c + 2],
                                     sums[0:64, c + 5:c + 6])
                    else:
                        v.tensor_copy(sums[0:64, c + 10:c + 11], sums[0:64, c + 1:c + 2])
                for h in range(H):
                    c = h * 12
                    if npart == 4:
                        v.tensor_add(sums[0:64, c + 11:c + 12], sums[0:64, c + 3:c + 4],
                                     sums[0:64, c + 7:c + 8])
                    elif npart >= 2:
                        v.tensor_copy(sums[0:64, c + 11:c + 12], sums[0:64, c + 3:c + 4])
                    else:
                        v.memset(sums[0:64, c + 11:c + 12], 0.0)
                for h in range(H):
                    c = h * 12
                    v.tensor_add(sums[0:64, c + 8:c + 9], sums[0:64, c + 8:c + 9],
                                 sums[0:64, c + 9:c + 10])
                for h in range(H):
                    c = h * 12
                    inst = v.tensor_add(sums[0:64, c + 10:c + 11], sums[0:64, c + 10:c + 11],
                                        sums[0:64, c + 11:c + 12])
                return inst
            return fn
        ctx.emit("vector", mkcomb(), b.ve)
        for h in range(H):
            def mkattn(h):
                c = h * 12
                def fn(v):
                    inst = None
                    for off in range(0, SL, 512):
                        q_ap = hid[0:64, h * SL + off:h * SL + off + 512]
                        v.tensor_scalar(fq, q_ap, sums[0:64, c + 10:c + 11], 1e-6,
                                        ALU.mult, ALU.add)
                        v.tensor_scalar(fk, q_ap, sums[0:64, c + 8:c + 9], None, ALU.mult)
                        v.reciprocal(fv, fq)
                        for _ in range(4):
                            v.engine_nop()
                        inst = v.tensor_mul(q_ap, fk, fv)
                    return inst
                return fn
            ctx.emit("vector", mkattn(h), b.ve)
        attn_ve = b.ve.n
        for cchunk in range(8):
            slot0 = (cchunk * 2) % b.NWR
            slot1 = (cchunk * 2 + 1) % b.NWR
            ctx.wait("gpsimd", b.pe, wo_last[slot0])
            wo_tgt0 = ctx.emit("gpsimd", lambda g, cchunk=cchunk, slot=slot0: g.dma_start(
                b.w2r[slot][0:64, :], cfg["wo_d"][cchunk, 0]), b.ch_w2[slot0], 16)
            ctx.wait("gpsimd", b.pe, wo_last[slot1])
            wo_tgt1 = ctx.emit("gpsimd", lambda g, cchunk=cchunk, slot=slot1: g.dma_start(
                b.w2r[slot][0:64, :], cfg["wo_d"][cchunk, 1]), b.ch_w2[slot1], 16)
            for th in range(SL // 512):
                pi = 6 if th % 2 == 0 else 2
                ctx.wait("tensor", b.ch_w2[slot0], wo_tgt0)
                ctx.wait("tensor", b.ch_w2[slot1], wo_tgt1)
                ctx.wait("tensor", b.ve, attn_ve)
                ctx.wait("tensor", b.ve, pool_free[pi])
                def mkoc(th, s, slot0, slot1, pi):
                    def fn(te):
                        for h in range(H):
                            sl = slot0 if h < 8 else slot1
                            inst = te.matmul(
                                pool[pi][:, :],
                                b.w2r[sl][0:64, (h % 8) * 128:(h % 8 + 1) * 128],
                                hid[0:64, h * SL + th * 512:h * SL + (th + 1) * 512],
                                start=(h == 0), stop=(h == H - 1))
                        return inst
                    return fn
                ctx.emit("tensor", mkoc(th, s, slot0, slot1, pi), b.pe)
                wo_last[slot0] = b.pe.n
                wo_last[slot1] = b.pe.n
                ctx.wait("vector", b.pe)
                ctx.wait("vector", b.ch_bo)
                def mkgelu(cchunk, pi):
                    def fn(v):
                        u = stage[0][:, 0:512]
                        u2 = stage[0][:, 512:1024]
                        t3 = stage[1][:, 0:512]
                        v.tensor_scalar(u, pool[pi][:, :], b.bo_c[:, cchunk:cchunk + 1],
                                        None, ALU.add)
                        v.tensor_mul(u2, u, u)
                        v.tensor_mul(t3, u2, u)
                        return v.scalar_tensor_tensor(t3, t3, 0.044715, u, ALU.mult, ALU.add)
                    return fn
                pool_free[pi] = ctx.emit("vector", mkgelu(cchunk, pi), b.ve)
                ctx.wait("scalar", b.ve)
                ctx.emit("scalar", lambda sc: sc.activation(
                    stage[1][:, 0:512], stage[1][:, 0:512], AF.Tanh,
                    scale=0.7978845608028654), b.se)
                ctx.wait("vector", b.se)
                def mkgelu2():
                    def fn(v):
                        a = stage[1][:, 512:1024]
                        v.tensor_scalar(a, stage[1][:, 0:512], 0.5, 0.5, ALU.mult, ALU.add)
                        return v.tensor_mul(xtr32[:, 0:512], a, stage[0][:, 0:512])
                    return fn
                gel_ve = ctx.emit("vector", mkgelu2(), b.ve)
                pit = th % 2
                ctx.wait("tensor", b.ve, gel_ve)
                ctx.wait("tensor", b.ve, pool_free[pit])
                def mktr(pit):
                    def fn(te):
                        for j in range(4):
                            inst = te.transpose(pool[pit][:, j * 128:(j + 1) * 128],
                                                xtr32[:, j * 128:(j + 1) * 128],
                                                b.identsb[:, :])
                        return inst
                    return fn
                ctx.emit("tensor", mktr(pit), b.pe)
                ctx.wait("vector", b.pe)
                def mkacc(cchunk, th, s, pit):
                    def fn(v):
                        for j in range(4):
                            gt = s * (SL // 128) + th * 4 + j
                            dst = b.xacc[:, gt * D + cchunk * 128:gt * D + (cchunk + 1) * 128]
                            inst = v.scalar_tensor_tensor(
                                dst, pool[pit][:, j * 128:(j + 1) * 128], 1.0, dst,
                                ALU.mult, ALU.add)
                        return inst
                    return fn
                ctx.emit("vector", mkacc(cchunk, th, s, pit), b.ve)
                pool_free[pit] = b.ve.n
        prev_seq_pe = b.pe.n



"""L0_ffn: dense rank-4 LoRA MoE with full-denominator top-2 weights,
plus optional y_attn streaming add (xacc += y).

Weights resident in hid (free during layer 0): W_lin slabs hid[:,0:8192],
leftall hid[:,8192:8448], rightall hid[0:32, 8448:9472]. b_lin row in b2r[0].
psum: pool[4] t-matmul, pool[5]/[6] base+lora, poolB transpose.
"""


def host_pack_l0ffn(W_lin, b_lin, left, right):
    wl = pack_rhs(np.ascontiguousarray(W_lin.T))
    E, D, R = left.shape
    la = left.transpose(1, 0, 2).reshape(D, E * R)
    la = np.ascontiguousarray(la.reshape(8, 128, E * R))
    ra = np.ascontiguousarray(right.reshape(E * R, D))
    return {"wl_p": wl, "la_p": la, "ra_p": ra,
            "blin_p": np.ascontiguousarray(b_lin.reshape(1, D))}


def emit_l0ffn(b, cfg):
    ctx, nc = b.ctx, b.nc
    D, nd, TB = b.D, b.nd, b.TB
    NT, TPB = b.NT, b.TB // 128
    NTB = NT // TPB
    E = 8
    pool, poolB = b.pool, b.poolB
    hid, xtr32 = b.hid, b.xtr32

    wl = hid[:, 1024:9216]
    la = hid[:, 9216:9472]
    ra = hid[0:32, 9472:10496]
    ch_wres = ctx.chan("c_wres")
    ctx.emit("gpsimd", lambda g: g.dma_start(
        wl.rearrange("p (c j) -> p c j", j=1024), cfg["wl_d"].rearrange("c p j -> p c j")),
        ch_wres, 16)
    ctx.emit("gpsimd", lambda g: g.dma_start(
        la.rearrange("p (c j) -> p c j", j=32), cfg["la_d"].rearrange("c p j -> p c j")),
        ch_wres, 16)
    ctx.emit("gpsimd", lambda g: g.dma_start(ra, cfg["ra_d"]), ch_wres, 16)
    ctx.emit("gpsimd", lambda g: g.dma_start(b.b2r[0][:, :], cfg["blin_d"]), ch_wres, 16)
    wres_total = ch_wres.n

    xnfm_last = [b.pe.n] * 2
    pool_free = {4: b.ve.n, 5: b.ve.n, 6: b.ve.n}
    poolB_free = b.ve.n
    first = True
    y_ch = ctx.chan("c_y") if cfg.get("y_d") is not None else None
    tp_ve = {}

    for tb in range(NTB):
        buf = tb % 2
        ctx.wait("sync", b.pe, xnfm_last[buf])
        if first:
            ctx.wait("sync", b.ch_xst, b.xst_total)
        xnfm_tgt = ctx.emit("sync", lambda s, buf=buf, tb=tb: s.dma_start(
            b.xnfm[buf][:, :].rearrange("p (c t) -> p c t", c=nd),
            cfg["xnfm_d"][tb].rearrange("c p t -> p c t")), b.ch_xnfm[buf], 16)
        for tt in range(TPB):
            gt = tb * TPB + tt
            if y_ch is not None:
                if gt < 2:
                    ctx.wait("gpsimd", b.ch_xst, b.xst_total)
                ctx.wait("gpsimd", b.ve, tp_ve.get(gt - 2, -1))
                y_tgt = ctx.emit("gpsimd", lambda g, gt=gt: g.dma_start(
                    b.xtr[1][:, (gt % 2) * 512:(gt % 2) * 512 + 512]
                    .rearrange("p (a d) -> p a d", d=256) if False else
                    b.xtr[1][:, (gt % 2) * 512:(gt % 2) * 512 + 512],
                    cfg["y_d"][gt][:, 0:512]), y_ch, 16)
                y_tgt2 = ctx.emit("gpsimd", lambda g, gt=gt: g.dma_start(
                    b.xtr[0][:, (gt % 2) * 512:(gt % 2) * 512 + 512],
                    cfg["y_d"][gt][:, 512:1024]), y_ch, 16)
                ctx.wait("vector", y_ch, y_tgt2)
                def mkyadd(gt):
                    def fn(v):
                        d0 = b.xacc[:, gt * D:gt * D + 512]
                        d1 = b.xacc[:, gt * D + 512:gt * D + 1024]
                        v.tensor_add(d0, d0, b.xtr[1][:, (gt % 2) * 512:(gt % 2) * 512 + 512])
                        return v.tensor_add(d1, d1, b.xtr[0][:, (gt % 2) * 512:(gt % 2) * 512 + 512])
                    return fn
                tp_ve[gt] = ctx.emit("vector", mkyadd(gt), b.ve)
            ctx.wait("tensor", b.ch_xnfm[buf], xnfm_tgt)
            ctx.wait("tensor", ch_wres, wres_total)
            ctx.wait("tensor", b.ve, pool_free[4])
            if first:
                first = False
            def mkt(buf, tt):
                def fn(te):
                    for c in range(nd):
                        inst = te.matmul(
                            pool[4][:, 0:32],
                            b.xnfm[buf][:, c * TB + tt * 128:c * TB + (tt + 1) * 128],
                            la[:, c * 32:(c + 1) * 32],
                            start=(c == 0), stop=(c == nd - 1))
                    return inst
                return fn
            ctx.emit("tensor", mkt(buf, tt), b.pe)
            xnfm_last[buf] = b.pe.n
            ctx.wait("vector", b.pe)
            if gt == 0:
                ctx.wait("vector", b.ve, b.combine_ve)
            def mkapply(gt):
                def fn(v):
                    v.tensor_copy(xtr32[:, 512:544], pool[4][:, 0:32])
                    for _ in range(5):
                        v.engine_nop()
                    for e in range(E):
                        inst = v.tensor_scalar(
                            b.att_s[:, e * 4:(e + 1) * 4], xtr32[:, 512 + e * 4:516 + e * 4],
                            b.combine[:, gt * E + e:gt * E + e + 1], None, ALU.mult)
                    return inst
                return fn
            pool_free[4] = ctx.emit("vector", mkapply(gt), b.ve)
            ap_ve = b.ve.n
            ctx.wait("tensor", b.ve, ap_ve)
            ctx.wait("tensor", b.ve, poolB_free)
            ctx.wait("tensor", b.ch_identbf)
            ctx.emit("tensor", lambda te: te.transpose(
                poolB[0:32, 0:128], b.att_s[:, 0:32], b.identbf[:, :]), b.pe)
            ctx.wait("vector", b.pe)
            poolB_free = ctx.emit("vector", lambda v: v.tensor_copy(
                b.att_s[0:32, 128:256], poolB[0:32, 0:128]), b.ve)
            tpT_ve = b.ve.n
            for nh in range(2):
                pi = 5 + nh
                ctx.wait("tensor", b.ve, pool_free[pi])
                ctx.wait("tensor", b.ve, tpT_ve)
                ctx.wait("tensor", b.ve, b.ones_ve)
                def mkbase(buf, tt, nh, pi):
                    def fn(te):
                        te.matmul(pool[pi][:, :], b.ones1[0:1, :],
                                  b.b2r[0][0:1, nh * 512:(nh + 1) * 512],
                                  start=True, stop=False)
                        for c in range(nd):
                            te.matmul(
                                pool[pi][:, :],
                                b.xnfm[buf][:, c * TB + tt * 128:c * TB + (tt + 1) * 128],
                                wl[:, c * 1024 + nh * 512:c * 1024 + (nh + 1) * 512],
                                start=False, stop=False)
                        return te.matmul(pool[pi][:, :], b.att_s[0:32, 128:256],
                                         ra[:, nh * 512:(nh + 1) * 512],
                                         start=False, stop=True)
                    return fn
                ctx.emit("tensor", mkbase(buf, tt, nh, pi), b.pe)
                xnfm_last[buf] = b.pe.n
                ctx.wait("vector", b.pe)
                def mkev(gt, nh, pi):
                    def fn(v):
                        dst = b.xacc[:, gt * D + nh * 512:gt * D + (nh + 1) * 512]
                        return v.scalar_tensor_tensor(dst, pool[pi][:, :], 1.0, dst,
                                                      ALU.mult, ALU.add)
                    return fn
                pool_free[pi] = ctx.emit("vector", mkev(gt, nh, pi), b.ve)



"""L0_attn TTT-linear, e-sharded (this core owns 128 of the 1024 output dims).

ref: xn=LN(x) (computed on-device here); v = xn@Wkv[D:].T + bkv[D:];
scan over steps s (batch-coupled): y_t = xn_t@W^T; E_t = y_t - v_t;
W -= c*(E_t^T@xn_t), c = eta*2/(B*D). Output y.

Rows r = s*B + b (step-major). Chunk = C=64 steps -> 1024 rows.
E = (I + c*SL)^{-1}(xn@Ws^T - v), SL strictly-16-block-lower of S=xn xn^T.
Solve in 16 stages of 64 rows; within stage exact (I+cLd)^{-1} =
I - cLd + c^2Ld^2 - c^3Ld^3 (Ld strictly-block-tri, nilpotent^4).
y = E + v; Ws -= c*(Xc^T E).
"""


def host_pack_l0attn(W0s, WvTs, bkvs):
    w0 = np.ascontiguousarray(W0s.T.reshape(8, 128, 128))
    wv = np.ascontiguousarray(WvTs.reshape(8, 128, 128))
    return {"w0t": w0, "wvt": wv,
            "bkvs": np.ascontiguousarray(bkvs.reshape(1, 128))}


def host_masks128():
    blk = np.arange(128) // 16
    L = (blk[:, None] > blk[None, :]).astype(np.float32)
    U = (blk[:, None] < blk[None, :]).astype(np.float32)
    return np.stack([L, U])


def build_l0attn(S_len=2048, B=16, D=1024, C=64, phase=99):
    rows = C * B
    nch = S_len // C
    nd = D // 128
    ntr = rows // 128
    nst = rows // 128
    nc = bass.Bass("TRN2", target_bir_lowering=False)
    ctx = Ctx(nc)

    xs_d = nc.dram_tensor("xs", [S_len, B, D], F32, kind="ExternalInput").ap()
    w0_d = nc.dram_tensor("w0t", [8, 128, 128], F32, kind="ExternalInput").ap()
    wv_d = nc.dram_tensor("wvt", [8, 128, 128], F32, kind="ExternalInput").ap()
    bkv_d = nc.dram_tensor("bkvs", [1, 128], F32, kind="ExternalInput").ap()
    eta_d = nc.dram_tensor("eta", [1, 1], F32, kind="ExternalInput").ap()
    m64_d = nc.dram_tensor("m64", [2, 128, 128], F32, kind="ExternalInput").ap()
    id_d = nc.dram_tensor("ident", [128, 128], F32, kind="ExternalInput").ap()
    y_d = nc.dram_tensor("y", [S_len, B, 128], BF16, kind="ExternalOutput").ap()

    stage = [nc.alloc_sbuf_tensor(f"stage{i}", [128, 1024], F32) for i in range(2)]
    xc_tm = nc.alloc_sbuf_tensor("xc_tm", [128, ntr * 1024], BF16)
    xc_fm = nc.alloc_sbuf_tensor("xc_fm", [128, nd * rows], BF16)
    S_sb = nc.alloc_sbuf_tensor("S_sb", [128, ntr * rows], BF16)
    E_sb = nc.alloc_sbuf_tensor("E_sb", [128, ntr * 128], BF16)
    base_sb = nc.alloc_sbuf_tensor("base_sb", [128, ntr * 128], F32)
    v_sb = nc.alloc_sbuf_tensor("v_sb", [128, ntr * 128], BF16)
    y_sb = nc.alloc_sbuf_tensor("y_sb", [128, ntr * 128], BF16)
    Wst = nc.alloc_sbuf_tensor("Wst", [128, nd * 128], F32)
    Wbf = nc.alloc_sbuf_tensor("Wbf", [128, nd * 128], BF16)
    wvt = nc.alloc_sbuf_tensor("wvt_sb", [128, nd * 128], BF16)
    wvtn = nc.alloc_sbuf_tensor("wvtn_sb", [128, nd * 128], BF16)
    bkv = nc.alloc_sbuf_tensor("bkv_sb", [1, 128], BF16)
    bkvn = nc.alloc_sbuf_tensor("bkvn_sb", [1, 128], BF16)
    identbf = nc.alloc_sbuf_tensor("identbf", [128, 128], BF16)
    ones1 = nc.alloc_sbuf_tensor("ones1", [1, 128], BF16)
    ones32 = nc.alloc_sbuf_tensor("ones32", [1, 128], F32)
    etac = nc.alloc_sbuf_tensor("etac", [128, 4], F32)
    m64 = nc.alloc_sbuf_tensor("m64sb", [128, 256], F32)
    ld64 = nc.alloc_sbuf_tensor("ld64", [128, 128], BF16)
    P64 = nc.alloc_sbuf_tensor("P64", [128, 128 * 14], F32)
    stats = nc.alloc_sbuf_tensor("stats", [128, 64], F32)
    spacer = nc.alloc_sbuf_tensor("spacer", [128, 512], F32)
    epsc = nc.alloc_sbuf_tensor("epsc", [128, 1], F32)

    pool = [nc.alloc_psum_tensor(f"pool{i}", [128, 512], F32) for i in range(7)]
    poolB = nc.alloc_psum_tensor("poolB", [128, 1024], BF16)

    pe = ctx.chan("pe"); ve = ctx.chan("ve"); se = ctx.chan("se")
    gst = ctx.chan("gst")
    ch_st = [ctx.chan(f"c_st{i}") for i in range(2)]
    ch_c = ctx.chan("c_c")
    ch_cg = ctx.chan("c_cg")

    ctx.emit("vector", lambda v: v.memset(ones1[:, :], 1.0), ve)
    ctx.emit("vector", lambda v: v.memset(ones32[:, :], 1.0), ve)
    ctx.emit("vector", lambda v: v.memset(epsc[:, :], 1e-5), ve)
    ones_ve = ve.n
    ctx.emit("sync", lambda s: s.dma_start(
        Wst[:, :].rearrange("p (c e) -> p c e", e=128),
        w0_d.rearrange("c p e -> p c e")), ch_c, 16)
    ctx.emit("sync", lambda s: s.dma_start(
        m64[:, :].rearrange("p (a e) -> p a e", e=128),
        m64_d.rearrange("a p e -> p a e")), ch_c, 16)
    ctx.emit("sync", lambda s: s.dma_start(etac[0:1, 3:4], eta_d), ch_c, 16)
    ctx.emit("gpsimd", lambda g: g.dma_start(
        wvt[:, :].rearrange("p (c e) -> p c e", e=128),
        wv_d.rearrange("c p e -> p c e")), ch_cg, 16)
    ctx.emit("gpsimd", lambda g: g.dma_start(bkv[:, :], bkv_d), ch_cg, 16)
    ctx.emit("gpsimd", lambda g: g.dma_start(identbf[:, :], id_d), ch_cg, 16)
    ctx.wait("tensor", ch_c)
    ctx.wait("tensor", ve, ones_ve)
    ctx.emit("tensor", lambda te: te.matmul(pool[6][:, 0:1], ones32[0:1, :],
                                            etac[0:1, 3:4], start=True, stop=True), pe)
    ctx.wait("vector", pe)
    cfac = 2.0 / (B * D)
    def mkc(v):
        v.tensor_scalar(etac[:, 0:1], pool[6][:, 0:1], -cfac, None, ALU.mult)
        v.tensor_scalar(etac[:, 1:2], pool[6][:, 0:1], cfac, None, ALU.mult)
        for _ in range(5):
            v.engine_nop()
        v.tensor_mul(etac[:, 1:2], etac[:, 1:2], etac[:, 1:2])
        for _ in range(5):
            v.engine_nop()
        return v.tensor_mul(etac[:, 2:3], etac[:, 1:2], etac[:, 1:2])
    ctx.emit("vector", mkc, ve)
    cready_ve = ve.n
    ctx.wait("vector", ch_cg)
    ctx.emit("vector", lambda v: v.tensor_scalar(
        wvtn[:, :], wvt[:, :], -1.0, None, ALU.mult), ve)
    ctx.emit("vector", lambda v: v.tensor_scalar(
        bkvn[:, :], bkv[:, :], -1.0, None, ALU.mult), ve)
    wbf_ve = ctx.emit("vector", lambda v: v.tensor_copy(Wbf[:, :], Wst[:, :]), ve)

    stage_free = [ve.n, ve.n]
    pool_free = [ve.n] * 7
    poolB_free = ve.n
    E_ready_ve = ve.n
    wupd_pe = pe.n
    y_st = {}

    if phase < 1:
        ctx.emit("gpsimd", lambda g: g.dma_start(
            y_d[0:1].rearrange("s b e -> (s b) e"), y_sb[0:16, 0:128]), gst, 16)
        ctx.wait("gpsimd", gst)
        ctx.emit("vector", lambda v: v.memset(y_sb[:, :], 0.0), ve)
        ctx.build()
        return nc
    for ch in range(nch):
        for tr in range(ntr):
            si = (ch * ntr + tr) % 2
            sb = tr * 8
            ctx.wait("sync", ve, stage_free[si])
            st_tgt = ctx.emit("sync", lambda s, ch=ch, tr=tr, si=si: s.dma_start(
                stage[si][:, :],
                xs_d[ch * C:(ch + 1) * C].rearrange("s b d -> (s b) d")
                [tr * 128:(tr + 1) * 128, :]), ch_st[si], 16)
            ctx.wait("vector", ch_st[si], st_tgt)
            ctx.emit("vector", lambda v, si=si, sb=sb: v.tensor_reduce(
                stats[:, sb:sb + 1], stage[si][:, :], AX.X, ALU.add, negate=True), ve)
            ctx.wait("scalar", ve)
            ctx.wait("scalar", ch_st[si], st_tgt)
            if ch > 0 and tr == 0:
                ctx.wait("scalar", pe, wupd_pe)
            def mksq(tr, si, sb):
                def fn(sc):
                    sc.activation(xc_tm[:, tr * 1024:(tr + 1) * 1024], stage[si][:, :],
                                  AF.Square, accum_out=stats[:, sb + 1:sb + 2])
                    sc.activation(stats[:, sb + 2:sb + 3], stats[:, sb:sb + 1],
                                  AF.Copy, scale=1.0 / D)
                    return sc.activation(stats[:, sb + 3:sb + 4], stats[:, sb:sb + 1],
                                         AF.Square, scale=1.0 / D)
                return fn
            ctx.emit("scalar", mksq(tr, si, sb), se)
            ctx.wait("vector", se)
            ctx.emit("vector", lambda v, sb=sb: v.scalar_tensor_tensor(
                stats[:, sb + 4:sb + 5], stats[:, sb + 1:sb + 2], 1.0 / D,
                stats[:, sb + 3:sb + 4], ALU.mult, ALU.subtract), ve)
            ctx.wait("scalar", ve)
            def mkrstd(si, sb):
                def fn(sc):
                    sc.activation(stats[:, sb + 5:sb + 6], stats[:, sb + 4:sb + 5],
                                  AF.Ln, bias=epsc[:, 0:1])
                    sc.activation(spacer[:, :], stage[si][:, 0:512], AF.Copy)
                    return sc.activation(stats[:, sb + 5:sb + 6], stats[:, sb + 5:sb + 6],
                                         AF.Exp, scale=-0.5)
                return fn
            ctx.emit("scalar", mkrstd(si, sb), se)
            ctx.wait("vector", se)
            ctx.emit("vector", lambda v, tr=tr, si=si, sb=sb: v.tensor_scalar(
                xc_tm[:, tr * 1024:(tr + 1) * 1024], stage[si][:, :],
                stats[:, sb + 2:sb + 3], stats[:, sb + 5:sb + 6],
                ALU.add, ALU.mult), ve)
            stage_free[si] = ve.n
            apply_ve = ve.n
            if phase < 2:
                continue
            ctx.wait("tensor", ve, apply_ve)
            ctx.wait("tensor", ve, poolB_free)
            ctx.wait("tensor", ch_cg)
            def mktr(tr):
                def fn(te):
                    for cc in range(8):
                        inst = te.transpose(
                            poolB[:, cc * 128:(cc + 1) * 128],
                            xc_tm[:, tr * 1024 + cc * 128:tr * 1024 + (cc + 1) * 128],
                            identbf[:, :])
                    return inst
                return fn
            ctx.emit("tensor", mktr(tr), pe)
            ctx.wait("vector", pe)
            def mkevt(tr):
                def fn(v):
                    inst = None
                    for cc in range(8):
                        inst = v.tensor_copy(
                            xc_fm[:, cc * rows + tr * 128:cc * rows + (tr + 1) * 128],
                            poolB[:, cc * 128:(cc + 1) * 128])
                    return inst
                return fn
            poolB_free = ctx.emit("vector", mkevt(tr), ve)
            fm_ve = ve.n
            if phase < 3:
                continue
            ctx.wait("tensor", ve, fm_ve)
            ctx.wait("tensor", ve, pool_free[2])
            ctx.wait("tensor", ve, wbf_ve)
            def mkvb(tr):
                def fn(te):
                    te.matmul(pool[2][:, 0:128], ones1[0:1, :], bkv[0:1, :],
                              start=True, stop=False)
                    for c in range(nd):
                        te.matmul(pool[2][:, 0:128],
                                  xc_fm[:, c * rows + tr * 128:c * rows + (tr + 1) * 128],
                                  wvt[:, c * 128:(c + 1) * 128],
                                  start=False, stop=(c == nd - 1))
                    te.matmul(pool[2][:, 128:256], ones1[0:1, :], bkvn[0:1, :],
                              start=True, stop=False)
                    for c in range(nd):
                        te.matmul(pool[2][:, 128:256],
                                  xc_fm[:, c * rows + tr * 128:c * rows + (tr + 1) * 128],
                                  wvtn[:, c * 128:(c + 1) * 128],
                                  start=False, stop=False)
                    for c in range(nd):
                        inst = te.matmul(
                            pool[2][:, 128:256],
                            xc_fm[:, c * rows + tr * 128:c * rows + (tr + 1) * 128],
                            Wbf[:, c * 128:(c + 1) * 128],
                            start=False, stop=(c == nd - 1))
                    return inst
                return fn
            ctx.emit("tensor", mkvb(tr), pe)
            ctx.wait("vector", pe)
            def mkvbe(tr):
                def fn(v):
                    v.tensor_copy(v_sb[:, tr * 128:(tr + 1) * 128], pool[2][:, 0:128])
                    return v.tensor_copy(base_sb[:, tr * 128:(tr + 1) * 128],
                                         pool[2][:, 128:256])
                return fn
            pool_free[2] = ctx.emit("vector", mkvbe(tr), ve)
        for tr in range(ntr if phase >= 4 else 0):
            for cg in range(0, rows, 512):
                pi = 3 + ((tr * 2 + cg // 512) % 2)
                ctx.wait("tensor", ve, pool_free[pi])
                def mks(tr, cg, pi):
                    def fn(te):
                        for c in range(nd):
                            inst = te.matmul(
                                pool[pi][:, 0:512],
                                xc_fm[:, c * rows + tr * 128:c * rows + (tr + 1) * 128],
                                xc_fm[:, c * rows + cg:c * rows + cg + 512],
                                start=(c == 0), stop=(c == nd - 1))
                        return inst
                    return fn
                ctx.emit("tensor", mks(tr, cg, pi), pe)
                ctx.wait("vector", pe)
                pool_free[pi] = ctx.emit(
                    "vector", lambda v, tr=tr, cg=cg, pi=pi:
                    v.tensor_copy(S_sb[:, tr * rows + cg:tr * rows + cg + 512],
                                  pool[pi][:, 0:512]), ve)
        S_ve = ve.n
        for st in range(nst if phase >= 5 else 0):
            r0 = st * 128
            big_pe = -1
            if st > 0:
                ctx.wait("tensor", ve, S_ve)
                ctx.wait("tensor", ve, pool_free[5])
                ctx.wait("tensor", ve, E_ready_ve)
                def mkbig(st, r0):
                    def fn(te):
                        for tj in range(st):
                            inst = te.matmul(
                                pool[5][:, 0:128],
                                S_sb[:, tj * rows + r0:tj * rows + r0 + 128],
                                E_sb[:, tj * 128:(tj + 1) * 128],
                                start=(tj == 0), stop=(tj == st - 1))
                        return inst
                    return fn
                ctx.emit("tensor", mkbig(st, r0), pe)
                big_pe = pe.n
            if big_pe >= 0:
                ctx.wait("vector", pe, big_pe)
            ctx.wait("vector", ve, cready_ve)
            if st == 0:
                ctx.wait("vector", gst, y_st.get((ch - 1, ntr - 1), -1))
            def mkP(st, r0):
                def fn(v):
                    bsl = base_sb[:, st * 128:(st + 1) * 128]
                    ssl = S_sb[:, st * rows + r0:st * rows + r0 + 128]
                    if st == 0:
                        v.tensor_copy(P64[:, 0:128], bsl)
                        v.tensor_copy(y_sb[:, 0:128], bsl)
                    else:
                        v.scalar_tensor_tensor(
                            P64[:, 0:128], pool[5][:, 0:128], etac[:, 0:1], bsl,
                            ALU.mult, ALU.add)
                        v.scalar_tensor_tensor(
                            y_sb[:, 0:128], pool[5][:, 0:128], etac[:, 0:1],
                            bsl, ALU.mult, ALU.add)
                    return v.tensor_mul(ld64[:, :], ssl, m64[:, 128:256])
                return fn
            pool_free[5] = ctx.emit("vector", mkP(st, r0), ve)
            ld_ve = ve.n
            for it in range(7):
                ctx.wait("tensor", ve, ld_ve if it == 0 else ve.n)
                ctx.wait("tensor", ve, pool_free[6])
                ctx.emit("tensor", lambda te: te.matmul(
                    pool[6][:, 0:128], ld64[:, :], y_sb[:, 0:128],
                    start=True, stop=True), pe)
                ctx.wait("vector", pe)
                def mkue(it):
                    def fn(v):
                        v.tensor_copy(P64[:, (it + 1) * 128:(it + 2) * 128],
                                      pool[6][:, 0:128])
                        return v.tensor_copy(y_sb[:, 0:128], pool[6][:, 0:128])
                    return fn
                pool_free[6] = ctx.emit("vector", mkue(it), ve)
            def mkE(st):
                def fn(v):
                    for _ in range(4):
                        v.engine_nop()
                    for j in range(4):
                        v.scalar_tensor_tensor(
                            P64[:, (8 + j) * 128:(9 + j) * 128],
                            P64[:, (2 * j + 1) * 128:(2 * j + 2) * 128],
                            etac[:, 0:1], P64[:, 2 * j * 128:(2 * j + 1) * 128],
                            ALU.mult, ALU.add)
                    for _ in range(3):
                        v.engine_nop()
                    v.scalar_tensor_tensor(
                        P64[:, 12 * 128:13 * 128], P64[:, 9 * 128:10 * 128],
                        etac[:, 1:2], P64[:, 8 * 128:9 * 128], ALU.mult, ALU.add)
                    v.scalar_tensor_tensor(
                        P64[:, 13 * 128:14 * 128], P64[:, 11 * 128:12 * 128],
                        etac[:, 1:2], P64[:, 10 * 128:11 * 128], ALU.mult, ALU.add)
                    for _ in range(4):
                        v.engine_nop()
                    return v.scalar_tensor_tensor(
                        E_sb[:, st * 128:(st + 1) * 128],
                        P64[:, 13 * 128:14 * 128], etac[:, 2:3],
                        P64[:, 12 * 128:13 * 128], ALU.mult, ALU.add)
                return fn
            ctx.emit("vector", mkE(st), ve)
            E_ready_ve = ve.n
        for tr in range(ntr if phase >= 6 else 0):
            def mky(tr):
                def fn(v):
                    return v.tensor_add(y_sb[:, tr * 128:(tr + 1) * 128],
                                        E_sb[:, tr * 128:(tr + 1) * 128],
                                        v_sb[:, tr * 128:(tr + 1) * 128])
                return fn
            ctx.emit("vector", mky(tr), ve)
            ctx.wait("gpsimd", ve)
            y_st[(ch, tr)] = ctx.emit("gpsimd", lambda g, ch=ch, tr=tr: g.dma_start(
                y_d[ch * C:(ch + 1) * C].rearrange("s b e -> (s b) e")
                [tr * 128:(tr + 1) * 128, :],
                y_sb[:, tr * 128:(tr + 1) * 128]), gst, 16)
        for c in range(nd if phase >= 6 else 0):
            pi = 3 + c % 2
            ctx.wait("tensor", ve, pool_free[pi])
            ctx.wait("tensor", ve, E_ready_ve)
            def mkwu(c, pi):
                def fn(te):
                    for tr in range(ntr):
                        inst = te.matmul(
                            pool[pi][:, 0:128],
                            xc_tm[:, tr * 1024 + c * 128:tr * 1024 + (c + 1) * 128],
                            E_sb[:, tr * 128:(tr + 1) * 128],
                            start=(tr == 0), stop=(tr == ntr - 1))
                    return inst
                return fn
            ctx.emit("tensor", mkwu(c, pi), pe)
            ctx.wait("vector", pe)
            def mkwue(c, pi):
                def fn(v):
                    return v.scalar_tensor_tensor(
                        Wst[:, c * 128:(c + 1) * 128], pool[pi][:, 0:128],
                        etac[:, 0:1], Wst[:, c * 128:(c + 1) * 128],
                        ALU.mult, ALU.add)
                return fn
            pool_free[pi] = ctx.emit("vector", mkwue(c, pi), ve)
        wupd_pe = pe.n
        if ch < nch - 1 and phase >= 6:
            def mkwbf(v):
                for _ in range(5):
                    v.engine_nop()
                return v.tensor_copy(Wbf[:, :], Wst[:, :])
            wbf_ve = ctx.emit("vector", mkwbf, ve)
    if phase < 6:
        ctx.emit("vector", lambda v: v.memset(y_sb[:, :], 0.0), ve)
        ctx.wait("gpsimd", ve)
        ctx.emit("gpsimd", lambda g: g.dma_start(
            y_d[0:1].rearrange("s b e -> (s b) e"), y_sb[0:16, 0:128]), gst, 16)
    ctx.wait("gpsimd", gst)
    ctx.build()
    return nc



"""L2_ffn: 2-layer local-window transformer (window 512, rotary-xpos,
l2norm qk ×scale, shift_token, LN·lg+lb + gelu-erf GLU FFN). Own launch.

Per inner layer l: x1 = x + attn(shift(x)); x2 = x1 + ffn(LN(shift(x1))).
x starts as xn (from xnfm_d); after l=1: out = xaccp + x2.
T tokens per core = n_seq * SL. th = 512-token chunk (4 tiles).
fm layouts: (NTB, 8, 128, 1024) bf16.
"""


def host_rotary(SL, window=512):
    half = np.arange(0, 64, 2, dtype=np.float64)
    inv_freq = 1.0 / 10000.0 ** (half / 64.0)
    t = np.arange(SL, dtype=np.float64)
    fr = t[:, None] * inv_freq[None, :]
    fr = np.concatenate([fr, fr], axis=-1)
    base = (half + 0.4 * 64) / (1.4 * 64)
    power = (t - SL // 2) / float(window // 2)
    sc = base[None, :] ** power[:, None]
    sc = np.concatenate([sc, sc], axis=-1)
    cos, sin = np.cos(fr), np.sin(fr)
    til = lambda a: np.tile(a, (1, 8)).astype(np.float32)
    return np.stack([til(cos * sc * 8.0), til(sin * sc * 8.0),
                     til(cos / sc), til(sin / sc)])


def host_pack_l2ffn(wqkv, qs, ksc, wo, lg, lb, w1, b1, w2, b2):
    D, HID, HP = 1024, 2730, 3072
    w1a = np.zeros((D, HP), np.float32); w1a[:, :HID] = w1[:, :HID]
    w1g = np.zeros((D, HP), np.float32); w1g[:, :HID] = w1[:, HID:]
    b1a = np.zeros((1, HP), np.float32); b1a[0, :HID] = b1[:HID]
    b1g = np.zeros((1, HP), np.float32); b1g[0, :HID] = b1[HID:]
    w2p = np.zeros((HP, D), np.float32); w2p[:HID] = w2
    return {
        "wqkv": np.ascontiguousarray(wqkv.reshape(8, 128, 1536)),
        "qs": np.tile(qs, 8).reshape(1, 512).astype(np.float32),
        "ks": np.tile(ksc, 8).reshape(1, 512).astype(np.float32),
        "wos": np.ascontiguousarray(
            wo.reshape(8, 64, 1024).transpose(1, 0, 2).reshape(64, 8192)),
        "lgb": np.stack([lg, lb]).reshape(2, 1024).astype(np.float32),
        "w1a": np.ascontiguousarray(w1a.reshape(8, 128, 6, 512).transpose(2, 0, 1, 3)),
        "w1g": np.ascontiguousarray(w1g.reshape(8, 128, 6, 512).transpose(2, 0, 1, 3)),
        "w2": np.ascontiguousarray(w2p.reshape(6, 4, 128, 1024)),
        "b1a": b1a, "b1g": b1g,
        "b2": b2.reshape(1, 1024).astype(np.float32),
    }


def make_blocks_local(qt):
    blocks = []
    klo = qt * 128 - 512
    if klo >= 0:
        blocks.append((klo, 128, "diagT"))
        k0 = klo + 128
    else:
        k0 = 0
    while k0 < qt * 128:
        blocks.append((k0, 128, None))
        k0 += 128
    blocks.append((qt * 128, 128, "diag"))
    return max(klo, 0), qt * 128 + 128, blocks


class NS:
    pass


def build_l2ffn(T=4096, SL=2048):
    NT, NTB, NTH = T // 128, T // 1024, T // 512
    D, nd = 1024, 8
    nc = bass.Bass("TRN2", target_bir_lowering=False)
    ctx = Ctx(nc)

    xnfm_d = nc.dram_tensor("xnfm", [NTB, nd, 128, 1024], BF16,
                            kind="ExternalInput").ap()
    xaccp_d = nc.dram_tensor("xaccp", [NT, 128, D], F32, kind="ExternalInput").ap()
    rot_d = nc.dram_tensor("rot", [4, SL, 512], F32, kind="ExternalInput").ap()
    id_d = nc.dram_tensor("ident", [128, 128], F32, kind="ExternalInput").ap()
    mk_d = nc.dram_tensor("masks_p", [128, 256], BF16, kind="ExternalInput").ap()
    wd = []
    for l in range(2):
        w = NS()
        for nm, shp in (("wqkv", [8, 128, 1536]), ("qs", [1, 512]), ("ks", [1, 512]),
                        ("wos", [64, 8192]), ("lgb", [2, 1024]),
                        ("w1a", [6, 8, 128, 512]), ("w1g", [6, 8, 128, 512]),
                        ("w2", [6, 4, 128, 1024]), ("b1a", [1, 3072]),
                        ("b1g", [1, 3072]), ("b2", [1, 1024])):
            setattr(w, nm, nc.dram_tensor(f"{nm}_{l}", shp, F32,
                                          kind="ExternalInput").ap())
        wd.append(w)
    out_d = nc.dram_tensor("out", [NT, 128, D], F32, kind="ExternalOutput").ap()
    x1fm_d = nc.dram_tensor("x1fm", [NTB, nd, 128, 1024], BF16, kind="Internal").ap()
    x1tm_d = nc.dram_tensor("x1tm", [NT, 128, D], BF16, kind="Internal").ap()
    x2fm_d = nc.dram_tensor("x2fm", [NTB, nd, 128, 1024], BF16, kind="Internal").ap()

    arena = nc.alloc_sbuf_tensor("arena", [128, 37376], BF16)
    Qr = arena[0:64, 0:4096]
    Kr = arena[0:64, 4096:12288]
    Vr = arena[:, 12288:16384]
    AO = arena[0:64, 16384:20480]
    rxn = arena[:, 20480:24576]
    x1th = arena[:, 24576:28672]
    qkbf = arena[:, 28672:29696]
    h2nfm = arena[:, 4096:8192]
    rx1tm = arena[:, 8192:12288]
    hp_fm = arena[:, 12288:12800]
    wsl = [arena[:, 12800 + i * 12288:12800 + (i + 1) * 12288] for i in range(2)]

    hfm = [nc.alloc_sbuf_tensor("hfm0", [128, 4096], BF16)]
    stage = [nc.alloc_sbuf_tensor(f"stg{i}", [128, 1024], F32) for i in range(2)]
    rotb = nc.alloc_sbuf_tensor("rotb", [128, 2048], BF16)
    qsb = nc.alloc_sbuf_tensor("qsb", [128, 1024], BF16)
    wqkv_sb = nc.alloc_sbuf_tensor("wqkv_sb", [128, 12288], BF16)
    wos_sb = nc.alloc_sbuf_tensor("wos_sb", [64, 8192], BF16)
    lgb = nc.alloc_sbuf_tensor("lgb", [128, 2048], BF16)
    rows_b = nc.alloc_sbuf_tensor("rows_b", [1, 8192], BF16)
    rowbuf = nc.alloc_sbuf_tensor("rowbuf", [1, 4096], BF16)
    x2acc = nc.alloc_sbuf_tensor("x2acc", [128, 4096], F32)
    h2t = nc.alloc_sbuf_tensor("h2t", [128, 1024], BF16)
    h2n = nc.alloc_sbuf_tensor("h2n", [128, 1024], BF16)
    gbuf = nc.alloc_sbuf_tensor("gbuf", [128, 512], F32)
    hp_tm = nc.alloc_sbuf_tensor("hp_tm", [128, 512], BF16)
    x2bf = h2t
    xaccp_sb = nc.alloc_sbuf_tensor("xaccp_sb", [128, 1024], F32)
    stg_fm = [nc.alloc_sbuf_tensor(f"stgfm{i}", [128, 1024], BF16) for i in range(2)]
    att_p = nc.alloc_sbuf_tensor("att_p", [128, 640], BF16)
    ptb = nc.alloc_sbuf_tensor("ptb", [128, 640], BF16)
    masks = nc.alloc_sbuf_tensor("masks", [128, 256], BF16)
    identbf = nc.alloc_sbuf_tensor("identbf", [128, 128], BF16)
    ones1 = nc.alloc_sbuf_tensor("ones1", [1, 128], BF16)
    ones32 = nc.alloc_sbuf_tensor("ones32", [1, 128], F32)
    epsc = nc.alloc_sbuf_tensor("epsc", [128, 1], F32)
    tiny = nc.alloc_sbuf_tensor("tiny", [128, 1], F32)
    stats = nc.alloc_sbuf_tensor("stats", [128, 64], F32)
    spacer = stage[1][:, 0:512]

    pool = [nc.alloc_psum_tensor(f"pool{i}", [128, 512], F32) for i in range(7)]
    poolB = nc.alloc_psum_tensor("poolB", [128, 1024], BF16)

    pe = ctx.chan("pe"); ve = ctx.chan("ve"); se = ctx.chan("se")
    gst = ctx.chan("gst")
    ch_h = [ctx.chan(f"c_h{i}") for i in range(2)]
    ch_rx = ctx.chan("c_rx")
    ch_rx1 = ctx.chan("c_rx1")
    ch_xa = ctx.chan("c_xa")
    ch_rot = ctx.chan("c_rot")
    ch_w = [ctx.chan(f"c_w{i}") for i in range(2)]
    ch_c = ctx.chan("c_c")
    ch_cg = ctx.chan("c_cg")
    ch_st = ctx.chan("c_st")
    ch_lw = ctx.chan("c_lw")

    ctx.emit("vector", lambda v: v.memset(ones1[:, :], 1.0), ve)
    ctx.emit("vector", lambda v: v.memset(ones32[:, :], 1.0), ve)
    ctx.emit("vector", lambda v: v.memset(epsc[:, :], 1e-5), ve)
    ctx.emit("vector", lambda v: v.memset(tiny[:, :], 1e-30), ve)
    ones_ve = ve.n
    ctx.emit("gpsimd", lambda g: g.dma_start(identbf[:, :], id_d), ch_cg, 16)
    ctx.emit("sync", lambda s: s.dma_start(masks[:, :], mk_d), ch_c, 16)

    pool_free = [ve.n] * 7
    poolB_free = ve.n
    hfm_pe = [pe.n]
    hfm_mz = [-1, -1]
    stage_ve = ve.n
    layer_pe = pe.n
    rot_ve = -1
    wsl_pe = [pe.n, pe.n]
    rxn_ve = ve.n
    rx1_ve = ve.n
    xaccp_ve = ve.n
    gst_guard = -1
    stg_fm_tgt = [-1, -1]

    def emit_shift_load(buf, fm_d, th, chan, pe_guard):
        g0 = th * 512
        tb, off = g0 // 1024, g0 % 1024
        seq_start = (g0 % SL) == 0
        ctx.wait("sync", pe, pe_guard)
        ctx.emit("sync", lambda s: s.dma_start(
            buf[:, 0:2048].rearrange("p (c t) -> p c t", c=4),
            fm_d[tb, 0:4, :, off:off + 512].rearrange("c p t -> p c t")), chan, 16)
        mz = -1
        if seq_start:
            ctx.emit("sync", lambda s: s.dma_start(
                buf[:, 2048:4096].rearrange("p (c t) -> p c t", c=4)[:, :, 1:512],
                fm_d[tb, 4:8, :, off:off + 511].rearrange("c p t -> p c t")), chan, 16)
            ctx.wait("vector", pe, pe_guard)
            def mkz(v):
                inst = None
                for c in range(4):
                    inst = v.memset(buf[:, 2048 + c * 512:2048 + c * 512 + 1], 0.0)
                return inst
            ctx.emit("vector", mkz, ve)
            mz = ve.n
        elif off == 0:
            for c in range(4):
                ctx.emit("sync", lambda s, c=c: s.dma_start(
                    buf[:, 2048 + c * 512:2048 + c * 512 + 1],
                    fm_d[tb - 1, 4 + c, :, 1023:1024],
                    allow_slow_non_contiguous=True), chan, 16)
            ctx.emit("sync", lambda s: s.dma_start(
                buf[:, 2048:4096].rearrange("p (c t) -> p c t", c=4)[:, :, 1:512],
                fm_d[tb, 4:8, :, 0:511].rearrange("c p t -> p c t")), chan, 16)
        else:
            ctx.emit("sync", lambda s: s.dma_start(
                buf[:, 2048:4096].rearrange("p (c t) -> p c t", c=4),
                fm_d[tb, 4:8, :, off - 1:off + 511].rearrange("c p t -> p c t")), chan, 16)
        return chan.n, mz

    for l in range(2):
        w = wd[l]
        src_fm = xnfm_d if l == 0 else x2fm_d
        ctx.wait("gpsimd", pe, layer_pe)
        ctx.emit("gpsimd", lambda g, w=w: g.dma_start(
            wqkv_sb[:, :].rearrange("p (c j) -> p c j", j=1536),
            w.wqkv.rearrange("c p j -> p c j")), ch_lw, 16)
        ctx.emit("gpsimd", lambda g, w=w: g.dma_start(wos_sb[:, :], w.wos), ch_lw, 16)
        ctx.emit("gpsimd", lambda g, w=w: g.dma_start(rows_b[0:1, 0:3072], w.b1a), ch_lw, 16)
        ctx.emit("gpsimd", lambda g, w=w: g.dma_start(rows_b[0:1, 3072:6144], w.b1g), ch_lw, 16)
        ctx.emit("gpsimd", lambda g, w=w: g.dma_start(rows_b[0:1, 6144:7168], w.b2), ch_lw, 16)
        ctx.emit("gpsimd", lambda g, w=w: g.dma_start(rowbuf[0:1, 0:1024], w.lgb[0:1]), ch_lw, 16)
        ctx.emit("gpsimd", lambda g, w=w: g.dma_start(rowbuf[0:1, 1024:2048], w.lgb[1:2]), ch_lw, 16)
        ctx.emit("gpsimd", lambda g, w=w: g.dma_start(rowbuf[0:1, 2048:2560], w.qs), ch_lw, 16)
        ctx.emit("gpsimd", lambda g, w=w: g.dma_start(rowbuf[0:1, 2560:3072], w.ks), ch_lw, 16)
        lw_gp = ch_lw.n
        ctx.wait("tensor", ch_lw, lw_gp)
        ctx.wait("tensor", ch_c)
        ctx.wait("tensor", ve, ones_ve)
        for half, src0, src1, dst0, dst1 in (
                (0, rowbuf[0:1, 0:512], rowbuf[0:1, 512:1024], lgb[:, 0:512], lgb[:, 512:1024]),
                (1, rowbuf[0:1, 1024:1536], rowbuf[0:1, 1536:2048], lgb[:, 1024:1536], lgb[:, 1536:2048]),
                (2, rowbuf[0:1, 2048:2560], rowbuf[0:1, 2560:3072], qsb[:, 0:512], qsb[:, 512:1024])):
            ctx.wait("tensor", ve, pool_free[0])
            ctx.wait("tensor", ve, pool_free[1])
            def mkm(src0=src0, src1=src1):
                def fn(te):
                    te.matmul(pool[0][:, 0:512], ones1[0:1, :], src0,
                              start=True, stop=True)
                    return te.matmul(pool[1][:, 0:512], ones1[0:1, :], src1,
                                     start=True, stop=True)
                return fn
            ctx.emit("tensor", mkm(), pe)
            ctx.wait("vector", pe)
            ctx.emit("vector", lambda v, dst0=dst0: v.tensor_copy(dst0, pool[0][:, 0:512]), ve)
            pool_free[0] = ve.n
            ctx.emit("vector", lambda v, dst1=dst1: v.tensor_copy(dst1, pool[1][:, 0:512]), ve)
            pool_free[1] = ve.n
        lw_ve = ve.n

        rxn_ve = ve.n
        for th in range(NTH):
            s = (th * 512) // SL
            th_in_seq = (th * 512 % SL) // 512
            hb = 0
            ctx.wait("sync", ch_st)
            htgt, hmz = emit_shift_load(hfm[hb], src_fm, th, ch_h[hb], hfm_pe[hb])
            g0 = th * 512
            tb, off = g0 // 1024, g0 % 1024
            ctx.wait("sync", ve, rxn_ve)
            ctx.wait("sync", pe, layer_pe)
            rxt = ctx.emit("sync", lambda s2, tb=tb, off=off, src_fm=src_fm: s2.dma_start(
                rxn[:, :].rearrange("p (c t) -> p c t", c=8),
                src_fm[tb, :, :, off:off + 512].rearrange("c p t -> p c t")), ch_rx, 16)
            for tt in range(4):
                qt = th_in_seq * 4 + tt
                gt = th * 4 + tt
                slot8 = gt % 8
                rb = 0
                ctx.wait("gpsimd", ve, rot_ve)
                rott = ctx.emit("gpsimd", lambda g, qt=qt: g.dma_start(
                    rotb[:, 0:2048].rearrange("p (a t) -> p a t", a=4),
                    rot_d[:, qt * 128:(qt + 1) * 128, :].rearrange("a t j -> t a j")),
                    ch_rot, 16)
                ctx.wait("tensor", ch_h[hb], htgt)
                if hmz >= 0:
                    ctx.wait("tensor", ve, hmz)
                ctx.wait("tensor", ch_lw, lw_gp)
                ctx.wait("tensor", ve, pool_free[2])
                ctx.wait("tensor", ve, pool_free[3])
                ctx.wait("tensor", ve, pool_free[5])
                def mkqkv(hb, tt):
                    def fn(te):
                        for part, pi in ((0, 2), (1, 3), (2, 5)):
                            for c in range(nd):
                                inst = te.matmul(
                                    pool[pi][:, 0:512],
                                    hfm[hb][:, c * 512 + tt * 128:c * 512 + (tt + 1) * 128],
                                    wqkv_sb[:, c * 1536 + part * 512:c * 1536 + (part + 1) * 512],
                                    start=(c == 0), stop=(c == nd - 1))
                        return inst
                    return fn
                ctx.emit("tensor", mkqkv(hb, tt), pe)
                hfm_pe[hb] = pe.n
                ctx.wait("vector", pe)
                pool_free[5] = ctx.emit("vector", lambda v, slot8=slot8: v.tensor_copy(
                    Vr[:, slot8 * 512:(slot8 + 1) * 512], pool[5][:, 0:512]), ve)
                ctx.wait("vector", ve, stage_ve)
                ctx.emit("vector", lambda v: v.tensor_copy(
                    stage[0][:, 0:512], pool[2][:, 0:512]), ve)
                pool_free[2] = ve.n
                ctx.emit("vector", lambda v: v.tensor_copy(
                    stage[0][:, 512:1024], pool[3][:, 0:512]), ve)
                pool_free[3] = ve.n
                qk32_ve = ve.n
                ctx.wait("scalar", ve, qk32_ve)
                def mksq2(sc):
                    sc.activation(stage[1][:, 0:512], stage[0][:, 0:512], AF.Square)
                    return sc.activation(stage[1][:, 512:1024], stage[0][:, 512:1024],
                                         AF.Square)
                ctx.emit("scalar", mksq2, se)
                ctx.wait("vector", se)
                def mkred(v):
                    for h in range(16):
                        inst = v.tensor_reduce(stats[:, h:h + 1],
                                               stage[1][:, h * 64:(h + 1) * 64],
                                               AX.X, ALU.add)
                    return inst
                ctx.emit("vector", mkred, ve)
                ctx.wait("scalar", ve)
                def mkln(sc):
                    sc.activation(stats[:, 16:32], stats[:, 0:16], AF.Ln,
                                  bias=tiny[:, 0:1])
                    sc.activation(spacer[:, 0:512], stage[0][:, 0:512], AF.Copy)
                    return sc.activation(stats[:, 16:32], stats[:, 16:32], AF.Exp,
                                         scale=-0.5)
                ctx.emit("scalar", mkln, se)
                ctx.wait("vector", se)
                ctx.wait("vector", ve, lw_ve)
                def mkapp(v):
                    for h in range(8):
                        v.tensor_scalar(stage[0][:, h * 64:(h + 1) * 64],
                                        stage[0][:, h * 64:(h + 1) * 64],
                                        stats[:, 16 + h:17 + h], None, ALU.mult)
                    for h in range(8):
                        v.tensor_scalar(stage[0][:, 512 + h * 64:512 + (h + 1) * 64],
                                        stage[0][:, 512 + h * 64:512 + (h + 1) * 64],
                                        stats[:, 24 + h:25 + h], None, ALU.mult)
                    v.tensor_mul(stage[0][:, 0:512], stage[0][:, 0:512], qsb[:, 0:512])
                    return v.tensor_mul(stage[0][:, 512:1024], stage[0][:, 512:1024],
                                        qsb[:, 512:1024])
                ctx.emit("vector", mkapp, ve)
                ctx.wait("vector", ch_rot, rott)
                def mkrot(rb):
                    def fn(v):
                        q3 = stage[0][:, 0:512].rearrange("p (h a) -> p h a", a=64)
                        k3 = stage[0][:, 512:1024].rearrange("p (h a) -> p h a", a=64)
                        r3 = stage[1][:, 0:512].rearrange("p (h a) -> p h a", a=64)
                        rk3 = stage[1][:, 512:1024].rearrange("p (h a) -> p h a", a=64)
                        v.tensor_scalar(r3[:, :, 0:32], q3[:, :, 32:64], -1.0,
                                        None, ALU.mult)
                        v.tensor_copy(r3[:, :, 32:64], q3[:, :, 0:32])
                        v.tensor_scalar(rk3[:, :, 0:32], k3[:, :, 32:64], -1.0,
                                        None, ALU.mult)
                        v.tensor_copy(rk3[:, :, 32:64], k3[:, :, 0:32])
                        ro = rotb[:, rb * 2048:(rb + 1) * 2048]
                        v.tensor_mul(stage[0][:, 0:512], stage[0][:, 0:512], ro[:, 0:512])
                        v.tensor_mul(stage[0][:, 512:1024], stage[0][:, 512:1024],
                                     ro[:, 1024:1536])
                        v.tensor_mul(stage[1][:, 0:512], stage[1][:, 0:512],
                                     ro[:, 512:1024])
                        v.tensor_mul(stage[1][:, 512:1024], stage[1][:, 512:1024],
                                     ro[:, 1536:2048])
                        for _ in range(3):
                            v.engine_nop()
                        v.tensor_add(qkbf[:, 0:512], stage[0][:, 0:512],
                                     stage[1][:, 0:512])
                        return v.tensor_add(qkbf[:, 512:1024], stage[0][:, 512:1024],
                                            stage[1][:, 512:1024])
                    return fn
                ctx.emit("vector", mkrot(rb), ve)
                stage_ve = ve.n
                rot_ve = ve.n
                qkbf_ve = ve.n
                ctx.wait("tensor", ve, qkbf_ve)
                ctx.wait("tensor", ve, poolB_free)
                ctx.wait("tensor", ch_cg)
                def mktrq(te):
                    for h in range(8):
                        inst = te.transpose(poolB[0:64, h * 128:(h + 1) * 128],
                                            qkbf[:, h * 64:(h + 1) * 64], identbf[:, :])
                    return inst
                ctx.emit("tensor", mktrq, pe)
                ctx.wait("vector", pe)
                poolB_free = ctx.emit("vector", lambda v, tt=tt: v.tensor_copy(
                    Qr[:, tt * 1024:(tt + 1) * 1024], poolB[0:64, 0:1024]), ve)
                ctx.wait("tensor", ve, poolB_free)
                def mktrk(te):
                    for h in range(8):
                        inst = te.transpose(poolB[0:64, h * 128:(h + 1) * 128],
                                            qkbf[:, 512 + h * 64:512 + (h + 1) * 64],
                                            identbf[:, :])
                    return inst
                ctx.emit("tensor", mktrk, pe)
                ctx.wait("vector", pe)
                poolB_free = ctx.emit("vector", lambda v, slot8=slot8: v.tensor_copy(
                    Kr[:, slot8 * 1024:(slot8 + 1) * 1024], poolB[0:64, 0:1024]), ve)
            ring_ve = ve.n
            for tt in range(4):
                qt = th_in_seq * 4 + tt
                ext0, ext1, blocks = make_blocks_local(qt)
                ext = ext1 - ext0
                nb = len(blocks)
                for h in range(8):
                    ctx.wait("tensor", ve, ring_ve)
                    ctx.wait("tensor", ve, pool_free[0])
                    if ext > 512:
                        ctx.wait("tensor", ve, pool_free[1])
                    def mksc(h, tt, blocks, ext0, s):
                        def fn(te):
                            for (k0, wdt, msk) in blocks:
                                r = k0 - ext0
                                kslot = (s * (SL // 128) + k0 // 128) % 8
                                inst = te.matmul(
                                    pool[r // 512][:, r % 512:r % 512 + 128],
                                    Qr[:, tt * 1024 + h * 128:tt * 1024 + (h + 1) * 128],
                                    Kr[:, kslot * 1024 + h * 128:kslot * 1024 + (h + 1) * 128],
                                    start=True, stop=True)
                            return inst
                        return fn
                    ctx.emit("tensor", mksc(h, tt, blocks, ext0, s), pe)
                    ctx.wait("vector", pe)
                    ctx.wait("vector", ch_c)
                    def mkmask(blocks, ext0):
                        def fn(v):
                            inst = v.engine_nop()
                            for (k0, wdt, msk) in blocks:
                                if msk is None:
                                    continue
                                r = k0 - ext0
                                moff = 0 if msk == "diag" else 128
                                inst = v.tensor_tensor(
                                    pool[r // 512][:, r % 512:r % 512 + 128],
                                    pool[r // 512][:, r % 512:r % 512 + 128],
                                    masks[:, moff:moff + 128], ALU.add)
                            for _ in range(4):
                                v.engine_nop()
                            return inst
                        return fn
                    ctx.emit("vector", mkmask(blocks, ext0), ve)
                    def mkmp(blocks, ext0, nb):
                        def fn(v):
                            for i, (k0, wdt, msk) in enumerate(blocks):
                                r = k0 - ext0
                                v.tensor_reduce(stats[:, 40 + i:41 + i],
                                                pool[r // 512][:, r % 512:r % 512 + 128],
                                                AX.X, ALU.max)
                            mfin = _tree(v, stats, list(range(40, 40 + nb)), 48, "max")
                            return v.tensor_copy(stats[:, 54:55], stats[:, mfin:mfin + 1])
                        return fn
                    ctx.emit("vector", mkmp(blocks, ext0, nb), ve)
                    mt_ve = ve.n
                    ctx.wait("scalar", ve, mt_ve)
                    def mkexp(blocks, ext0):
                        def fn(sc):
                            sc.activation(stats[:, 55:56], stats[:, 54:55], AF.Copy,
                                          scale=-1.0)
                            sc.activation(spacer[:, 0:256], masks[:, 0:256], AF.Copy)
                            inst = None
                            for i, (k0, wdt, msk) in enumerate(blocks):
                                r = k0 - ext0
                                inst = sc.activation(
                                    att_p[:, r:r + 128],
                                    pool[r // 512][:, r % 512:r % 512 + 128], AF.Exp,
                                    bias=stats[:, 55:56],
                                    accum_out=stats[:, 56 + i:57 + i])
                            return inst
                        return fn
                    ctx.emit("scalar", mkexp(blocks, ext0), se)
                    exp_se = se.n
                    ctx.wait("vector", se, exp_se)
                    def mklt(nb, ext):
                        def fn(v):
                            lfin = _tree(v, stats, list(range(56, 56 + nb)), 32, "add")
                            v.reciprocal(stats[:, 38:39], stats[:, lfin:lfin + 1])
                            for _ in range(6):
                                v.engine_nop()
                            return v.tensor_scalar(att_p[:, 0:ext], att_p[:, 0:ext],
                                                   stats[:, 38:39], None, ALU.mult)
                        return fn
                    ctx.emit("vector", mklt(nb, ext), ve)
                    pnorm_ve = ve.n
                    pool_free[0] = pnorm_ve
                    if ext > 512:
                        pool_free[1] = pnorm_ve
                    nkt = ext // 128
                    ctx.wait("tensor", ve, pnorm_ve)
                    ctx.wait("tensor", ve, poolB_free)
                    ctx.wait("tensor", ve, pool_free[4])
                    def mkpt(nkt):
                        def fn(te):
                            for j in range(nkt):
                                inst = te.transpose(
                                    poolB[:, j * 128:(j + 1) * 128],
                                    att_p[:, j * 128:(j + 1) * 128], identbf[:, :])
                            return inst
                        return fn
                    ctx.emit("tensor", mkpt(nkt), pe)
                    ctx.wait("vector", pe)
                    poolB_free = ctx.emit("vector", lambda v, nkt=nkt: v.tensor_copy(
                        ptb[:, 0:nkt * 128], poolB[:, 0:nkt * 128]), ve)
                    ctx.wait("tensor", ve, poolB_free)
                    def mkpv(nkt, ext0, h, s):
                        def fn(te):
                            for j in range(nkt):
                                kslot = (s * (SL // 128) + ext0 // 128 + j) % 8
                                inst = te.matmul(
                                    pool[4][0:64, 0:128],
                                    Vr[:, kslot * 512 + h * 64:kslot * 512 + (h + 1) * 64],
                                    ptb[:, j * 128:(j + 1) * 128],
                                    start=(j == 0), stop=(j == nkt - 1))
                            return inst
                        return fn
                    ctx.emit("tensor", mkpv(nkt, ext0, h, s), pe)
                    ctx.wait("vector", pe)
                    pool_free[4] = ctx.emit("vector", lambda v, tt=tt, h=h: v.tensor_copy(
                        AO[:, tt * 1024 + h * 128:tt * 1024 + (h + 1) * 128],
                        pool[4][0:64, 0:128]), ve)
            ao_ve = ve.n
            for c8 in range(8):
                pi = 6 if c8 % 2 == 0 else 2
                ctx.wait("tensor", ve, pool_free[pi])
                ctx.wait("tensor", ve, ao_ve)
                def mkop(c8, pi):
                    def fn(te):
                        for ttr in range(4):
                            for h in range(8):
                                inst = te.matmul(
                                    pool[pi][:, ttr * 128:(ttr + 1) * 128],
                                    wos_sb[:, h * 1024 + c8 * 128:h * 1024 + (c8 + 1) * 128],
                                    AO[:, ttr * 1024 + h * 128:ttr * 1024 + (h + 1) * 128],
                                    start=(h == 0), stop=(h == 7))
                        return inst
                    return fn
                ctx.emit("tensor", mkop(c8, pi), pe)
                ctx.wait("vector", pe)
                ctx.wait("vector", ch_rx, rxt)
                if c8 == 0:
                    ctx.wait("vector", ch_st)
                def mkopev(c8, pi):
                    def fn(v):
                        return v.tensor_add(x1th[:, c8 * 512:(c8 + 1) * 512],
                                            pool[pi][:, 0:512],
                                            rxn[:, c8 * 512:(c8 + 1) * 512])
                    return fn
                pool_free[pi] = ctx.emit("vector", mkopev(c8, pi), ve)
            x1th_ve = ve.n
            rxn_ve = x1th_ve
            ctx.wait("sync", ve, x1th_ve)
            ctx.emit("sync", lambda s2, tb=tb, off=off: s2.dma_start(
                x1fm_d[tb, :, :, off:off + 512].rearrange("c p t -> p c t"),
                x1th[:, :].rearrange("p (c t) -> p c t", c=8)), ch_st, 16)
            for ttr in range(4):
                bi = ttr % 2
                ctx.wait("tensor", ve, x1th_ve)
                ctx.wait("tensor", ve, poolB_free)
                def mktrx(ttr):
                    def fn(te):
                        for c in range(8):
                            inst = te.transpose(
                                poolB[:, c * 128:(c + 1) * 128],
                                x1th[:, c * 512 + ttr * 128:c * 512 + (ttr + 1) * 128],
                                identbf[:, :])
                        return inst
                    return fn
                ctx.emit("tensor", mktrx(ttr), pe)
                ctx.wait("vector", pe)
                ctx.wait("vector", ch_st, stg_fm_tgt[bi])
                poolB_free = ctx.emit("vector", lambda v, bi=bi: v.tensor_copy(
                    stg_fm[bi][:, :], poolB[:, :]), ve)
                ctx.wait("sync", ve)
                stg_fm_tgt[bi] = ctx.emit("sync", lambda s2, th=th, ttr=ttr, bi=bi:
                    s2.dma_start(x1tm_d[th * 4 + ttr], stg_fm[bi][:, :]), ch_st, 16)

        stb_pe0 = pe.n
        wsl_pe = [stb_pe0, stb_pe0]
        rx1_ve = ve.n
        for th in range(NTH):
            hb = 0
            g0 = th * 512
            tb, off = g0 // 1024, g0 % 1024
            ctx.wait("sync", ch_st)
            htgt, hmz = emit_shift_load(hfm[hb], x1fm_d, th, ch_h[hb], hfm_pe[hb])
            ctx.wait("sync", ve, rx1_ve)
            ctx.wait("sync", pe, stb_pe0)
            rx1t = ctx.emit("sync", lambda s2, th=th: s2.dma_start(
                rx1tm[:, :].rearrange("p (a t) -> p a t", a=4),
                x1tm_d[th * 4:th * 4 + 4].rearrange("a p t -> p a t")), ch_rx1, 16)
            for tt in range(4):
                ctx.wait("tensor", ch_h[hb], htgt)
                if hmz >= 0:
                    ctx.wait("tensor", ve, hmz)
                ctx.wait("tensor", ve, poolB_free)
                def mktr2(hb, tt):
                    def fn(te):
                        for c in range(8):
                            inst = te.transpose(
                                poolB[:, c * 128:(c + 1) * 128],
                                hfm[hb][:, c * 512 + tt * 128:c * 512 + (tt + 1) * 128],
                                identbf[:, :])
                        return inst
                    return fn
                ctx.emit("tensor", mktr2(hb, tt), pe)
                hfm_pe[hb] = pe.n
                ctx.wait("vector", pe)
                poolB_free = ctx.emit("vector", lambda v: v.tensor_copy(
                    h2t[:, :], poolB[:, :]), ve)
                ctx.emit("vector", lambda v: v.tensor_reduce(
                    stats[:, 0:1], h2t[:, :], AX.X, ALU.add, negate=True), ve)
                ctx.wait("scalar", ve)
                def mkst2(sc):
                    sc.activation(spacer[:, 0:512], h2t[:, 0:512], AF.Square,
                                  accum_out=stats[:, 1:2])
                    sc.activation(spacer[:, 0:512], h2t[:, 512:1024], AF.Square,
                                  accum_out=stats[:, 2:3])
                    sc.activation(stats[:, 3:4], stats[:, 0:1], AF.Copy, scale=1.0 / D)
                    return sc.activation(stats[:, 4:5], stats[:, 0:1], AF.Square,
                                         scale=1.0 / D)
                ctx.emit("scalar", mkst2, se)
                ctx.wait("vector", se)
                def mkvar(v):
                    v.tensor_add(stats[:, 5:6], stats[:, 1:2], stats[:, 2:3])
                    for _ in range(4):
                        v.engine_nop()
                    return v.scalar_tensor_tensor(stats[:, 5:6], stats[:, 5:6],
                                                  1.0 / D, stats[:, 4:5],
                                                  ALU.mult, ALU.subtract)
                ctx.emit("vector", mkvar, ve)
                ctx.wait("scalar", ve)
                def mkrstd2(sc):
                    sc.activation(stats[:, 6:7], stats[:, 5:6], AF.Ln, bias=epsc[:, 0:1])
                    sc.activation(spacer[:, 0:512], h2t[:, 0:512], AF.Copy)
                    return sc.activation(stats[:, 6:7], stats[:, 6:7], AF.Exp, scale=-0.5)
                ctx.emit("scalar", mkrstd2, se)
                ctx.wait("vector", se)
                ctx.wait("vector", ve, lw_ve)
                def mkapply2(v):
                    v.tensor_scalar(stage[0][:, :], h2t[:, :], stats[:, 3:4],
                                    stats[:, 6:7], ALU.add, ALU.mult)
                    for _ in range(3):
                        v.engine_nop()
                    v.tensor_mul(stage[0][:, :], stage[0][:, :], lgb[:, 0:1024])
                    for _ in range(3):
                        v.engine_nop()
                    return v.tensor_add(h2n[:, :], stage[0][:, :], lgb[:, 1024:2048])
                ctx.emit("vector", mkapply2, ve)
                h2n_ve = ve.n
                ctx.wait("tensor", ve, h2n_ve)
                ctx.wait("tensor", ve, poolB_free)
                def mktr3(te):
                    for c in range(8):
                        inst = te.transpose(poolB[:, c * 128:(c + 1) * 128],
                                            h2n[:, c * 128:(c + 1) * 128], identbf[:, :])
                    return inst
                ctx.emit("tensor", mktr3, pe)
                ctx.wait("vector", pe)
                def mkevh(tt):
                    def fn(v):
                        for c in range(8):
                            inst = v.tensor_copy(
                                h2nfm[:, c * 512 + tt * 128:c * 512 + (tt + 1) * 128],
                                poolB[:, c * 128:(c + 1) * 128])
                        return inst
                    return fn
                poolB_free = ctx.emit("vector", mkevh(tt), ve)
            h2nfm_ve = ve.n
            for j in range(6):
                ws = j % 2
                ctx.wait("gpsimd", pe, wsl_pe[ws])
                ctx.wait("gpsimd", ch_st)
                ctx.emit("gpsimd", lambda g, j=j, ws=ws, w=w: g.dma_start(
                    wsl[ws][:, 0:4096].rearrange("p (c t) -> p c t", c=8),
                    w.w1a[j].rearrange("c p t -> p c t")), ch_w[ws], 16)
                ctx.emit("gpsimd", lambda g, j=j, ws=ws, w=w: g.dma_start(
                    wsl[ws][:, 4096:8192].rearrange("p (c t) -> p c t", c=8),
                    w.w1g[j].rearrange("c p t -> p c t")), ch_w[ws], 16)
                wt = ctx.emit("gpsimd", lambda g, j=j, ws=ws, w=w: g.dma_start(
                    wsl[ws][:, 8192:12288].rearrange("p (c t) -> p c t", c=4),
                    w.w2[j].rearrange("c p t -> p c t")), ch_w[ws], 16)
                for tt in range(4):
                    ctx.wait("tensor", ch_w[ws], wt)
                    ctx.wait("tensor", ve, h2nfm_ve)
                    ctx.wait("tensor", ve, pool_free[2])
                    ctx.wait("tensor", ve, pool_free[3])
                    ctx.wait("tensor", ch_lw, lw_gp)
                    ctx.wait("tensor", ve, ones_ve)
                    def mkag(j, ws, tt):
                        def fn(te):
                            te.matmul(pool[2][:, 0:512], ones1[0:1, :],
                                      rows_b[0:1, j * 512:(j + 1) * 512],
                                      start=True, stop=False)
                            for c in range(nd):
                                te.matmul(
                                    pool[2][:, 0:512],
                                    h2nfm[:, c * 512 + tt * 128:c * 512 + (tt + 1) * 128],
                                    wsl[ws][:, c * 512:(c + 1) * 512],
                                    start=False, stop=(c == nd - 1))
                            te.matmul(pool[3][:, 0:512], ones1[0:1, :],
                                      rows_b[0:1, 3072 + j * 512:3072 + (j + 1) * 512],
                                      start=True, stop=False)
                            for c in range(nd):
                                inst = te.matmul(
                                    pool[3][:, 0:512],
                                    h2nfm[:, c * 512 + tt * 128:c * 512 + (tt + 1) * 128],
                                    wsl[ws][:, 4096 + c * 512:4096 + (c + 1) * 512],
                                    start=False, stop=(c == nd - 1))
                            return inst
                        return fn
                    ctx.emit("tensor", mkag(j, ws, tt), pe)
                    ctx.wait("scalar", pe)
                    ctx.emit("scalar", lambda sc: sc.activation(
                        gbuf[:, :], pool[3][:, 0:512], AF.Gelu), se)
                    ctx.wait("vector", se)
                    ctx.emit("vector", lambda v: v.tensor_mul(
                        hp_tm[:, :], pool[2][:, 0:512], gbuf[:, :]), ve)
                    pool_free[2] = ve.n
                    pool_free[3] = ve.n
                    hp_ve = ve.n
                    ctx.wait("tensor", ve, hp_ve)
                    ctx.wait("tensor", ve, poolB_free)
                    def mktr4(te):
                        for c in range(4):
                            inst = te.transpose(poolB[:, c * 128:(c + 1) * 128],
                                                hp_tm[:, c * 128:(c + 1) * 128],
                                                identbf[:, :])
                        return inst
                    ctx.emit("tensor", mktr4, pe)
                    ctx.wait("vector", pe)
                    poolB_free = ctx.emit("vector", lambda v: v.tensor_copy(
                        hp_fm[:, 0:512], poolB[:, 0:512]), ve)
                    hpf_ve = ve.n
                    for dh in range(2):
                        pi = 4 + dh
                        ctx.wait("tensor", ve, hpf_ve)
                        ctx.wait("tensor", ve, pool_free[pi])
                        def mkw2(j, ws, dh, pi):
                            def fn(te):
                                if j == 0:
                                    te.matmul(pool[pi][:, 0:512], ones1[0:1, :],
                                              rows_b[0:1, 6144 + dh * 512:6144 + (dh + 1) * 512],
                                              start=True, stop=False)
                                inst = None
                                for c in range(4):
                                    inst = te.matmul(
                                        pool[pi][:, 0:512],
                                        hp_fm[:, c * 128:(c + 1) * 128],
                                        wsl[ws][:, 8192 + c * 1024 + dh * 512:
                                                8192 + c * 1024 + (dh + 1) * 512],
                                        start=(c == 0 and j != 0), stop=(c == 3))
                                return inst
                            return fn
                        ctx.emit("tensor", mkw2(j, ws, dh, pi), pe)
                        wsl_pe[ws] = pe.n
                        ctx.wait("vector", pe)
                        if j == 0:
                            ctx.wait("vector", ch_rx1, rx1t)
                            ctx.wait("vector", gst, gst_guard)
                            def mkx2i(tt, dh, pi):
                                def fn(v):
                                    return v.tensor_add(
                                        x2acc[:, tt * 1024 + dh * 512:tt * 1024 + (dh + 1) * 512],
                                        pool[pi][:, 0:512],
                                        rx1tm[:, tt * 1024 + dh * 512:tt * 1024 + (dh + 1) * 512])
                                return fn
                            pool_free[pi] = ctx.emit("vector", mkx2i(tt, dh, pi), ve)
                        else:
                            def mkx2a(tt, dh, pi):
                                def fn(v):
                                    dst = x2acc[:, tt * 1024 + dh * 512:tt * 1024 + (dh + 1) * 512]
                                    return v.scalar_tensor_tensor(
                                        dst, pool[pi][:, 0:512], 1.0, dst,
                                        ALU.mult, ALU.add)
                                return fn
                            pool_free[pi] = ctx.emit("vector", mkx2a(tt, dh, pi), ve)
            rx1_ve = ve.n
            for tt in range(4):
                gt = th * 4 + tt
                if l == 0:
                    ctx.emit("vector", lambda v, tt=tt: v.tensor_copy(
                        x2bf[:, :], x2acc[:, tt * 1024:(tt + 1) * 1024]), ve)
                    x2b_ve = ve.n
                    ctx.wait("tensor", ve, x2b_ve)
                    ctx.wait("tensor", ve, poolB_free)
                    def mktr5(te):
                        for c in range(8):
                            inst = te.transpose(poolB[:, c * 128:(c + 1) * 128],
                                                x2bf[:, c * 128:(c + 1) * 128],
                                                identbf[:, :])
                        return inst
                    ctx.emit("tensor", mktr5, pe)
                    ctx.wait("vector", pe)
                    bi = tt % 2
                    ctx.wait("vector", ch_st, stg_fm_tgt[bi])
                    poolB_free = ctx.emit("vector", lambda v, bi=bi: v.tensor_copy(
                        stg_fm[bi][:, :], poolB[:, :]), ve)
                    ctx.wait("sync", ve)
                    stg_fm_tgt[bi] = ctx.emit("sync",
                        lambda s2, tb=tb, off=off, tt=tt, bi=bi: s2.dma_start(
                            x2fm_d[tb, :, :, off + tt * 128:off + (tt + 1) * 128]
                            .rearrange("c p t -> p c t"),
                            stg_fm[bi][:, :].rearrange("p (c t) -> p c t", c=8)),
                        ch_st, 16)
                else:
                    ctx.wait("sync", ve, xaccp_ve)
                    xat = ctx.emit("sync", lambda s2, gt=gt: s2.dma_start(
                        xaccp_sb[:, :], xaccp_d[gt]), ch_xa, 16)
                    ctx.wait("vector", ch_xa, xat)
                    def mkout(tt):
                        def fn(v):
                            return v.tensor_add(x2acc[:, tt * 1024:(tt + 1) * 1024],
                                                x2acc[:, tt * 1024:(tt + 1) * 1024],
                                                xaccp_sb[:, :])
                        return fn
                    ctx.emit("vector", mkout(tt), ve)
                    xaccp_ve = ve.n
                    ctx.wait("gpsimd", ve)
                    ctx.emit("gpsimd", lambda g, gt=gt, tt=tt: g.dma_start(
                        out_d[gt], x2acc[:, tt * 1024:(tt + 1) * 1024]), gst, 16)
            gst_guard = gst.n
        layer_pe = pe.n
    ctx.wait("gpsimd", gst)
    ctx.build()
    return nc


import ml_dtypes
from concourse.bass_utils import run_bass_kernel_spmd

D_ = 1024
ND_ = 8
NT_ = 32
T_ = 4096
TBK = 1024
NSEQ = 2
SEQ_TB = 2


def build_l0rest():
    nc = bass.Bass("TRN2", target_bir_lowering=False)
    ctx = Ctx(nc)
    x_d = nc.dram_tensor("x", [NT_, 128, D_], F32, kind="ExternalInput").ap()
    y_d = nc.dram_tensor("y", [NT_, 128, D_], BF16, kind="ExternalInput").ap()
    id_d = nc.dram_tensor("ident", [128, 128], F32, kind="ExternalInput").ap()
    gw_d = nc.dram_tensor("gwp", [128, ND_ * 8], F32, kind="ExternalInput").ap()
    wl_d = nc.dram_tensor("wl_p", [8, 128, 1024], F32, kind="ExternalInput").ap()
    la_d = nc.dram_tensor("la_p", [8, 128, 32], F32, kind="ExternalInput").ap()
    ra_d = nc.dram_tensor("ra_p", [32, 1024], F32, kind="ExternalInput").ap()
    bl_d = nc.dram_tensor("blin_p", [1, D_], F32, kind="ExternalInput").ap()
    out_d = nc.dram_tensor("out", [NT_, 128, D_], F32, kind="ExternalOutput").ap()
    xnfm_d = nc.dram_tensor("xnfm", [T_ // TBK, ND_, 128, TBK], BF16, kind="Internal").ap()

    b = alloc_common(nc, ctx, NT_, D_, 8, 8, TBK)
    b.xnfm = [nc.alloc_sbuf_tensor(f"xnfm{i}", [128, ND_ * TBK], BF16) for i in range(2)]
    b.w1r = [nc.alloc_sbuf_tensor(f"w1r{i}", [128, ND_ * 128], BF16) for i in range(b.NWR)]
    b.wur = [nc.alloc_sbuf_tensor(f"wur{i}", [128, ND_ * 128], BF16) for i in range(b.NWR)]
    b.w2r = [nc.alloc_sbuf_tensor(f"w2r{i}", [128, D_], BF16) for i in range(b.NWR)]
    b.b2r = [nc.alloc_sbuf_tensor("b2r0", [1, D_], BF16)]
    b.ch_xnfm = [ctx.chan(f"c_xnf{i}") for i in range(2)]
    b.ch_w1 = [ctx.chan(f"c_w1_{i}") for i in range(b.NWR)]
    b.ch_wu = [ctx.chan(f"c_wu_{i}") for i in range(b.NWR)]
    b.ch_w2 = [ctx.chan(f"c_w2_{i}") for i in range(b.NWR)]
    b.ch_b2 = [ctx.chan("c_b2_0")]
    gw32 = nc.alloc_sbuf_tensor("gw32", [128, ND_ * 8], F32)

    emit_consts(b, id_d)
    alloc_attn(b, id_d)
    ctx.emit("sync", lambda s: s.dma_start(gw32[:, :], gw_d), b.ch_cst[0], 16)
    prologue(b, x_d, xnfm_d, gw32=gw32, gw_E=8, ln=True)
    softmax_top2(b, 8, renorm=False)
    cfg = {"wl_d": wl_d, "la_d": la_d, "ra_d": ra_d, "blin_d": bl_d,
           "xnfm_d": xnfm_d, "y_d": y_d}
    emit_l0ffn(b, cfg)
    ctx.wait("gpsimd", b.ve)
    for t in range(0, NT_, 8):
        ctx.emit("gpsimd", lambda g, t=t: g.dma_start(
            out_d[t:t + 8].rearrange("a p d -> p a d"),
            b.xacc[:, t * D_:(t + 8) * D_].rearrange("p (a d) -> p a d", d=D_)),
            b.gst, 16)
    ctx.wait("gpsimd", b.gst)
    ctx.build()
    return nc


def build_l1():
    nc = bass.Bass("TRN2", target_bir_lowering=False)
    ctx = Ctx(nc)
    x_d = nc.dram_tensor("x", [NT_, 128, D_], F32, kind="ExternalInput").ap()
    id_d = nc.dram_tensor("ident", [128, 128], F32, kind="ExternalInput").ap()
    qk_d = nc.dram_tensor("qk_p", [16, 128, 1024], F32, kind="ExternalInput").ap()
    v_d = nc.dram_tensor("v_p", [16, 128, 512], F32, kind="ExternalInput").ap()
    wo_d = nc.dram_tensor("wo_p", [8, 2, 64, 1024], F32, kind="ExternalInput").ap()
    bqk_d = nc.dram_tensor("bqk_p", [128, 32], F32, kind="ExternalInput").ap()
    bv_d = nc.dram_tensor("bv_p", [1, 1024], F32, kind="ExternalInput").ap()
    bo_d = nc.dram_tensor("bo_p", [128, 8], F32, kind="ExternalInput").ap()
    mk_d = nc.dram_tensor("masks_p", [128, 256], BF16, kind="ExternalInput").ap()
    gwr_d = nc.dram_tensor("gwr", [128, ND_ * 8], F32, kind="ExternalInput").ap()
    w1_d = nc.dram_tensor("w1L", [8, 32, 128, ND_ * 128], F32, kind="ExternalInput").ap()
    w2_d = nc.dram_tensor("w2R", [8, 32, 128, D_], F32, kind="ExternalInput").ap()
    b1_dd = nc.dram_tensor("b1p", [8, 128, 32], F32, kind="ExternalInput").ap()
    b2_dd = nc.dram_tensor("b2p", [1, 8 * D_], F32, kind="ExternalInput").ap()
    out_d = nc.dram_tensor("out", [NT_, 128, D_], F32, kind="ExternalOutput").ap()
    xnfm_d = nc.dram_tensor("xnfm", [T_ // TBK, ND_, 128, TBK], BF16, kind="Internal").ap()

    b = alloc_common(nc, ctx, NT_, D_, 8, 32, TBK)
    b.NWR = 2
    b.xnfm = [nc.alloc_sbuf_tensor(f"xnfm{i}", [128, ND_ * TBK], BF16) for i in range(2)]
    b.w1r = [nc.alloc_sbuf_tensor(f"w1r{i}", [128, ND_ * 128], BF16) for i in range(b.NWR)]
    b.wur = [nc.alloc_sbuf_tensor(f"wur{i}", [128, ND_ * 128], BF16) for i in range(b.NWR)]
    b.w2r = [nc.alloc_sbuf_tensor(f"w2r{i}", [128, D_], BF16) for i in range(b.NWR)]
    b.b2r = [nc.alloc_sbuf_tensor("b2r0", [1, D_], BF16)]
    b.ch_xnfm = [ctx.chan(f"c_xnf{i}") for i in range(2)]
    b.ch_w1 = [ctx.chan(f"c_w1_{i}") for i in range(b.NWR)]
    b.ch_wu = [ctx.chan(f"c_wu_{i}") for i in range(b.NWR)]
    b.ch_w2 = [ctx.chan(f"c_w2_{i}") for i in range(b.NWR)]
    b.ch_b2 = [ctx.chan("c_b2_0")]
    gw32 = nc.alloc_sbuf_tensor("gw32", [128, ND_ * 8], F32)
    b1sb = nc.alloc_sbuf_tensor("b1sb", [128, 8 * 32], F32)

    emit_consts(b, id_d)
    alloc_attn(b, id_d)
    ctx.emit("sync", lambda s: s.dma_start(gw32[:, :], gwr_d), b.ch_cst[0], 16)
    ctx.emit("gpsimd", lambda g: g.dma_start(
        b1sb[:, :].rearrange("p (e f) -> p e f", f=32),
        b1_dd.rearrange("e p f -> p e f")), b.ch_cstg[0], 16)
    prologue(b, x_d, xnfm_d, gw32=gw32, gw_E=8, ln=True)
    softmax_top2(b, 8, renorm=True)
    acfg = {"mode": "alibi", "qk_d": qk_d, "v_d": v_d, "wo_d": wo_d,
            "bqk_d": bqk_d, "bv_d": bv_d, "bo_d": bo_d, "masks_d": mk_d,
            "xnfm_d": xnfm_d, "n_seq": NSEQ, "seq_tb": SEQ_TB}
    emit_attn(b, acfg)
    mcfg = {"E": 8, "H": 4096, "xnfm_d": xnfm_d, "first_acc": False,
            "act": "relu_bias", "w1_d": w1_d, "w2_d": w2_d, "b2_d": b2_dd,
            "b1sb": b1sb, "ch_b1": b.ch_cstg[0]}
    moe_loop(b, mcfg)
    ctx.wait("gpsimd", b.ve)
    for t in range(0, NT_, 8):
        ctx.emit("gpsimd", lambda g, t=t: g.dma_start(
            out_d[t:t + 8].rearrange("a p d -> p a d"),
            b.xacc[:, t * D_:(t + 8) * D_].rearrange("p (a d) -> p a d", d=D_)),
            b.gst, 16)
    ctx.wait("gpsimd", b.gst)
    ctx.build()
    return nc


def build_l2a():
    nc = bass.Bass("TRN2", target_bir_lowering=False)
    ctx = Ctx(nc)
    x_d = nc.dram_tensor("x", [NT_, 128, D_], F32, kind="ExternalInput").ap()
    id_d = nc.dram_tensor("ident", [128, 128], F32, kind="ExternalInput").ap()
    qk_d = nc.dram_tensor("qk_p", [16, 128, 1024], F32, kind="ExternalInput").ap()
    v_d = nc.dram_tensor("v_p", [16, 128, 512], F32, kind="ExternalInput").ap()
    wof_d = nc.dram_tensor("wof_p", [8, 2, 64, 1024], F32, kind="ExternalInput").ap()
    wob_d = nc.dram_tensor("wob_p", [8, 2, 64, 1024], F32, kind="ExternalInput").ap()
    wf_d = nc.dram_tensor("wf_p", [128, 128], F32, kind="ExternalInput").ap()
    bf_d = nc.dram_tensor("bf_p", [16, 1], F32, kind="ExternalInput").ap()
    mk_d = nc.dram_tensor("masks_p", [128, 256], BF16, kind="ExternalInput").ap()
    out_d = nc.dram_tensor("out", [NT_, 128, D_], F32, kind="ExternalOutput").ap()
    xnfm_d = nc.dram_tensor("xnfm", [T_ // TBK, ND_, 128, TBK], BF16,
                            kind="ExternalOutput").ap()
    gc_d = nc.dram_tensor("gcd", [NSEQ, 16, 2048], F32, kind="Internal").ap()
    u_d = nc.dram_tensor("ud", [NSEQ, 16, 2048], F32, kind="Internal").ap()

    b = alloc_common(nc, ctx, NT_, D_, 8, 8, TBK)
    b.xnfm = [nc.alloc_sbuf_tensor(f"xnfm{i}", [128, ND_ * TBK], BF16) for i in range(2)]
    b.w1r = [nc.alloc_sbuf_tensor(f"w1r{i}", [128, ND_ * 128], BF16) for i in range(b.NWR)]
    b.wur = [nc.alloc_sbuf_tensor(f"wur{i}", [128, ND_ * 128], BF16) for i in range(b.NWR)]
    b.w2r = [nc.alloc_sbuf_tensor(f"w2r{i}", [128, D_], BF16) for i in range(b.NWR)]
    b.b2r = [nc.alloc_sbuf_tensor("b2r0", [1, D_], BF16)]
    b.ch_xnfm = [ctx.chan(f"c_xnf{i}") for i in range(2)]
    b.ch_w1 = [ctx.chan(f"c_w1_{i}") for i in range(b.NWR)]
    b.ch_wu = [ctx.chan(f"c_wu_{i}") for i in range(b.NWR)]
    b.ch_w2 = [ctx.chan(f"c_w2_{i}") for i in range(b.NWR)]
    b.ch_b2 = [ctx.chan("c_b2_0")]

    emit_consts(b, id_d)
    alloc_attn(b, id_d)
    prologue(b, x_d, xnfm_d, gw32=None, ln=True)
    cfg = {"mode": "fgate", "qk_d": qk_d, "v_d": v_d,
           "wo_f_d": wof_d, "wo_b_d": wob_d, "wf_d": wf_d, "bf_d": bf_d,
           "masks_d": mk_d, "gc_dram": gc_d, "u_dram": u_d,
           "xnfm_d": xnfm_d, "n_seq": NSEQ, "seq_tb": SEQ_TB}
    emit_attn(b, cfg)
    ctx.wait("gpsimd", b.ve)
    for t in range(0, NT_, 8):
        ctx.emit("gpsimd", lambda g, t=t: g.dma_start(
            out_d[t:t + 8].rearrange("a p d -> p a d"),
            b.xacc[:, t * D_:(t + 8) * D_].rearrange("p (a d) -> p a d", d=D_)),
            b.gst, 16)
    ctx.wait("gpsimd", b.gst)
    ctx.build()
    return nc


def build_l3():
    nc = bass.Bass("TRN2", target_bir_lowering=False)
    ctx = Ctx(nc)
    x_d = nc.dram_tensor("x", [NT_, 128, D_], F32, kind="ExternalInput").ap()
    id_d = nc.dram_tensor("ident", [128, 128], F32, kind="ExternalInput").ap()
    qk_d = nc.dram_tensor("qk_p", [16, 128, 1024], F32, kind="ExternalInput").ap()
    v_d = nc.dram_tensor("v_p", [16, 128, 512], F32, kind="ExternalInput").ap()
    wo_d = nc.dram_tensor("wo_p", [8, 2, 64, 1024], F32, kind="ExternalInput").ap()
    bqkv_d = nc.dram_tensor("bqkv_p", [64, 48], F32, kind="ExternalInput").ap()
    bo_d = nc.dram_tensor("bo_p", [128, 8], F32, kind="ExternalInput").ap()
    gwr_d = nc.dram_tensor("gwr", [128, ND_ * 8], F32, kind="ExternalInput").ap()
    g_dd = nc.dram_tensor("gL", [8, 32, 128, ND_ * 128], F32, kind="ExternalInput").ap()
    u_dd = nc.dram_tensor("uL", [8, 32, 128, ND_ * 128], F32, kind="ExternalInput").ap()
    d_dd = nc.dram_tensor("dR", [8, 32, 128, D_], F32, kind="ExternalInput").ap()
    out_d = nc.dram_tensor("out", [NT_, 128, D_], F32, kind="ExternalOutput").ap()
    xnfm_d = nc.dram_tensor("xnfm", [T_ // TBK, ND_, 128, TBK], BF16, kind="Internal").ap()

    b = alloc_common(nc, ctx, NT_, D_, 8, 32, TBK)
    b.NWR = 2
    b.xnfm = [nc.alloc_sbuf_tensor(f"xnfm{i}", [128, ND_ * TBK], BF16) for i in range(2)]
    b.w1r = [nc.alloc_sbuf_tensor(f"w1r{i}", [128, ND_ * 128], BF16) for i in range(b.NWR)]
    b.wur = [nc.alloc_sbuf_tensor(f"wur{i}", [128, ND_ * 128], BF16) for i in range(b.NWR)]
    b.w2r = [nc.alloc_sbuf_tensor(f"w2r{i}", [128, D_], BF16) for i in range(b.NWR)]
    b.ch_xnfm = [ctx.chan(f"c_xnf{i}") for i in range(2)]
    b.ch_w1 = [ctx.chan(f"c_w1_{i}") for i in range(b.NWR)]
    b.ch_wu = [ctx.chan(f"c_wu_{i}") for i in range(b.NWR)]
    b.ch_w2 = [ctx.chan(f"c_w2_{i}") for i in range(b.NWR)]
    gw32 = nc.alloc_sbuf_tensor("gw32", [128, ND_ * 8], F32)

    emit_consts(b, id_d)
    ctx.emit("sync", lambda s: s.dma_start(gw32[:, :], gwr_d), b.ch_cst[0], 16)
    prologue(b, x_d, xnfm_d, gw32=gw32, gw_E=8, ln=True)
    softmax_top2(b, 8, renorm=True)
    acfg = {"qk_d": qk_d, "v_d": v_d, "wo_d": wo_d, "bqkv_d": bqkv_d, "bo_d": bo_d,
            "xnfm_d": xnfm_d, "n_seq": NSEQ, "seq_tb": SEQ_TB}
    emit_l3attn(b, acfg)
    mcfg = {"E": 8, "H": 4096, "xnfm_d": xnfm_d, "first_acc": False,
            "act": "swiglu", "g_d": g_dd, "u_d": u_dd, "w2_d": d_dd}
    moe_loop(b, mcfg)
    ctx.wait("gpsimd", b.ve)
    for t in range(0, NT_, 8):
        ctx.emit("gpsimd", lambda g, t=t: g.dma_start(
            out_d[t:t + 8].rearrange("a p d -> p a d"),
            b.xacc[:, t * D_:(t + 8) * D_].rearrange("p (a d) -> p a d", d=D_)),
            b.gst, 16)
    ctx.wait("gpsimd", b.gst)
    ctx.build()
    return nc


_PROGS = {}


def _get_prog(name):
    if name not in _PROGS:
        _PROGS[name] = {"A": lambda: build_l0attn(S_len=2048, B=16, D=1024),
                        "B": build_l0rest, "C": build_l1, "D1": build_l2a,
                        "D2": lambda: build_l2ffn(T=T_, SL=2048),
                        "E": build_l3}[name]()
    return _PROGS[name]


def kernel(**inputs):
    gi = lambda k: np.asarray(inputs[k])
    x = gi("L0_attn_x")
    ident = np.eye(128, dtype=np.float32)
    cores = list(range(8))
    bf16 = ml_dtypes.bfloat16

    W0 = gi("L0_attn_W0"); Wkv = gi("L0_attn_Wkv"); bkv = gi("L0_attn_bkv")
    eta = np.asarray(inputs["L0_attn_eta"], dtype=np.float32).reshape(1, 1)
    xs_sbd = np.ascontiguousarray(x.transpose(1, 0, 2))
    m64 = host_masks128()
    imA = []
    for i in cores:
        e0 = i * 128
        im = {"xs": xs_sbd, "eta": eta, "m64": m64, "ident": ident}
        im.update(host_pack_l0attn(
            W0[e0:e0 + 128, :],
            np.ascontiguousarray(Wkv[1024 + e0:1024 + e0 + 128, :].T),
            bkv[1024 + e0:1024 + e0 + 128]))
        imA.append(im)
    rA = run_bass_kernel_spmd(_get_prog("A"), imA, core_ids=cores)
    y_full = np.concatenate([np.asarray(rA.results[i]["y"]) for i in cores], axis=2)

    gwp = np.ascontiguousarray(
        gi("L0_ffn_gate_w").T.reshape(ND_, 128, 8).transpose(1, 0, 2)).reshape(128, ND_ * 8)
    l0f = host_pack_l0ffn(gi("L0_ffn_W_lin"), gi("L0_ffn_b_lin"),
                          gi("L0_ffn_left"), gi("L0_ffn_right"))
    imB = []
    for i in cores:
        xc = x[2 * i:2 * i + 2].reshape(T_, D_)
        yc = np.ascontiguousarray(
            y_full[:, 2 * i:2 * i + 2, :].transpose(1, 0, 2)).reshape(T_, D_)
        im = {"x": tm_view(xc), "y": tm_view(yc), "ident": ident, "gwp": gwp}
        im.update(l0f)
        imB.append(im)
    rB = run_bass_kernel_spmd(_get_prog("B"), imB, core_ids=cores)

    w_in = gi("L1_attn_w_in"); b_in = gi("L1_attn_b_in")
    w_out = gi("L1_attn_w_out"); b_out = gi("L1_attn_b_out")
    wq = np.ascontiguousarray(w_in[:, 0:1024]).reshape(D_, 16, 64)
    wk = np.ascontiguousarray(w_in[:, 1024:2048]).reshape(D_, 16, 64)
    wv = np.ascontiguousarray(w_in[:, 2048:3072]).reshape(D_, 16, 64)
    bqk_p = np.zeros((128, 32), np.float32)
    bqk_p[64:128, 0:16] = b_in[0:1024].reshape(16, 64).T
    bqk_p[64:128, 16:32] = b_in[1024:2048].reshape(16, 64).T
    gwr1 = np.ascontiguousarray(
        gi("L1_ffn_gate_w").T.reshape(ND_, 128, 8).transpose(1, 0, 2)).reshape(128, ND_ * 8)
    w1 = gi("L1_ffn_w1"); b1 = gi("L1_ffn_b1"); w2 = gi("L1_ffn_w2"); b2 = gi("L1_ffn_b2")
    shared_C = {
        "ident": ident,
        "qk_p": host_pack_headproj(wq, wk), "v_p": host_pack_vslab(wv),
        "wo_p": host_pack_wo(w_out), "bqk_p": bqk_p,
        "bv_p": np.ascontiguousarray(b_in[2048:3072].reshape(1, 1024)),
        "bo_p": np.ascontiguousarray(b_out.reshape(8, 128).T),
        "masks_p": host_masks().astype(bf16),
        "gwr": gwr1,
        "w1L": np.stack([pack_lhsT(w1[e].T) for e in range(8)]),
        "w2R": np.stack([pack_rhs(w2[e].T) for e in range(8)]),
        "b1p": np.ascontiguousarray(b1.reshape(8, 32, 128).transpose(0, 2, 1)),
        "b2p": np.ascontiguousarray(b2.reshape(1, 8 * D_)),
    }
    imC = [dict(shared_C, x=np.asarray(rB.results[i]["out"])) for i in cores]
    rC = run_bass_kernel_spmd(_get_prog("C"), imC, core_ids=cores)

    WoT = np.ascontiguousarray(gi("L2_attn_Wo").T)
    shared_D1 = {
        "ident": ident,
        "qk_p": host_pack_headproj(
            np.ascontiguousarray(gi("L2_attn_Wq").T).reshape(D_, 16, 64),
            np.ascontiguousarray(gi("L2_attn_Wk").T).reshape(D_, 16, 64)),
        "v_p": host_pack_vslab(
            np.ascontiguousarray(gi("L2_attn_Wv").T).reshape(D_, 16, 64)),
        "wof_p": host_pack_wo(WoT[0:1024]),
        "wob_p": host_pack_wo(WoT[1024:2048]),
        "wf_p": np.ascontiguousarray(
            gi("L2_attn_Wf").T.reshape(8, 128, 16).transpose(1, 0, 2).reshape(128, 128)),
        "bf_p": np.ascontiguousarray(gi("L2_attn_bf").reshape(16, 1)),
        "masks_p": host_masks().astype(bf16),
    }
    imD1 = [dict(shared_D1, x=np.asarray(rC.results[i]["out"])) for i in cores]
    rD1 = run_bass_kernel_spmd(_get_prog("D1"), imD1, core_ids=cores)

    wqkv2 = gi("L2_ffn_wqkv"); qs2 = gi("L2_ffn_q_scale"); ks2 = gi("L2_ffn_k_scale")
    wo2 = gi("L2_ffn_wo"); lg2 = gi("L2_ffn_ln_g"); lb2 = gi("L2_ffn_ln_b")
    w12 = gi("L2_ffn_w1"); b12 = gi("L2_ffn_b1")
    w22 = gi("L2_ffn_w2"); b22 = gi("L2_ffn_b2")
    shared_D2 = {"rot": host_rotary(2048), "ident": ident,
                 "masks_p": host_masks().astype(bf16)}
    for l in range(2):
        pk = host_pack_l2ffn(wqkv2[l], qs2[l], ks2[l], wo2[l], lg2[l], lb2[l],
                             w12[l], b12[l], w22[l], b22[l])
        for k2, v2 in pk.items():
            shared_D2[f"{k2}_{l}"] = v2
    imD2 = [dict(shared_D2, xnfm=np.asarray(rD1.results[i]["xnfm"]),
                 xaccp=np.asarray(rD1.results[i]["out"])) for i in cores]
    rD2 = run_bass_kernel_spmd(_get_prog("D2"), imD2, core_ids=cores)

    gwr3 = np.ascontiguousarray(
        gi("L3_ffn_router_w").T.reshape(ND_, 128, 8).transpose(1, 0, 2)).reshape(128, ND_ * 8)
    gw3 = gi("L3_ffn_gate_w"); uw3 = gi("L3_ffn_up_w"); dw3 = gi("L3_ffn_down_w")
    shared_E = {"ident": ident, "gwr": gwr3}
    shared_E.update(host_pack_l3attn(
        gi("L3_attn_wq"), gi("L3_attn_bq"), gi("L3_attn_wk"), gi("L3_attn_bk"),
        gi("L3_attn_wv"), gi("L3_attn_bv"), gi("L3_attn_wo"), gi("L3_attn_bo")))
    shared_E["gL"] = np.stack([pack_lhsT(gw3[e].T) for e in range(8)])
    shared_E["uL"] = np.stack([pack_lhsT(uw3[e].T) for e in range(8)])
    shared_E["dR"] = np.stack([pack_rhs(dw3[e].T) for e in range(8)])
    imE = [dict(shared_E, x=np.asarray(rD2.results[i]["out"])) for i in cores]
    rE = run_bass_kernel_spmd(_get_prog("E"), imE, core_ids=cores)

    out = np.empty((16, 2048, 1024), np.float32)
    for i in cores:
        out[2 * i:2 * i + 2] = np.asarray(
            rE.results[i]["out"], dtype=np.float32).reshape(2, 2048, 1024)
    return out


# version 02
# speedup vs baseline: 1.0320x; 1.0320x over previous
import sys
sys.path.insert(0, "/opt/trn_rl_repo")
import numpy as np
import concourse.bass as bass
import concourse.mybir as mybir

F32 = mybir.dt.float32
BF16 = mybir.dt.bfloat16
AF = mybir.ActivationFunctionType
ALU = mybir.AluOpType
AX = mybir.AxisListType

ENGINES = ("tensor", "vector", "scalar", "gpsimd", "sync")


class Chan:
    def __init__(self, nc, name):
        self.sem = nc.alloc_semaphore(name)
        self.name = name
        self.n = 0

    def tick(self, k=1):
        self.n += k
        return self.n


class Ctx:
    def __init__(self, nc):
        self.nc = nc
        self.q = {e: [] for e in ENGINES}
        self.chans = {}

    def chan(self, name):
        if name not in self.chans:
            self.chans[name] = Chan(self.nc, name)
        return self.chans[name]

    def on(self, eng, fn):
        self.q[eng].append(fn)

    def emit(self, eng, maker, ch=None, k=1):
        if ch is None:
            self.on(eng, maker)
            return None
        tgt = ch.tick(k)
        def fn(e, maker=maker, ch=ch, k=k):
            maker(e).then_inc(ch.sem, k)
        self.on(eng, fn)
        return tgt

    def wait(self, eng, ch, val=None):
        v = ch.n if val is None else val
        if v < 0:
            return
        self.on(eng, lambda e, s=ch.sem, v=v: e.wait_ge(s, v))

    def build(self):
        nc = self.nc
        with nc.Block() as blk:
            for name in ENGINES:
                fns = self.q[name]
                def body(e, fns=fns):
                    for f in fns:
                        f(e)
                getattr(blk, name)(body)
        self.q = {e: [] for e in ENGINES}



def pack_lhsT(w, d_in=None):
    d_in, f = w.shape
    assert d_in % 128 == 0
    nd = d_in // 128
    nf = (f + 127) // 128
    if f % 128:
        w = np.concatenate([w, np.zeros((d_in, nf * 128 - f), w.dtype)], axis=1)
    a = w.reshape(nd, 128, nf, 128).transpose(2, 1, 0, 3)
    return np.ascontiguousarray(a.reshape(nf, 128, nd * 128))


def pack_rhs(w):
    f, n = w.shape
    nf = (f + 127) // 128
    if f % 128:
        w = np.concatenate([w, np.zeros((nf * 128 - f, n), w.dtype)], axis=0)
    return np.ascontiguousarray(w.reshape(nf, 128, n))


def tm_view(x):
    T, d = x.shape
    assert T % 128 == 0
    return np.ascontiguousarray(x.reshape(T // 128, 128, d))



"""Reusable emitters: LN prologue (stage->LN->xacc/xnfm_d/fp32-routing),
top-2 softmax, and the dense-MoE expert loop (relu_bias | swiglu).

Buffer container B holds all SBUF/PSUM handles + channels so emitters can be
chained into one program (and later fused across layers).
"""


class B:
    pass


def alloc_common(nc, ctx, NT, D=1024, n_e_max=8, nf_max=32, TB=1024):
    nd = D // 128
    b = B()
    b.nc, b.ctx = nc, ctx
    b.NT, b.D, b.nd, b.TB = NT, D, nd, TB
    b.NWR = 3
    b.xacc = nc.alloc_sbuf_tensor("xacc", [128, NT * D], BF16)
    b.hid = nc.alloc_sbuf_tensor("hid", [128, 32 * 1024], BF16)
    b.sqscr = b.hid
    b.stage = [nc.alloc_sbuf_tensor(f"stage{i}", [128, D], F32) for i in range(2)]
    b.xtr = [nc.alloc_sbuf_tensor(f"xtr{i}", [128, nd * 128], BF16) for i in range(2)]
    b.xtr32 = nc.alloc_sbuf_tensor("xtr32", [128, nd * 128], F32)
    b.identsb = nc.alloc_sbuf_tensor("identsb", [128, 128], F32)
    b.ones1 = nc.alloc_sbuf_tensor("ones1", [1, 128], BF16)
    b.epsc = nc.alloc_sbuf_tensor("epsc", [128, 1], F32)
    
    b.stats = nc.alloc_sbuf_tensor("stats", [128, max(6 * NT, 192)], F32)
    b.lg = nc.alloc_sbuf_tensor("lg", [128, NT * n_e_max], F32)
    b.combine = nc.alloc_sbuf_tensor("combine", [128, NT * n_e_max], F32)
    b.sm1 = nc.alloc_sbuf_tensor("sm1", [128, NT], F32)
    b.sm2 = nc.alloc_sbuf_tensor("sm2", [128, NT], F32)
    b.sden = nc.alloc_sbuf_tensor("sden", [128, NT], F32)
    b.scr8 = nc.alloc_sbuf_tensor("scr8", [128, NT * n_e_max], F32)
    b.pool = [nc.alloc_psum_tensor(f"pool{i}", [128, 512], F32) for i in range(7)]
    b.poolB = nc.alloc_psum_tensor("poolB", [128, 1024], BF16)
    b.pe = ctx.chan("pe"); b.ve = ctx.chan("ve"); b.se = ctx.chan("se")
    b.gst = ctx.chan("gst")
    b.ch_stage = [ctx.chan(f"c_stg{i}") for i in range(2)]
    b.ch_xst = ctx.chan("c_xst")
    b.ch_ident = ctx.chan("c_ident")
    b.ch_cst = [ctx.chan(f"c_cst{i}") for i in range(4)]
    b.ch_cstg = [ctx.chan(f"c_cstg{i}") for i in range(2)]
    return b


def emit_consts(b, id_d):
    ctx = b.ctx
    b.ones_ve = ctx.emit("vector", lambda v: v.memset(b.ones1[:, :], 1.0), b.ve)
    ctx.emit("vector", lambda v: v.memset(b.epsc[:, :], 1e-5), b.ve)
    ctx.emit("sync", lambda s: s.dma_start(b.identsb[:, :], id_d), b.ch_ident, 16)


def prologue(b, x_d, xnfm_d, gw32=None, gw_E=0, ln=True):
    ctx = b.ctx
    NT, nd, D, TPB = b.NT, b.nd, b.D, b.TB // 128
    stage, pool, xtr, xtr32 = b.stage, b.pool, b.xtr, b.xtr32
    stage_free_pe = {}
    xst_tgt = {}
    tr_pool_free = {0: b.ve.n, 1: b.ve.n}
    route_pe = b.pe.n
    for t in range(NT):
        if t >= 2:
            ctx.wait("sync", b.pe, stage_free_pe[t - 2])
        st_tgt = ctx.emit("sync", lambda s, t=t: s.dma_start(
            stage[t % 2][:, :], x_d[t]), b.ch_stage[t % 2], 16)
        ctx.wait("vector", b.ch_stage[t % 2], st_tgt)
        def mkstat(t):
            def fn(v):
                v.tensor_copy(b.xacc[:, t * D:(t + 1) * D], stage[t % 2][:, :])
                if ln:
                    inst = v.tensor_reduce(b.stats[:, t:t + 1], stage[t % 2][:, :],
                                           AX.X, ALU.add, negate=True)
                else:
                    inst = v.engine_nop()
                return inst
            return fn
        ctx.emit("vector", mkstat(t), b.ve)
        if ln:
            ctx.wait("scalar", b.ch_stage[t % 2], st_tgt)
            ctx.wait("scalar", b.ve)
            def mksq(t):
                def fn(sc):
                    sc.activation(b.sqscr[:, 0:D], stage[t % 2][:, :], AF.Square,
                                  accum_out=b.stats[:, NT + t:NT + t + 1])
                    sc.activation(b.stats[:, 2 * NT + t:2 * NT + t + 1],
                                  b.stats[:, t:t + 1], AF.Copy, scale=1.0 / D)
                    return sc.activation(b.stats[:, 3 * NT + t:3 * NT + t + 1],
                                         b.stats[:, t:t + 1], AF.Square,
                                         scale=1.0 / D)
                return fn
            ctx.emit("scalar", mksq(t), b.se)
            ctx.wait("vector", b.se)
            ctx.emit("vector", lambda v, t=t: v.scalar_tensor_tensor(
                b.stats[:, 4 * NT + t:4 * NT + t + 1],
                b.stats[:, NT + t:NT + t + 1], 1.0 / D,
                b.stats[:, 3 * NT + t:3 * NT + t + 1],
                ALU.mult, ALU.subtract), b.ve)
            ctx.wait("scalar", b.ve)
            def mkrstd(t):
                def fn(sc):
                    sc.activation(b.stats[:, 5 * NT + t:5 * NT + t + 1],
                                  b.stats[:, 4 * NT + t:4 * NT + t + 1],
                                  AF.Ln, bias=b.epsc[:, 0:1])
                    sc.activation(b.sqscr[:, 0:512], stage[t % 2][:, 0:512], AF.Copy)
                    return sc.activation(b.stats[:, 5 * NT + t:5 * NT + t + 1],
                                         b.stats[:, 5 * NT + t:5 * NT + t + 1],
                                         AF.Exp, scale=-0.5)
                return fn
            ctx.emit("scalar", mkrstd(t), b.se)
            ctx.wait("vector", b.se)
            ctx.emit("vector", lambda v, t=t: v.tensor_scalar(
                stage[t % 2][:, :], stage[t % 2][:, :],
                b.stats[:, 2 * NT + t:2 * NT + t + 1],
                b.stats[:, 5 * NT + t:5 * NT + t + 1],
                ALU.add, ALU.mult), b.ve)
        apply_ve = b.ve.n
        for grp in range(2):
            ctx.wait("tensor", b.ve, apply_ve)
            if t == 0 and grp == 0:
                ctx.wait("tensor", b.ch_ident)
            ctx.wait("tensor", b.ve, tr_pool_free[grp])
            def mk(tt, gg):
                def f(te):
                    for c in range(4):
                        inst = te.transpose(
                            pool[gg][:, c * 128:(c + 1) * 128],
                            stage[tt % 2][:, (gg * 4 + c) * 128:(gg * 4 + c + 1) * 128],
                            b.identsb[:, :])
                    return inst
                return f
            ctx.emit("tensor", mk(t, grp), b.pe)
            if grp == 1:
                stage_free_pe[t] = b.pe.n
            ctx.wait("vector", b.pe)
            if grp == 0:
                if t >= 2:
                    ctx.wait("vector", b.ch_xst, xst_tgt[t - 2])
                if t >= 1:
                    ctx.wait("vector", b.pe, route_pe)
            def mkevtr(t, grp):
                def fn(v):
                    v.tensor_copy(xtr32[:, grp * 512:(grp + 1) * 512], pool[grp][:, :])
                    return v.tensor_copy(
                        xtr[t % 2][:, grp * 512:(grp + 1) * 512], pool[grp][:, :])
                return fn
            tr_pool_free[grp] = ctx.emit("vector", mkevtr(t, grp), b.ve)
        xtr_ve = b.ve.n
        if gw32 is not None:
            E = gw_E
            ctx.wait("tensor", b.ve, xtr_ve)
            if t == 0:
                ctx.wait("tensor", b.ch_cst[0])
            def mkroute(t):
                def fn(te):
                    for c in range(nd):
                        inst = te.matmul(pool[2][:, t * E:(t + 1) * E],
                                         xtr32[:, c * 128:(c + 1) * 128],
                                         gw32[:, c * E:(c + 1) * E],
                                         start=(c == 0), stop=(c == nd - 1))
                    return inst
                return fn
            route_pe = ctx.emit("tensor", mkroute(t), b.pe)
        tb, toff = t // TPB, t % TPB
        ctx.wait("sync", b.ve, xtr_ve)
        if t >= 1:
            ctx.wait("sync", b.ch_xst, xst_tgt[t - 1])
        xst_tgt[t] = ctx.emit("sync", lambda s, tb=tb, toff=toff, t=t: s.dma_start(
            xnfm_d[tb][:, :, toff * 128:(toff + 1) * 128].rearrange("c p j -> p c j"),
            xtr[t % 2][:, :].rearrange("p (c j) -> p c j", j=128)), b.ch_xst, 16)
    b.xst_total = b.ch_xst.n


def softmax_top2(b, E, renorm=True):
    ctx, NT = b.ctx, b.NT
    lg, combine, scr8 = b.lg, b.combine, b.scr8
    ctx.wait("vector", b.pe)
    ctx.emit("vector", lambda v: v.tensor_copy(lg[:, 0:NT * E], b.pool[2][:, 0:NT * E]), b.ve)

    def sm_a(v):
        v.tensor_reduce(b.sm1[:, 0:NT], lg[:, 0:NT * E].rearrange("p (t e) -> p t e", e=E),
                        AX.X, ALU.max, negate=True)
        for t in range(NT):
            sl = slice(t * E, (t + 1) * E)
            v.tensor_scalar(lg[:, sl], lg[:, sl], b.sm1[:, t:t + 1], None, ALU.add)
        for _ in range(4):
            v.engine_nop()
        v.tensor_single_scalar(scr8[:, 0:NT * E], lg[:, 0:NT * E], 0.0, ALU.is_ge)
        for _ in range(4):
            v.engine_nop()
        v.scalar_tensor_tensor(scr8[:, 0:NT * E], scr8[:, 0:NT * E], -1e30,
                               lg[:, 0:NT * E], ALU.mult, ALU.add)
        v.tensor_reduce(b.sm2[:, 0:NT],
                        scr8[:, 0:NT * E].rearrange("p (t e) -> p t e", e=E),
                        AX.X, ALU.max)
        for t in range(NT):
            sl = slice(t * E, (t + 1) * E)
            inst = v.tensor_scalar(scr8[:, sl], lg[:, sl], b.sm2[:, t:t + 1], None, ALU.is_ge)
        return inst
    ctx.emit("vector", sm_a, b.ve)
    ctx.wait("scalar", b.ve)
    ctx.emit("scalar", lambda sc: sc.activation(
        combine[:, 0:NT * E], lg[:, 0:NT * E], AF.Exp), b.se)
    ctx.wait("vector", b.se)
    def sm_c1(v):
        if renorm:
            v.tensor_mul(combine[:, 0:NT * E], combine[:, 0:NT * E], scr8[:, 0:NT * E])
        return v.tensor_reduce(b.sden[:, 0:NT],
                               combine[:, 0:NT * E].rearrange("p (t e) -> p t e", e=E),
                               AX.X, ALU.add)
    ctx.emit("vector", sm_c1, b.ve)
    ctx.wait("scalar", b.ve)
    def sm_recip(sc):
        sc.activation(b.sm2[:, 0:NT], b.sden[:, 0:NT], AF.Ln)
        return sc.activation(b.sden[:, 0:NT], b.sm2[:, 0:NT], AF.Exp, scale=-1.0)
    ctx.emit("scalar", sm_recip, b.se)
    ctx.wait("vector", b.se)
    def sm_c2(v):
        if not renorm:
            v.tensor_mul(combine[:, 0:NT * E], combine[:, 0:NT * E], scr8[:, 0:NT * E])
        for t in range(NT):
            sl = slice(t * E, (t + 1) * E)
            inst = v.tensor_scalar(combine[:, sl], combine[:, sl],
                                   b.sden[:, t:t + 1], None, ALU.mult)
        return inst
    ctx.emit("vector", sm_c2, b.ve)
    b.combine_ve = b.ve.n


def moe_loop(b, cfg):
    ctx, nc = b.ctx, b.nc
    E, H, TB, D, nd = cfg["E"], cfg["H"], b.TB, b.D, b.nd
    NT, TPB = b.NT, b.TB // 128
    NTB = NT // TPB
    nf = H // 128
    P1W = min(512, TB)
    nhalves = TB // P1W
    T4G = 2
    nhalfg = (TPB + T4G - 1) // T4G
    act = cfg["act"]
    pool = b.pool
    if not hasattr(b, "xnfm"):
        b.xnfm = [nc.alloc_sbuf_tensor(f"xnfm{i}", [128, nd * TB], BF16) for i in range(2)]
        b.w1r = [nc.alloc_sbuf_tensor(f"w1r{i}", [128, nd * 128], BF16) for i in range(b.NWR)]
        b.wur = [nc.alloc_sbuf_tensor(f"wur{i}", [128, nd * 128], BF16) for i in range(b.NWR)]
        b.w2r = [nc.alloc_sbuf_tensor(f"w2r{i}", [128, D], BF16) for i in range(b.NWR)]
        b.b2r = [nc.alloc_sbuf_tensor("b2r0", [1, D], BF16)]
        b.ch_xnfm = [ctx.chan(f"c_xnf{i}") for i in range(2)]
        b.ch_w1 = [ctx.chan(f"c_w1_{i}") for i in range(b.NWR)]
        b.ch_wu = [ctx.chan(f"c_wu_{i}") for i in range(b.NWR)]
        b.ch_w2 = [ctx.chan(f"c_w2_{i}") for i in range(b.NWR)]
        b.ch_b2 = [ctx.chan("c_b2_0")]
    assert nf * TB <= 32 * 1024
    hid, xnfm = b.hid, b.xnfm

    NWR = b.NWR
    w1_last = [b.pe.n] * NWR
    wu_last = [b.pe.n] * NWR
    w2_last = [b.pe.n] * NWR
    b2_last = [b.pe.n]
    xnfm_last = [b.pe.n] * 2
    pool_free = [b.ve.n] * 8
    pool01_free_se = b.se.n
    ph2_done = b.pe.n
    first_xnfm = True

    for e in range(E):
        if act == "relu_bias":
            ctx.wait("gpsimd", b.pe, b2_last[0])
            b2_tgt = ctx.emit("gpsimd", lambda g, e=e: g.dma_start(
                b.b2r[0][:, :], cfg["b2_d"][0:1, e * D:(e + 1) * D]), b.ch_b2[0], 16)
        for tb in range(NTB):
            buf = (e * NTB + tb) % 2
            ctx.wait("sync", b.pe, xnfm_last[buf])
            if first_xnfm:
                ctx.wait("sync", b.ch_xst, b.xst_total)
                first_xnfm = False
            xnfm_tgt = ctx.emit("sync", lambda s, buf=buf, tb=tb: s.dma_start(
                xnfm[buf][:, :].rearrange("p (c t) -> p c t", c=nd),
                cfg["xnfm_d"][tb].rearrange("c p t -> p c t")), b.ch_xnfm[buf], 16)
            first_evac = True
            for f in range(nf):
                slot = f % NWR
                if act == "relu_bias":
                    ctx.wait("gpsimd", b.pe, w1_last[slot])
                    w1_tgt = ctx.emit("gpsimd", lambda g, e=e, f=f, slot=slot: g.dma_start(
                        b.w1r[slot][:, :], cfg["w1_d"][e, f]), b.ch_w1[slot], 16)
                    for half in range(nhalves):
                        gi = f * nhalves + half
                        pi = 5 + gi % 2
                        ctx.wait("tensor", b.ch_w1[slot], w1_tgt)
                        ctx.wait("tensor", b.ch_xnfm[buf], xnfm_tgt)
                        ctx.wait("tensor", b.ve, pool_free[pi])
                        def mk1(f, half, pi, buf, slot):
                            def fn(te):
                                for c in range(nd):
                                    inst = te.matmul(
                                        pool[pi][:, 0:P1W], b.w1r[slot][:, c * 128:(c + 1) * 128],
                                        xnfm[buf][:, c * TB + half * P1W:c * TB + (half + 1) * P1W],
                                        start=(c == 0), stop=(c == nd - 1))
                                return inst
                            return fn
                        ctx.emit("tensor", mk1(f, half, pi, buf, slot), b.pe)
                        w1_last[slot] = b.pe.n
                        xnfm_last[buf] = b.pe.n
                        ctx.wait("vector", b.pe)
                        if first_evac:
                            ctx.wait("vector", b.pe, ph2_done)
                            ctx.wait("vector", cfg["ch_b1"])
                            first_evac = False
                        pool_free[pi] = ctx.emit(
                            "vector", lambda v, f=f, half=half, pi=pi, e=e: v.tensor_scalar(
                                hid[:, f * TB + half * P1W:f * TB + (half + 1) * P1W],
                                pool[pi][:, 0:P1W],
                                cfg["b1sb"][:, e * nf + f:e * nf + f + 1], 0.0,
                                ALU.add, ALU.max), b.ve)
                else:
                    ctx.wait("gpsimd", b.pe, w1_last[slot])
                    wg_tgt = ctx.emit("gpsimd", lambda g, e=e, f=f, slot=slot: g.dma_start(
                        b.w1r[slot][:, :], cfg["g_d"][e, f]), b.ch_w1[slot], 16)
                    ctx.wait("gpsimd", b.pe, wu_last[slot])
                    wu_tgt = ctx.emit("gpsimd", lambda g, e=e, f=f, slot=slot: g.dma_start(
                        b.wur[slot][:, :], cfg["u_d"][e, f]), b.ch_wu[slot], 16)
                    for half in range(nhalves):
                        ctx.wait("tensor", b.ch_w1[slot], wg_tgt)
                        ctx.wait("tensor", b.ch_wu[slot], wu_tgt)
                        ctx.wait("tensor", b.ch_xnfm[buf], xnfm_tgt)
                        ctx.wait("tensor", b.se, pool01_free_se)
                        ctx.wait("tensor", b.ve, pool_free[6])
                        def mkgu(f, half, buf, slot):
                            def fn(te):
                                for c in range(nd):
                                    te.matmul(
                                        pool[5][:, 0:P1W], b.w1r[slot][:, c * 128:(c + 1) * 128],
                                        xnfm[buf][:, c * TB + half * P1W:c * TB + (half + 1) * P1W],
                                        start=(c == 0), stop=(c == nd - 1))
                                for c in range(nd):
                                    inst = te.matmul(
                                        pool[6][:, 0:P1W], b.wur[slot][:, c * 128:(c + 1) * 128],
                                        xnfm[buf][:, c * TB + half * P1W:c * TB + (half + 1) * P1W],
                                        start=(c == 0), stop=(c == nd - 1))
                                return inst
                            return fn
                        ctx.emit("tensor", mkgu(f, half, buf, slot), b.pe)
                        w1_last[slot] = b.pe.n
                        wu_last[slot] = b.pe.n
                        xnfm_last[buf] = b.pe.n
                        sl_h = slice(f * TB + half * P1W, f * TB + (half + 1) * P1W)
                        ctx.wait("scalar", b.pe)
                        if first_evac:
                            ctx.wait("scalar", b.pe, ph2_done)
                            first_evac = False
                        pool01_free_se = ctx.emit("scalar", lambda sc, sl_h=sl_h: sc.activation(
                            hid[:, sl_h], pool[5][:, 0:P1W], AF.Silu), b.se)
                        ctx.wait("vector", b.se, pool01_free_se)
                        pool_free[6] = ctx.emit("vector", lambda v, sl_h=sl_h: v.tensor_tensor(
                            hid[:, sl_h], pool[6][:, 0:P1W], hid[:, sl_h], ALU.mult), b.ve)
            hid_ve = b.ve.n
            for halfg in range(nhalfg):
                t4 = min(T4G, TPB - halfg * T4G)
                for f in range(nf):
                    slot = f % NWR
                    ctx.wait("gpsimd", b.pe, w2_last[slot])
                    w2_tgt = ctx.emit("gpsimd", lambda g, e=e, f=f, slot=slot: g.dma_start(
                        b.w2r[slot][:, :], cfg["w2_d"][e, f]), b.ch_w2[slot], 16)
                    ctx.wait("tensor", b.ch_w2[slot], w2_tgt)
                    if f == 0:
                        ctx.wait("tensor", b.ve, hid_ve)
                        if act == "swiglu":
                            ctx.wait("tensor", b.ve, pool_free[6])
                        if act == "relu_bias":
                            ctx.wait("tensor", b.ch_b2[0], b2_tgt)
                            ctx.wait("tensor", b.ve, b.ones_ve)
                        for q in range(2 * t4):
                            ctx.wait("tensor", b.ve, pool_free[q])
                    def mk2(e, f, halfg, t4, slot):
                        def fn(te):
                            for t4i in range(t4):
                                tsub = halfg * T4G + t4i
                                for nh in range(2):
                                    if f == 0 and act == "relu_bias":
                                        te.matmul(
                                            pool[t4i * 2 + nh][:, :], b.ones1[0:1, :],
                                            b.b2r[0][0:1, nh * 512:(nh + 1) * 512],
                                            start=True, stop=False)
                                    inst = te.matmul(
                                        pool[t4i * 2 + nh][:, :],
                                        hid[:, f * TB + tsub * 128:f * TB + (tsub + 1) * 128],
                                        b.w2r[slot][:, nh * 512:(nh + 1) * 512],
                                        start=(f == 0 and act != "relu_bias"),
                                        stop=(f == nf - 1))
                            return inst
                        return fn
                    ctx.emit("tensor", mk2(e, f, halfg, t4, slot), b.pe)
                    w2_last[slot] = b.pe.n
                    if f == 0 and act == "relu_bias":
                        b2_last[0] = b.pe.n
                ph2_done = b.pe.n
                ctx.wait("vector", b.pe)
                if e == 0 and tb == 0 and halfg == 0:
                    ctx.wait("vector", b.ve, b.combine_ve)
                def mkev(e, tb, halfg, t4):
                    def fn(v):
                        for t4i in range(t4):
                            tsub = halfg * T4G + t4i
                            gt = tb * TPB + tsub
                            for nh in range(2):
                                dst = b.xacc[:, gt * D + nh * 512:gt * D + (nh + 1) * 512]
                                cap = b.combine[:, gt * E + e:gt * E + e + 1]
                                if e == 0 and cfg.get("first_acc", False):
                                    inst = v.tensor_scalar(
                                        dst, pool[t4i * 2 + nh][:, :], cap, None, ALU.mult)
                                else:
                                    inst = v.scalar_tensor_tensor(
                                        dst, pool[t4i * 2 + nh][:, :], cap, dst,
                                        ALU.mult, ALU.add)
                        return inst
                    return fn
                ctx.emit("vector", mkev(e, tb, halfg, t4), b.ve)
                for q in range(2 * t4):
                    pool_free[q] = b.ve.n



"""Full-row attention emitter for L1_attn (alibi causal MHA) and L2_attn
(forgetting attention fwd + anti-causal bwd). See design notes in memory.

att_s (128,6144 bf16): Q[0:2048] | K[2048:4096] | P[4096:6144]
xtr[0]=PT_sb (128,1024 bf16); xtr[1]=V (128,1024 bf16, 16 tiles x 64)
hid[0:64, h*SL+col]: attn_fm per seq; stage rows: jrow/gcrow/urow
stats cols: mparts 0..9 | mtree 10..15 | lparts 16..25 | ltree 26..31 |
            rl 32 | bcol 33 | ebias 34
psum: pool[0..3] scores, pool[4] PV acc, pool[5,6] proj/outproj, poolB P^T.
"""


def get_alibi_slopes(n):
    import math
    def pow2_slopes(n):
        start = 2 ** (-2 ** (-(math.log2(n) - 3)))
        return [start * start ** i for i in range(n)]
    if math.log2(n).is_integer():
        return pow2_slopes(n)
    c = 2 ** math.floor(math.log2(n))
    return pow2_slopes(c) + get_alibi_slopes(2 * c)[0::2][:n - c]


def host_pack_headproj(wq_cols, wk_cols):
    D, H, dh = wq_cols.shape
    qk = np.concatenate([wq_cols, wk_cols], axis=2)
    a = qk.reshape(8, 128, H, 2 * dh).transpose(2, 1, 0, 3)
    return np.ascontiguousarray(a.reshape(H, 128, 8 * 2 * dh))


def host_pack_vslab(wv_cols):
    D, H, dh = wv_cols.shape
    v = wv_cols.reshape(8, 128, H, dh).transpose(2, 1, 0, 3)
    return np.ascontiguousarray(v.reshape(H, 128, 8 * dh))


def host_pack_wo(w_out):
    w = w_out.reshape(16, 64, 8, 128).transpose(2, 0, 1, 3)
    w = w.reshape(8, 2, 8, 64, 128).transpose(0, 1, 3, 2, 4)
    return np.ascontiguousarray(w.reshape(8, 2, 64, 8 * 128))


def host_masks():
    i = np.arange(128)[:, None]
    j = np.arange(128)[None, :]
    diag = np.where(j > i, -1e30, 0.0).astype(np.float32)
    diagT = np.where(j < i, -1e30, 0.0).astype(np.float32)
    return np.concatenate([diag, diagT], axis=1)


def alloc_attn(b, id_d):
    nc, ctx = b.nc, b.ctx
    if hasattr(b, "att_s"):
        return
    b.att_s = nc.alloc_sbuf_tensor("att_s", [128, 2048], BF16)
    b.masks = nc.alloc_sbuf_tensor("masks", [128, 256], BF16)
    b.ones32 = nc.alloc_sbuf_tensor("ones32", [1, 128], F32)
    b.negones32 = nc.alloc_sbuf_tensor("negones32", [1, 128], F32)
    b.onecol = nc.alloc_sbuf_tensor("onecol", [128, 1], F32)
    b.iotap = nc.alloc_sbuf_tensor("iotap", [128, 1], F32)
    b.identbf = nc.alloc_sbuf_tensor("identbf", [128, 128], BF16)
    b.bqk = nc.alloc_sbuf_tensor("bqk", [128, 32], F32)
    b.rowsW = nc.alloc_sbuf_tensor("rowsW", [128, 128], F32)
    b.bo_c2 = nc.alloc_sbuf_tensor("bo_c2", [128, 8], F32)
    b.ch_masks = ctx.chan("c_masks")
    b.ch_bqk = ctx.chan("c_bqk")
    b.ch_bo2 = ctx.chan("c_bo2")
    b.ch_gcrow = [ctx.chan(f"c_gcr{i}") for i in range(2)]
    b.ch_gcst = ctx.chan("c_gcst")
    b.ch_identbf = ctx.chan("c_identbf")
    b.gpe = ctx.chan("gpe")
    ctx.emit("vector", lambda v: v.memset(b.ones32[:, :], 1.0), b.ve)
    ctx.emit("vector", lambda v: v.memset(b.negones32[:, :], -1.0), b.ve)
    ctx.emit("vector", lambda v: v.memset(b.onecol[:, :], 1.0), b.ve)
    ctx.emit("vector", lambda v: v.memset(b.rowsW[64:128, 0:128], 1.0), b.ve)
    b.attconst_ve = b.ve.n
    ctx.emit("gpsimd", lambda g: g.iota(b.iotap[:, :], [[0, 1]], base=0,
                                        channel_multiplier=1,
                                        allow_small_or_imprecise_dtypes=True), b.gpe)
    ctx.emit("gpsimd", lambda g: g.dma_start(b.identbf[:, :], id_d), b.ch_identbf, 16)


def _tree(v, sums, srcs, scratch0, op):
    dst = scratch0
    cur = list(srcs)
    for _ in range(5):
        v.engine_nop()
    while len(cur) > 1:
        nxt = []
        for j in range(0, len(cur) - 1, 2):
            if op == "max":
                v.tensor_max(sums[:, dst:dst + 1], sums[:, cur[j]:cur[j] + 1],
                             sums[:, cur[j + 1]:cur[j + 1] + 1])
            else:
                v.tensor_add(sums[:, dst:dst + 1], sums[:, cur[j]:cur[j] + 1],
                             sums[:, cur[j + 1]:cur[j + 1] + 1])
            nxt.append(dst)
            dst += 1
        if len(cur) % 2 == 1:
            nxt.append(cur[-1])
        cur = nxt
        for _ in range(5):
            v.engine_nop()
    return cur[0]


def make_blocks(direction, qt, SL):
    blocks = []
    if direction == 0:
        ext0, ext1 = 0, qt * 128 + 128
        k0 = 0
        while k0 + 512 <= ext1 - 128:
            blocks.append((k0, 512, None))
            k0 += 512
        while k0 < ext1 - 128:
            blocks.append((k0, 128, None))
            k0 += 128
        blocks.append((k0, 128, "diag"))
    else:
        ext0, ext1 = qt * 128, SL
        blocks.append((ext0, 128, "diagT"))
        k0 = ext0 + 128
        while k0 < ext1 and (k0 - ext0) % 512 != 0:
            blocks.append((k0, 128, None))
            k0 += 128
        while k0 + 512 <= ext1:
            blocks.append((k0, 512, None))
            k0 += 512
        while k0 < ext1:
            blocks.append((k0, 128, None))
            k0 += 128
    return ext0, ext1, blocks


def emit_attn(b, cfg):
    ctx, nc = b.ctx, b.nc
    D, nd, TB = b.D, b.nd, b.TB
    H, dh = 16, 64
    mode = cfg["mode"]
    n_seq, seq_tb = cfg["n_seq"], cfg["seq_tb"]
    SL = seq_tb * TB
    NQT = SL // 128
    pool, poolB = b.pool, b.poolB
    att_s, stage, xtr32 = b.att_s, b.stage, b.xtr32
    hid = b.hid
    QO, KO, PO = 0, 2048, 0
    sums = b.stats
    slopes = get_alibi_slopes(H) if mode == "alibi" else None

    ctx.emit("sync", lambda s: s.dma_start(b.masks[:, :], cfg["masks_d"]), b.ch_masks, 16)
    if mode == "alibi":
        ctx.emit("gpsimd", lambda g: g.dma_start(b.bqk[:, :], cfg["bqk_d"]), b.ch_bqk, 16)
        ctx.emit("gpsimd", lambda g: g.dma_start(b.b2r[0][:, :], cfg["bv_d"]), b.ch_bqk, 16)
        ctx.emit("gpsimd", lambda g: g.dma_start(b.bo_c2[:, :], cfg["bo_d"]), b.ch_bo2, 16)
    else:
        ctx.emit("gpsimd", lambda g: g.dma_start(b.bqk[0:16, 16:17], cfg["bf_d"]), b.ch_bqk, 16)
        if not hasattr(b, "wfsb"):
            b.wfsb = nc.alloc_sbuf_tensor("wfsb", [128, 128], BF16)
            b.ch_wf = ctx.chan("c_wf")
        ctx.emit("gpsimd", lambda g: g.dma_start(b.wfsb[:, :], cfg["wf_d"]), b.ch_wf, 16)

    qk_last = [b.pe.n] * b.NWR
    v_last = [b.pe.n] * b.NWR
    wo_last = [b.pe.n] * b.NWR
    xnfm_last = [b.pe.n] * 2
    pool_free = [b.ve.n] * 7
    poolB_free = b.ve.n
    stage_pe = b.pe.n
    prev_pass_pe = b.pe.n
    jrow_ve = b.ve.n
    first = True
    xnfm_tgts = {}
    qk_tgt = {}
    v_tgt = {}
    row_tgts = {}

    def emit_fpass(s):
        nonlocal pool_free, stage_pe
        zs = [stage[0][0:16, 0:1024], stage[1][0:16, 0:1024]]
        ctx.wait("tensor", b.ch_wf)
        for piece in range(2):
            for hw in range(2):
                toff = piece * 1024 + hw * 512
                stb, off = toff // TB, toff % TB
                ctx.wait("tensor", b.ch_xnfm[stb], xnfm_tgts[stb])
                ctx.wait("tensor", b.ve, pool_free[5])
                if piece == 0 and hw == 0:
                    ctx.wait("tensor", b.pe, prev_pass_pe)
                def mkfg(stb, off):
                    def fn(te):
                        for c in range(nd):
                            inst = te.matmul(
                                pool[5][0:16, 0:512], b.wfsb[:, c * 16:(c + 1) * 16],
                                b.xnfm[stb][:, c * TB + off:c * TB + off + 512],
                                start=(c == 0), stop=(c == nd - 1))
                        return inst
                    return fn
                ctx.emit("tensor", mkfg(stb, off), b.pe)
                ctx.wait("vector", b.pe)
                if piece == 0 and hw == 0:
                    ctx.wait("vector", b.ch_bqk)
                pool_free[5] = ctx.emit("vector", lambda v, piece=piece, hw=hw: v.tensor_scalar(
                    stage[piece][0:16, hw * 512:(hw + 1) * 512], pool[5][0:16, 0:512],
                    b.bqk[0:16, 16:17], None, ALU.add), b.ve)
        for piece in range(2):
            ctx.wait("scalar", b.ve)
            def mkls(piece):
                def fn(sc):
                    sc.activation(xtr32[0:16, 0:1024], zs[piece], AF.Abs)
                    sc.activation(xtr32[0:16, 0:1024], xtr32[0:16, 0:1024],
                                  AF.Exp, scale=-1.0)
                    return sc.activation(xtr32[0:16, 0:1024], xtr32[0:16, 0:1024],
                                         AF.Ln, bias=b.onecol[0:16, 0:1])
                return fn
            ctx.emit("scalar", mkls(piece), b.se)
            ctx.wait("vector", b.se)
            def mkls2(piece):
                def fn(v):
                    v.tensor_single_scalar(zs[piece], zs[piece], 0.0, ALU.min)
                    return v.tensor_sub(zs[piece], zs[piece], xtr32[0:16, 0:1024])
                return fn
            ctx.emit("vector", mkls2(piece), b.ve)
            ctx.wait("scalar", b.ve)
        def mkscan_a(v):
            return v.tensor_tensor_scan(xtr32[0:16, 0:1024], zs[0], zs[0], 0.0,
                                        ALU.add, ALU.bypass)
        ctx.emit("vector", mkscan_a, b.ve)
        ctx.emit("vector", lambda v: v.tensor_sub(zs[0], xtr32[0:16, 0:1024], zs[0]), b.ve)
        gca_ve = b.ve.n
        ctx.wait("gpsimd", b.ve, gca_ve)
        ctx.emit("gpsimd", lambda g, s=s: g.dma_start(
            cfg["u_dram"][s, :, 0:1024], zs[0]), b.ch_gcst, 16)
        gst_a = b.ch_gcst.n
        ctx.wait("vector", b.ch_gcst, gst_a)
        def mkscan_b(v):
            return v.tensor_tensor_scan(zs[0], zs[1], zs[1],
                                        xtr32[0:16, 1023:1024], ALU.add, ALU.bypass)
        ctx.emit("vector", mkscan_b, b.ve)
        ctx.emit("vector", lambda v: v.tensor_sub(zs[1], zs[0], zs[1]), b.ve)
        ctx.emit("vector", lambda v: v.tensor_scalar(
            xtr32[0:16, 0:1024], xtr32[0:16, 0:1024], -1.0, None, ALU.mult), b.ve)
        ctx.emit("vector", lambda v: v.tensor_scalar(
            zs[0], zs[0], -1.0, None, ALU.mult), b.ve)
        gcb_ve = b.ve.n
        ctx.wait("gpsimd", b.ve, gcb_ve)
        ctx.emit("gpsimd", lambda g, s=s: g.dma_start(
            cfg["gc_dram"][s, :, 0:1024], xtr32[0:16, 0:1024]), b.ch_gcst, 16)
        ctx.emit("gpsimd", lambda g, s=s: g.dma_start(
            cfg["gc_dram"][s, :, 1024:2048], zs[0]), b.ch_gcst, 16)
        ctx.emit("gpsimd", lambda g, s=s: g.dma_start(
            cfg["u_dram"][s, :, 1024:2048], zs[1]), b.ch_gcst, 16)
        stage_pe = b.pe.n

    def emit_proj(s, h, slot):
        for part in range(2):
            for half in range(SL // 512):
                stb, off = (half * 512) // TB, (half * 512) % TB
                pi = 5 + (part * (SL // 512) + half) % 2
                ctx.wait("tensor", b.ch_w1[slot], qk_tgt[h])
                ctx.wait("tensor", b.ch_xnfm[stb], xnfm_tgts[stb])
                ctx.wait("tensor", b.ve, pool_free[pi])
                def mkqk(part, stb, off, pi, slot):
                    def fn(te):
                        for c in range(nd):
                            inst = te.matmul(
                                pool[pi][64:128, :],
                                b.w1r[slot][:, c * 128 + part * 64:c * 128 + (part + 1) * 64],
                                b.xnfm[stb][:, c * TB + off:c * TB + off + 512],
                                start=(c == 0), stop=(c == nd - 1),
                                tile_position=(0, 64))
                        return inst
                    return fn
                ctx.emit("tensor", mkqk(part, stb, off, pi, slot), b.pe)
                qk_last[slot] = b.pe.n
                xnfm_last[stb] = b.pe.n
                ctx.wait("vector", b.pe)
                if mode == "alibi" and h == 0 and part == 0 and half == 0:
                    ctx.wait("vector", b.ch_bqk)
                dst = hid[64:128, (QO if part == 0 else KO) + half * 512:
                          (QO if part == 0 else KO) + (half + 1) * 512]
                if mode == "alibi":
                    if part == 0:
                        pool_free[pi] = ctx.emit("vector",
                            lambda v, dst=dst, pi=pi, h=h: v.tensor_scalar(
                                dst, pool[pi][64:128, :], b.bqk[64:128, h:h + 1], 0.125,
                                ALU.add, ALU.mult), b.ve)
                    else:
                        pool_free[pi] = ctx.emit("vector",
                            lambda v, dst=dst, pi=pi, h=h: v.tensor_scalar(
                                dst, pool[pi][64:128, :], b.bqk[64:128, 16 + h:16 + h + 1],
                                None, ALU.add), b.ve)
                else:
                    if part == 0:
                        pool_free[pi] = ctx.emit("vector",
                            lambda v, dst=dst, pi=pi: v.tensor_scalar(
                                dst, pool[pi][64:128, :], 0.125, None, ALU.mult), b.ve)
                    else:
                        pool_free[pi] = ctx.emit("vector",
                            lambda v, dst=dst, pi=pi: v.tensor_copy(
                                dst, pool[pi][64:128, :]), b.ve)
        for g8 in range((SL + 1023) // 1024):
            ntt = min(8, SL // 128 - g8 * 8)
            pi = 5 + g8 % 2
            ctx.wait("tensor", b.ch_wu[slot], v_tgt[h])
            ctx.wait("tensor", b.ve, pool_free[pi])
            if mode == "alibi":
                ctx.wait("tensor", b.ch_bqk)
                ctx.wait("tensor", b.ve, b.ones_ve)
            def mkv(g8, pi, slot, h, ntt=ntt):
                def fn(te):
                    inst = None
                    for tt8 in range(ntt):
                        tt = g8 * 8 + tt8
                        stb, off = (tt * 128) // TB, (tt * 128) % TB
                        if mode == "alibi":
                            te.matmul(pool[pi][:, tt8 * 64:(tt8 + 1) * 64],
                                      b.ones1[0:1, :],
                                      b.b2r[0][0:1, h * 64:(h + 1) * 64],
                                      start=True, stop=False)
                        for c in range(nd):
                            inst = te.matmul(
                                pool[pi][:, tt8 * 64:(tt8 + 1) * 64],
                                b.xnfm[stb][:, c * TB + off:c * TB + off + 128],
                                b.wur[slot][:, c * 64:(c + 1) * 64],
                                start=(c == 0 and mode != "alibi"), stop=(c == nd - 1))
                    return inst
                return fn
            ctx.emit("tensor", mkv(g8, pi, slot, h), b.pe)
            v_last[slot] = b.pe.n
            xnfm_last[0] = b.pe.n
            xnfm_last[1] = b.pe.n
            ctx.wait("vector", b.pe)
            pool_free[pi] = ctx.emit("vector", lambda v, g8=g8, pi=pi, ntt=ntt: v.tensor_copy(
                b.xtr[1][:, g8 * 512:g8 * 512 + ntt * 64], pool[pi][:, 0:ntt * 64]), b.ve)

    def emit_head_attn(s, h, direction):
        nonlocal poolB_free, stage_pe, jrow_ve
        for qt in range(NQT):
            ext0, ext1, blocks = make_blocks(direction, qt, SL)
            ext = ext1 - ext0
            nb = len(blocks)
            assert nb <= 10
            for i, (k0, w, msk) in enumerate(blocks):
                r = k0 - ext0
                pj, pcol = r // 512, r % 512
                ctx.wait("tensor", b.ve, pool_free[pj])
                if mode == "fgate":
                    for piece in set([k0 // 1024, (k0 + w - 1) // 1024]):
                        ctx.wait("tensor", b.ch_gcrow[piece], row_tgts[piece])
                else:
                    ctx.wait("tensor", b.ve, jrow_ve)
                def mksc(h, qt, k0, w, pj, pcol, direction):
                    def fn(te):
                        te.matmul(pool[pj][:, pcol:pcol + w],
                                  hid[64:128, QO + qt * 128:QO + (qt + 1) * 128],
                                  hid[64:128, KO + k0:KO + k0 + w],
                                  start=True, stop=False)
                        lt = b.rowsW[64:65, 0:128]
                        segs = []
                        a0 = k0
                        while a0 < k0 + w:
                            a1 = min(k0 + w, (a0 // 1024 + 1) * 1024)
                            segs.append((a0, a1))
                            a0 = a1
                        for si, (a0, a1) in enumerate(segs):
                            inst = te.matmul(
                                pool[pj][:, pcol + (a0 - k0):pcol + (a1 - k0)],
                                lt,
                                stage[a0 // 1024][64:65, a0 % 1024:a0 % 1024 + (a1 - a0)],
                                start=False, stop=(si == len(segs) - 1))
                        return inst
                    return fn
                ctx.emit("tensor", mksc(h, qt, k0, w, pj, pcol, direction), b.pe)
                stage_pe = b.pe.n
            ctx.wait("vector", b.pe)
            ctx.wait("vector", b.ch_masks)
            def mkmask(blocks, ext0):
                def fn(v):
                    inst = v.engine_nop()
                    for (k0, w, msk) in blocks:
                        if msk is None:
                            continue
                        r = k0 - ext0
                        moff = 0 if msk == "diag" else 128
                        inst = v.tensor_tensor(
                            pool[r // 512][:, r % 512:r % 512 + 128],
                            pool[r // 512][:, r % 512:r % 512 + 128],
                            b.masks[:, moff:moff + 128], ALU.add)
                    for _ in range(4):
                        v.engine_nop()
                    return inst
                return fn
            ctx.emit("vector", mkmask(blocks, ext0), b.ve)
            nbk = (ext + 511) // 512
            def mkmp(nbk, ext):
                def fn(v):
                    for i in range(nbk):
                        w = min(512, ext - i * 512)
                        v.tensor_reduce(sums[:, i:i + 1], pool[i][:, 0:w],
                                        AX.X, ALU.max)
                    mfin = _tree(v, sums, list(range(nbk)), 10, "max")
                    return v.tensor_copy(sums[:, 15:16], sums[:, mfin:mfin + 1])
                return fn
            ctx.emit("vector", mkmp(nbk, ext), b.ve)
            mt_ve = b.ve.n
            ctx.wait("scalar", b.ve, mt_ve)
            ctx.emit("scalar", lambda sc: sc.activation(
                sums[:, 34:35], sums[:, 15:16], AF.Copy, scale=-1.0), b.se)
            def mkexp(nbk, ext):
                def fn(sc):
                    inst = None
                    for i in range(nbk):
                        w = min(512, ext - i * 512)
                        inst = sc.activation(
                            att_s[:, PO + i * 512:PO + i * 512 + w],
                            pool[i][:, 0:w], AF.Exp,
                            bias=sums[:, 34:35],
                            accum_out=sums[:, 16 + i:17 + i])
                    return inst
                return fn
            ctx.emit("scalar", lambda sc: sc.activation(
                xtr32[:, 0:256], b.masks[:, 0:256], AF.Copy), b.se)
            ctx.emit("scalar", mkexp(nbk, ext), b.se)
            exp_se = b.se.n
            ctx.wait("vector", b.se, exp_se)
            def mklt1(nbk):
                def fn(v):
                    lfin = _tree(v, sums, [16 + i for i in range(nbk)], 26, "add")
                    v.reciprocal(sums[:, 32:33], sums[:, lfin:lfin + 1])
                    for _ in range(6):
                        v.engine_nop()
                    return v.engine_nop()
                return fn
            ctx.emit("vector", mklt1(nbk), b.ve)
            for pj in range(4):
                pool_free[pj] = b.ve.n
            ctx.emit("vector", lambda v, ext=ext: v.tensor_scalar(
                att_s[:, PO:PO + ext], att_s[:, PO:PO + ext],
                sums[:, 32:33], None, ALU.mult), b.ve)
            pnorm_ve = b.ve.n
            nkt = ext // 128
            ctx.wait("tensor", b.ve, pnorm_ve)
            ctx.wait("tensor", b.ve, pool_free[4])
            ctx.wait("tensor", b.ch_identbf)
            first_pv = True
            nrnd = (nkt + 7) // 8
            for rnd in range(nrnd):
                r0 = rnd * 8
                rn = min(8, nkt - r0)
                ctx.wait("tensor", b.ve, poolB_free)
                def mktr(r0, rn):
                    def fn(te):
                        for j in range(rn):
                            inst = te.transpose(
                                poolB[:, j * 128:(j + 1) * 128],
                                att_s[:, PO + (r0 + j) * 128:PO + (r0 + j + 1) * 128],
                                b.identbf[:, :])
                        return inst
                    return fn
                ctx.emit("tensor", mktr(r0, rn), b.pe)
                ctx.wait("vector", b.pe)
                evb_ve = ctx.emit("vector", lambda v, rn=rn: v.tensor_copy(
                    b.xtr[0][:, 0:rn * 128], poolB[:, 0:rn * 128]), b.ve)
                poolB_free = evb_ve
                ctx.wait("tensor", b.ve, evb_ve)
                def mkpv(r0, rn, ext0, first_pv, last):
                    def fn(te):
                        inst = None
                        for j in range(rn):
                            tt = ext0 // 128 + r0 + j
                            inst = te.matmul(
                                pool[4][0:64, 0:128],
                                b.xtr[1][:, tt * 64:(tt + 1) * 64],
                                b.xtr[0][:, j * 128:(j + 1) * 128],
                                start=(first_pv and j == 0),
                                stop=(last and j == rn - 1))
                        return inst
                    return fn
                ctx.emit("tensor", mkpv(r0, rn, ext0, first_pv, rnd == nrnd - 1), b.pe)
                first_pv = False
            ctx.wait("vector", b.pe)
            pool_free[4] = ctx.emit("vector", lambda v, h=h, qt=qt: v.tensor_copy(
                hid[0:64, h * SL + qt * 128:h * SL + (qt + 1) * 128],
                pool[4][0:64, 0:128]), b.ve)

    def emit_outproj(s, direction):
        nonlocal pool_free
        wo_d = cfg["wo_d"] if mode == "alibi" else (
            cfg["wo_f_d"] if direction == 0 else cfg["wo_b_d"])
        attn_done_ve = b.ve.n
        for cchunk in range(8):
            slot0 = (cchunk * 2) % b.NWR
            slot1 = (cchunk * 2 + 1) % b.NWR
            ctx.wait("gpsimd", b.pe, wo_last[slot0])
            wo_t0 = ctx.emit("gpsimd", lambda g, cchunk=cchunk, slot=slot0: g.dma_start(
                b.w2r[slot][0:64, :], wo_d[cchunk, 0]), b.ch_w2[slot0], 16)
            ctx.wait("gpsimd", b.pe, wo_last[slot1])
            wo_t1 = ctx.emit("gpsimd", lambda g, cchunk=cchunk, slot=slot1: g.dma_start(
                b.w2r[slot][0:64, :], wo_d[cchunk, 1]), b.ch_w2[slot1], 16)
            for th in range(SL // 512):
                pi = 5 + (th % 2)
                ctx.wait("tensor", b.ch_w2[slot0], wo_t0)
                ctx.wait("tensor", b.ch_w2[slot1], wo_t1)
                ctx.wait("tensor", b.ve, attn_done_ve)
                ctx.wait("tensor", b.ve, pool_free[pi])
                def mkoc(th, slot0, slot1, pi):
                    def fn(te):
                        for h in range(16):
                            sl = slot0 if h < 8 else slot1
                            inst = te.matmul(
                                pool[pi][:, :],
                                b.w2r[sl][0:64, (h % 8) * 128:(h % 8 + 1) * 128],
                                hid[0:64, h * SL + th * 512:h * SL + (th + 1) * 512],
                                start=(h == 0), stop=(h == 15))
                        return inst
                    return fn
                ctx.emit("tensor", mkoc(th, slot0, slot1, pi), b.pe)
                wo_last[slot0] = b.pe.n
                wo_last[slot1] = b.pe.n
                ctx.wait("vector", b.pe)
                if mode == "alibi":
                    ctx.wait("vector", b.ch_bo2)
                def mkev(cchunk, pi):
                    def fn(v):
                        if mode == "alibi":
                            return v.tensor_scalar(xtr32[:, 0:512], pool[pi][:, :],
                                                   b.bo_c2[:, cchunk:cchunk + 1],
                                                   None, ALU.add)
                        return v.tensor_copy(xtr32[:, 0:512], pool[pi][:, :])
                    return fn
                u_ve = ctx.emit("vector", mkev(cchunk, pi), b.ve)
                pool_free[pi] = u_ve
                pit = th % 2
                ctx.wait("tensor", b.ve, u_ve)
                ctx.wait("tensor", b.ve, pool_free[pit])
                def mktr2(pit):
                    def fn(te):
                        for j in range(4):
                            inst = te.transpose(pool[pit][:, j * 128:(j + 1) * 128],
                                                xtr32[:, j * 128:(j + 1) * 128],
                                                b.identsb[:, :])
                        return inst
                    return fn
                ctx.emit("tensor", mktr2(pit), b.pe)
                ctx.wait("vector", b.pe)
                def mkacc(th, s, pit, cchunk=cchunk):
                    def fn(v):
                        inst = None
                        for j in range(4):
                            gt = s * (SL // 128) + th * 4 + j
                            dst = b.xacc[:, gt * D + cchunk * 128:gt * D + (cchunk + 1) * 128]
                            inst = v.scalar_tensor_tensor(
                                dst, pool[pit][:, j * 128:(j + 1) * 128], 1.0, dst,
                                ALU.mult, ALU.add)
                        return inst
                    return fn
                ctx.emit("vector", mkacc(th, s, pit), b.ve)
                pool_free[pit] = b.ve.n

    for s in range(n_seq):
        for direction in range([1, 2][mode == "fgate"]):
            if direction == 0:
                for stb in range(seq_tb):
                    tb = s * seq_tb + stb
                    ctx.wait("sync", b.pe, xnfm_last[stb])
                    if first:
                        ctx.wait("sync", b.ch_xst, b.xst_total)
                        first = False
                    xnfm_tgts[stb] = ctx.emit("sync", lambda sy, stb=stb, tb=tb: sy.dma_start(
                        b.xnfm[stb][:, :].rearrange("p (c t) -> p c t", c=nd),
                        cfg["xnfm_d"][tb].rearrange("c p t -> p c t")), b.ch_xnfm[stb], 16)
                if mode == "fgate":
                    emit_fpass(s)
            for h in range(H):
                slot = h % b.NWR
                ctx.wait("gpsimd", b.pe, qk_last[slot])
                qk_tgt[h] = ctx.emit("gpsimd", lambda g, h=h, slot=slot: g.dma_start(
                    b.w1r[slot][:, :], cfg["qk_d"][h]), b.ch_w1[slot], 16)
                ctx.wait("gpsimd", b.pe, v_last[slot])
                v_tgt[h] = ctx.emit("gpsimd", lambda g, h=h, slot=slot: g.dma_start(
                    b.wur[slot][:, 0:512], cfg["v_d"][h]), b.ch_wu[slot], 16)
                emit_proj(s, h, slot)
                if mode == "fgate":
                    dsrc = cfg["gc_dram"] if direction == 0 else cfg["u_dram"]
                    for piece in range(2):
                        ctx.wait("sync", b.pe, stage_pe)
                        ctx.wait("sync", b.ch_gcst)
                        row_tgts[piece] = ctx.emit("sync",
                            lambda sy, s=s, h=h, piece=piece, dsrc=dsrc: sy.dma_start(
                                stage[piece][64:65, 0:1024],
                                dsrc[s, h:h + 1, piece * 1024:(piece + 1) * 1024]),
                            b.ch_gcrow[piece], 16)
                else:
                    ctx.wait("gpsimd", b.pe, stage_pe)
                    for piece in range(2):
                        ctx.emit("gpsimd", lambda g, piece=piece: g.iota(
                            stage[piece][64:65, 0:1024], [[1, 1024]], base=piece * 1024,
                            channel_multiplier=0,
                            allow_small_or_imprecise_dtypes=True), b.gpe)
                    ctx.wait("vector", b.gpe)
                    ctx.wait("vector", b.pe, stage_pe)
                    def mkjrow(h):
                        def fn(v):
                            v.tensor_scalar(stage[0][64:65, 0:1024], stage[0][64:65, 0:1024],
                                            slopes[h], None, ALU.mult)
                            return v.tensor_scalar(stage[1][64:65, 0:1024],
                                                   stage[1][64:65, 0:1024],
                                                   slopes[h], None, ALU.mult)
                        return fn
                    jrow_ve = ctx.emit("vector", mkjrow(h), b.ve)
                emit_head_attn(s, h, direction)
            emit_outproj(s, direction)
            prev_pass_pe = b.pe.n



"""L3_attn: ELU-feature linear attention (diagonal KV), base-0 only.

Q stored per-seq in hid[0:64, h*SL + col]; V never stored (only diag sums).
Out-proj feature-major (wo as lhsT), gelu-tanh in fm, PE transpose, += xacc.
Weight streams reuse moe rings: qk->w1r/ch_w1, v->wur/ch_wu, wo->w2r/ch_w2.
Scratch aliases stage/xtr32 (dead after prologue); sums alias stats (pitch 12).
"""


def host_pack_l3attn(wq, bq, wk, bk, wv, bv, wo, bo):
    D, H, dh = wq.shape
    qk = np.concatenate([wq, wk], axis=2)
    a = qk.reshape(8, 128, H, 2 * dh).transpose(2, 1, 0, 3)
    qk_p = np.ascontiguousarray(a.reshape(H, 128, 8 * 2 * dh))
    v = wv.reshape(8, 128, H, dh).transpose(2, 1, 0, 3)
    v_p = np.ascontiguousarray(v.reshape(H, 128, 8 * dh))
    w = wo.reshape(H, dh, 8, 128).transpose(2, 0, 1, 3)
    w = w.reshape(8, 2, 8, dh, 128).transpose(0, 1, 3, 2, 4)
    wo_p = np.ascontiguousarray(w.reshape(8, 2, dh, 8 * 128))
    bqkv = np.stack([bq, bk, bv], axis=0)
    bqkv_p = np.ascontiguousarray(bqkv.transpose(2, 0, 1).reshape(dh, 3 * H))
    bo_p = np.ascontiguousarray(bo.reshape(8, 128).T)
    return {"qk_p": qk_p, "v_p": v_p, "wo_p": wo_p, "bqkv_p": bqkv_p, "bo_p": bo_p}


def emit_l3attn(b, cfg):
    ctx, nc = b.ctx, b.nc
    D, nd, TB = b.D, b.nd, b.TB
    H, dh = 16, 64
    pool = b.pool
    n_seq, seq_tb = cfg["n_seq"], cfg["seq_tb"]
    SL = seq_tb * TB
    NH = TB // 512
    npart = seq_tb * NH
    assert npart <= 4 and H * 12 <= b.stats.shape[1]
    hid, stage, xtr32 = b.hid, b.stage, b.xtr32
    assert H * SL <= 32 * 1024

    if not hasattr(b, "l3a"):
        b.l3a = True
        b.bqkv = nc.alloc_sbuf_tensor("bqkv", [64, 3 * H], F32)
        b.bo_c = nc.alloc_sbuf_tensor("bo_c", [128, 8], F32)
        b.ch_bqkv = ctx.chan("c_bqkv")
        b.ch_bo = ctx.chan("c_bo")
    ctx.emit("gpsimd", lambda g: g.dma_start(b.bqkv[:, :], cfg["bqkv_d"]), b.ch_bqkv, 16)
    ctx.emit("gpsimd", lambda g: g.dma_start(b.bo_c[:, :], cfg["bo_d"]), b.ch_bo, 16)

    fq = stage[0][0:64, 0:512]
    fk = stage[0][0:64, 512:1024]
    fv = stage[1][0:64, 0:512]
    mq = stage[1][0:64, 512:1024]
    mk = xtr32[0:64, 0:512]
    kvs = xtr32[0:64, 512:1024]
    sums = b.stats

    qk_last = [b.pe.n] * b.NWR
    v_last = [b.pe.n] * b.NWR
    wo_last = [b.pe.n] * b.NWR
    xnfm_last = [b.pe.n] * 2
    pool_free = [b.ve.n] * 8
    prev_seq_pe = b.pe.n
    first = True

    for s in range(n_seq):
        for stb in range(seq_tb):
            tb = s * seq_tb + stb
            buf = tb % 2
            ctx.wait("sync", b.pe, xnfm_last[buf])
            if first:
                ctx.wait("sync", b.ch_xst, b.xst_total)
                first = False
            xnfm_tgt = ctx.emit("sync", lambda sy, buf=buf, tb=tb: sy.dma_start(
                b.xnfm[buf][:, :].rearrange("p (c t) -> p c t", c=nd),
                cfg["xnfm_d"][tb].rearrange("c p t -> p c t")), b.ch_xnfm[buf], 16)
            for h in range(H):
                slot = h % b.NWR
                ctx.wait("gpsimd", b.pe, qk_last[slot])
                qk_tgt = ctx.emit("gpsimd", lambda g, h=h, slot=slot: g.dma_start(
                    b.w1r[slot][:, :], cfg["qk_d"][h]), b.ch_w1[slot], 16)
                ctx.wait("gpsimd", b.pe, v_last[slot])
                v_tgt = ctx.emit("gpsimd", lambda g, h=h, slot=slot: g.dma_start(
                    b.wur[slot][:, 0:512], cfg["v_d"][h]), b.ch_wu[slot], 16)
                for half in range(NH):
                    trio = ((tb * H + h) * NH + half) % 2
                    ps = [pool[3 * trio + i] for i in range(3)]
                    ctx.wait("tensor", b.ch_w1[slot], qk_tgt)
                    ctx.wait("tensor", b.ch_wu[slot], v_tgt)
                    ctx.wait("tensor", b.ch_xnfm[buf], xnfm_tgt)
                    for i in range(3):
                        ctx.wait("tensor", b.ve, pool_free[3 * trio + i])
                    def mkproj(h, half, buf, slot, ps):
                        def fn(te):
                            for part in range(3):
                                for c in range(nd):
                                    if part < 2:
                                        lhs = b.w1r[slot][:, c * 128 + part * 64:c * 128 + (part + 1) * 64]
                                    else:
                                        lhs = b.wur[slot][:, c * 64:(c + 1) * 64]
                                    inst = te.matmul(
                                        ps[part][0:64, :], lhs,
                                        b.xnfm[buf][:, c * TB + half * 512:c * TB + (half + 1) * 512],
                                        start=(c == 0), stop=(c == nd - 1))
                            return inst
                        return fn
                    ctx.emit("tensor", mkproj(h, half, buf, slot, ps), b.pe)
                    qk_last[slot] = b.pe.n
                    v_last[slot] = b.pe.n
                    xnfm_last[buf] = b.pe.n
                    ctx.wait("vector", b.pe)
                    if tb == s * seq_tb and h == 0 and half == 0:
                        ctx.wait("vector", b.ch_bqkv)
                        ctx.wait("vector", b.pe, prev_seq_pe)
                    def mkfeat(h, ps):
                        def fn(v):
                            v.tensor_scalar(fq, ps[0][0:64, :], b.bqkv[:, h:h + 1], None, ALU.add)
                            v.tensor_scalar(fk, ps[1][0:64, :], b.bqkv[:, H + h:H + h + 1], None, ALU.add)
                            v.tensor_scalar(fv, ps[2][0:64, :], b.bqkv[:, 2 * H + h:2 * H + h + 1], None, ALU.add)
                            v.tensor_single_scalar(mq, fq, 0.0, ALU.min)
                            return v.tensor_single_scalar(mk, fk, 0.0, ALU.min)
                        return fn
                    ctx.emit("vector", mkfeat(h, ps), b.ve)
                    ctx.wait("scalar", b.ve)
                    ctx.emit("scalar", lambda sc: sc.activation(mq, mq, AF.Exp), b.se)
                    ctx.emit("scalar", lambda sc: sc.activation(mk, mk, AF.Exp), b.se)
                    ctx.wait("vector", b.se)
                    def mkfeat2(h, half, stb):
                        qcol = h * SL + stb * TB + half * 512
                        cc = h * 12 + (stb * NH + half) * 2
                        def fn(v):
                            v.tensor_single_scalar(fk, fk, 0.0, ALU.max)
                            v.tensor_single_scalar(fq, fq, 0.0, ALU.max)
                            v.tensor_add(fk, fk, mk)
                            v.tensor_add(fq, fq, mq)
                            v.tensor_mul(kvs, fk, fv)
                            v.tensor_copy(hid[0:64, qcol:qcol + 512], fq)
                            v.tensor_reduce(sums[0:64, cc:cc + 1], kvs, AX.X, ALU.add)
                            return v.tensor_reduce(sums[0:64, cc + 1:cc + 2], fk, AX.X, ALU.add)
                        return fn
                    pf = ctx.emit("vector", mkfeat2(h, half, stb), b.ve)
                    for i in range(3):
                        pool_free[3 * trio + i] = pf
        def mkcomb():
            def fn(v):
                for h in range(H):
                    c = h * 12
                    if npart >= 3:
                        v.tensor_add(sums[0:64, c + 8:c + 9], sums[0:64, c:c + 1],
                                     sums[0:64, c + 4:c + 5])
                    else:
                        v.tensor_copy(sums[0:64, c + 8:c + 9], sums[0:64, c:c + 1])
                for h in range(H):
                    c = h * 12
                    if npart == 4:
                        v.tensor_add(sums[0:64, c + 9:c + 10], sums[0:64, c + 2:c + 3],
                                     sums[0:64, c + 6:c + 7])
                    elif npart >= 2:
                        v.tensor_copy(sums[0:64, c + 9:c + 10], sums[0:64, c + 2:c + 3])
                    else:
                        v.memset(sums[0:64, c + 9:c + 10], 0.0)
                for h in range(H):
                    c = h * 12
                    if npart >= 3:
                        v.tensor_add(sums[0:64, c + 10:c + 11], sums[0:64, c + 1:c + 2],
                                     sums[0:64, c + 5:c + 6])
                    else:
                        v.tensor_copy(sums[0:64, c + 10:c + 11], sums[0:64, c + 1:c + 2])
                for h in range(H):
                    c = h * 12
                    if npart == 4:
                        v.tensor_add(sums[0:64, c + 11:c + 12], sums[0:64, c + 3:c + 4],
                                     sums[0:64, c + 7:c + 8])
                    elif npart >= 2:
                        v.tensor_copy(sums[0:64, c + 11:c + 12], sums[0:64, c + 3:c + 4])
                    else:
                        v.memset(sums[0:64, c + 11:c + 12], 0.0)
                for h in range(H):
                    c = h * 12
                    v.tensor_add(sums[0:64, c + 8:c + 9], sums[0:64, c + 8:c + 9],
                                 sums[0:64, c + 9:c + 10])
                for h in range(H):
                    c = h * 12
                    inst = v.tensor_add(sums[0:64, c + 10:c + 11], sums[0:64, c + 10:c + 11],
                                        sums[0:64, c + 11:c + 12])
                return inst
            return fn
        ctx.emit("vector", mkcomb(), b.ve)
        for h in range(H):
            def mkattn(h):
                c = h * 12
                def fn(v):
                    inst = None
                    for off in range(0, SL, 512):
                        q_ap = hid[0:64, h * SL + off:h * SL + off + 512]
                        v.tensor_scalar(fq, q_ap, sums[0:64, c + 10:c + 11], 1e-6,
                                        ALU.mult, ALU.add)
                        v.tensor_scalar(fk, q_ap, sums[0:64, c + 8:c + 9], None, ALU.mult)
                        v.reciprocal(fv, fq)
                        for _ in range(4):
                            v.engine_nop()
                        inst = v.tensor_mul(q_ap, fk, fv)
                    return inst
                return fn
            ctx.emit("vector", mkattn(h), b.ve)
        attn_ve = b.ve.n
        for cchunk in range(8):
            slot0 = (cchunk * 2) % b.NWR
            slot1 = (cchunk * 2 + 1) % b.NWR
            ctx.wait("gpsimd", b.pe, wo_last[slot0])
            wo_tgt0 = ctx.emit("gpsimd", lambda g, cchunk=cchunk, slot=slot0: g.dma_start(
                b.w2r[slot][0:64, :], cfg["wo_d"][cchunk, 0]), b.ch_w2[slot0], 16)
            ctx.wait("gpsimd", b.pe, wo_last[slot1])
            wo_tgt1 = ctx.emit("gpsimd", lambda g, cchunk=cchunk, slot=slot1: g.dma_start(
                b.w2r[slot][0:64, :], cfg["wo_d"][cchunk, 1]), b.ch_w2[slot1], 16)
            for th in range(SL // 512):
                pi = 6 if th % 2 == 0 else 2
                ctx.wait("tensor", b.ch_w2[slot0], wo_tgt0)
                ctx.wait("tensor", b.ch_w2[slot1], wo_tgt1)
                ctx.wait("tensor", b.ve, attn_ve)
                ctx.wait("tensor", b.ve, pool_free[pi])
                def mkoc(th, s, slot0, slot1, pi):
                    def fn(te):
                        for h in range(H):
                            sl = slot0 if h < 8 else slot1
                            inst = te.matmul(
                                pool[pi][:, :],
                                b.w2r[sl][0:64, (h % 8) * 128:(h % 8 + 1) * 128],
                                hid[0:64, h * SL + th * 512:h * SL + (th + 1) * 512],
                                start=(h == 0), stop=(h == H - 1))
                        return inst
                    return fn
                ctx.emit("tensor", mkoc(th, s, slot0, slot1, pi), b.pe)
                wo_last[slot0] = b.pe.n
                wo_last[slot1] = b.pe.n
                ctx.wait("vector", b.pe)
                ctx.wait("vector", b.ch_bo)
                def mkgelu(cchunk, pi):
                    def fn(v):
                        u = stage[0][:, 0:512]
                        u2 = stage[0][:, 512:1024]
                        t3 = stage[1][:, 0:512]
                        v.tensor_scalar(u, pool[pi][:, :], b.bo_c[:, cchunk:cchunk + 1],
                                        None, ALU.add)
                        v.tensor_mul(u2, u, u)
                        v.tensor_mul(t3, u2, u)
                        return v.scalar_tensor_tensor(t3, t3, 0.044715, u, ALU.mult, ALU.add)
                    return fn
                pool_free[pi] = ctx.emit("vector", mkgelu(cchunk, pi), b.ve)
                ctx.wait("scalar", b.ve)
                ctx.emit("scalar", lambda sc: sc.activation(
                    stage[1][:, 0:512], stage[1][:, 0:512], AF.Tanh,
                    scale=0.7978845608028654), b.se)
                ctx.wait("vector", b.se)
                def mkgelu2():
                    def fn(v):
                        a = stage[1][:, 512:1024]
                        v.tensor_scalar(a, stage[1][:, 0:512], 0.5, 0.5, ALU.mult, ALU.add)
                        return v.tensor_mul(xtr32[:, 0:512], a, stage[0][:, 0:512])
                    return fn
                gel_ve = ctx.emit("vector", mkgelu2(), b.ve)
                pit = th % 2
                ctx.wait("tensor", b.ve, gel_ve)
                ctx.wait("tensor", b.ve, pool_free[pit])
                def mktr(pit):
                    def fn(te):
                        for j in range(4):
                            inst = te.transpose(pool[pit][:, j * 128:(j + 1) * 128],
                                                xtr32[:, j * 128:(j + 1) * 128],
                                                b.identsb[:, :])
                        return inst
                    return fn
                ctx.emit("tensor", mktr(pit), b.pe)
                ctx.wait("vector", b.pe)
                def mkacc(cchunk, th, s, pit):
                    def fn(v):
                        for j in range(4):
                            gt = s * (SL // 128) + th * 4 + j
                            dst = b.xacc[:, gt * D + cchunk * 128:gt * D + (cchunk + 1) * 128]
                            inst = v.scalar_tensor_tensor(
                                dst, pool[pit][:, j * 128:(j + 1) * 128], 1.0, dst,
                                ALU.mult, ALU.add)
                        return inst
                    return fn
                ctx.emit("vector", mkacc(cchunk, th, s, pit), b.ve)
                pool_free[pit] = b.ve.n
        prev_seq_pe = b.pe.n



"""L0_ffn: dense rank-4 LoRA MoE with full-denominator top-2 weights,
plus optional y_attn streaming add (xacc += y).

Weights resident in hid (free during layer 0): W_lin slabs hid[:,0:8192],
leftall hid[:,8192:8448], rightall hid[0:32, 8448:9472]. b_lin row in b2r[0].
psum: pool[4] t-matmul, pool[5]/[6] base+lora, poolB transpose.
"""


def host_pack_l0ffn(W_lin, b_lin, left, right):
    wl = pack_rhs(np.ascontiguousarray(W_lin.T))
    E, D, R = left.shape
    la = left.transpose(1, 0, 2).reshape(D, E * R)
    la = np.ascontiguousarray(la.reshape(8, 128, E * R))
    ra = np.ascontiguousarray(right.reshape(E * R, D))
    return {"wl_p": wl, "la_p": la, "ra_p": ra,
            "blin_p": np.ascontiguousarray(b_lin.reshape(1, D))}


def emit_l0ffn(b, cfg):
    ctx, nc = b.ctx, b.nc
    D, nd, TB = b.D, b.nd, b.TB
    NT, TPB = b.NT, b.TB // 128
    NTB = NT // TPB
    E = 8
    pool, poolB = b.pool, b.poolB
    hid, xtr32 = b.hid, b.xtr32

    wl = hid[:, 1024:9216]
    la = hid[:, 9216:9472]
    ra = hid[0:32, 9472:10496]
    ch_wres = ctx.chan("c_wres")
    ctx.emit("gpsimd", lambda g: g.dma_start(
        wl.rearrange("p (c j) -> p c j", j=1024), cfg["wl_d"].rearrange("c p j -> p c j")),
        ch_wres, 16)
    ctx.emit("gpsimd", lambda g: g.dma_start(
        la.rearrange("p (c j) -> p c j", j=32), cfg["la_d"].rearrange("c p j -> p c j")),
        ch_wres, 16)
    ctx.emit("gpsimd", lambda g: g.dma_start(ra, cfg["ra_d"]), ch_wres, 16)
    ctx.emit("gpsimd", lambda g: g.dma_start(b.b2r[0][:, :], cfg["blin_d"]), ch_wres, 16)
    wres_total = ch_wres.n

    xnfm_last = [b.pe.n] * 2
    pool_free = {4: b.ve.n, 5: b.ve.n, 6: b.ve.n}
    poolB_free = b.ve.n
    first = True
    y_ch = ctx.chan("c_y") if cfg.get("y_d") is not None else None
    tp_ve = {}

    for tb in range(NTB):
        buf = tb % 2
        ctx.wait("sync", b.pe, xnfm_last[buf])
        if first:
            ctx.wait("sync", b.ch_xst, b.xst_total)
        xnfm_tgt = ctx.emit("sync", lambda s, buf=buf, tb=tb: s.dma_start(
            b.xnfm[buf][:, :].rearrange("p (c t) -> p c t", c=nd),
            cfg["xnfm_d"][tb].rearrange("c p t -> p c t")), b.ch_xnfm[buf], 16)
        for tt in range(TPB):
            gt = tb * TPB + tt
            if y_ch is not None:
                if gt < 2:
                    ctx.wait("gpsimd", b.ch_xst, b.xst_total)
                ctx.wait("gpsimd", b.ve, tp_ve.get(gt - 2, -1))
                y_tgt = ctx.emit("gpsimd", lambda g, gt=gt: g.dma_start(
                    b.xtr[1][:, (gt % 2) * 512:(gt % 2) * 512 + 512]
                    .rearrange("p (a d) -> p a d", d=256) if False else
                    b.xtr[1][:, (gt % 2) * 512:(gt % 2) * 512 + 512],
                    cfg["y_d"][gt][:, 0:512]), y_ch, 16)
                y_tgt2 = ctx.emit("gpsimd", lambda g, gt=gt: g.dma_start(
                    b.xtr[0][:, (gt % 2) * 512:(gt % 2) * 512 + 512],
                    cfg["y_d"][gt][:, 512:1024]), y_ch, 16)
                ctx.wait("vector", y_ch, y_tgt2)
                def mkyadd(gt):
                    def fn(v):
                        d0 = b.xacc[:, gt * D:gt * D + 512]
                        d1 = b.xacc[:, gt * D + 512:gt * D + 1024]
                        v.tensor_add(d0, d0, b.xtr[1][:, (gt % 2) * 512:(gt % 2) * 512 + 512])
                        return v.tensor_add(d1, d1, b.xtr[0][:, (gt % 2) * 512:(gt % 2) * 512 + 512])
                    return fn
                tp_ve[gt] = ctx.emit("vector", mkyadd(gt), b.ve)
            ctx.wait("tensor", b.ch_xnfm[buf], xnfm_tgt)
            ctx.wait("tensor", ch_wres, wres_total)
            ctx.wait("tensor", b.ve, pool_free[4])
            if first:
                first = False
            def mkt(buf, tt):
                def fn(te):
                    for c in range(nd):
                        inst = te.matmul(
                            pool[4][:, 0:32],
                            b.xnfm[buf][:, c * TB + tt * 128:c * TB + (tt + 1) * 128],
                            la[:, c * 32:(c + 1) * 32],
                            start=(c == 0), stop=(c == nd - 1))
                    return inst
                return fn
            ctx.emit("tensor", mkt(buf, tt), b.pe)
            xnfm_last[buf] = b.pe.n
            ctx.wait("vector", b.pe)
            if gt == 0:
                ctx.wait("vector", b.ve, b.combine_ve)
            def mkapply(gt):
                def fn(v):
                    v.tensor_copy(xtr32[:, 512:544], pool[4][:, 0:32])
                    for _ in range(5):
                        v.engine_nop()
                    for e in range(E):
                        inst = v.tensor_scalar(
                            b.att_s[:, e * 4:(e + 1) * 4], xtr32[:, 512 + e * 4:516 + e * 4],
                            b.combine[:, gt * E + e:gt * E + e + 1], None, ALU.mult)
                    return inst
                return fn
            pool_free[4] = ctx.emit("vector", mkapply(gt), b.ve)
            ap_ve = b.ve.n
            ctx.wait("tensor", b.ve, ap_ve)
            ctx.wait("tensor", b.ve, poolB_free)
            ctx.wait("tensor", b.ch_identbf)
            ctx.emit("tensor", lambda te: te.transpose(
                poolB[0:32, 0:128], b.att_s[:, 0:32], b.identbf[:, :]), b.pe)
            ctx.wait("vector", b.pe)
            poolB_free = ctx.emit("vector", lambda v: v.tensor_copy(
                b.att_s[0:32, 128:256], poolB[0:32, 0:128]), b.ve)
            tpT_ve = b.ve.n
            for nh in range(2):
                pi = 5 + nh
                ctx.wait("tensor", b.ve, pool_free[pi])
                ctx.wait("tensor", b.ve, tpT_ve)
                ctx.wait("tensor", b.ve, b.ones_ve)
                def mkbase(buf, tt, nh, pi):
                    def fn(te):
                        te.matmul(pool[pi][:, :], b.ones1[0:1, :],
                                  b.b2r[0][0:1, nh * 512:(nh + 1) * 512],
                                  start=True, stop=False)
                        for c in range(nd):
                            te.matmul(
                                pool[pi][:, :],
                                b.xnfm[buf][:, c * TB + tt * 128:c * TB + (tt + 1) * 128],
                                wl[:, c * 1024 + nh * 512:c * 1024 + (nh + 1) * 512],
                                start=False, stop=False)
                        return te.matmul(pool[pi][:, :], b.att_s[0:32, 128:256],
                                         ra[:, nh * 512:(nh + 1) * 512],
                                         start=False, stop=True)
                    return fn
                ctx.emit("tensor", mkbase(buf, tt, nh, pi), b.pe)
                xnfm_last[buf] = b.pe.n
                ctx.wait("vector", b.pe)
                def mkev(gt, nh, pi):
                    def fn(v):
                        dst = b.xacc[:, gt * D + nh * 512:gt * D + (nh + 1) * 512]
                        return v.scalar_tensor_tensor(dst, pool[pi][:, :], 1.0, dst,
                                                      ALU.mult, ALU.add)
                    return fn
                pool_free[pi] = ctx.emit("vector", mkev(gt, nh, pi), b.ve)



"""L0_attn TTT-linear, e-sharded (this core owns 128 of the 1024 output dims).

ref: xn=LN(x) (computed on-device here); v = xn@Wkv[D:].T + bkv[D:];
scan over steps s (batch-coupled): y_t = xn_t@W^T; E_t = y_t - v_t;
W -= c*(E_t^T@xn_t), c = eta*2/(B*D). Output y.

Rows r = s*B + b (step-major). Chunk = C=64 steps -> 1024 rows.
E = (I + c*SL)^{-1}(xn@Ws^T - v), SL strictly-16-block-lower of S=xn xn^T.
Solve in 16 stages of 64 rows; within stage exact (I+cLd)^{-1} =
I - cLd + c^2Ld^2 - c^3Ld^3 (Ld strictly-block-tri, nilpotent^4).
y = E + v; Ws -= c*(Xc^T E).
"""


def host_pack_l0attn(W0s, WvTs, bkvs):
    w0 = np.ascontiguousarray(W0s.T.reshape(8, 128, 128))
    wv = np.ascontiguousarray(WvTs.reshape(8, 128, 128))
    return {"w0t": w0, "wvt": wv,
            "bkvs": np.ascontiguousarray(bkvs.reshape(1, 128))}


def host_masks128():
    blk = np.arange(128) // 16
    L = (blk[:, None] > blk[None, :]).astype(np.float32)
    U = (blk[:, None] < blk[None, :]).astype(np.float32)
    return np.stack([L, U])


def build_l0attn(S_len=2048, B=16, D=1024, C=64, phase=99):
    rows = C * B
    nch = S_len // C
    nd = D // 128
    ntr = rows // 128
    nst = rows // 128
    nc = bass.Bass("TRN2", target_bir_lowering=False)
    ctx = Ctx(nc)

    xs_d = nc.dram_tensor("xs", [S_len, B, D], F32, kind="ExternalInput").ap()
    w0_d = nc.dram_tensor("w0t", [8, 128, 128], F32, kind="ExternalInput").ap()
    wv_d = nc.dram_tensor("wvt", [8, 128, 128], F32, kind="ExternalInput").ap()
    bkv_d = nc.dram_tensor("bkvs", [1, 128], F32, kind="ExternalInput").ap()
    eta_d = nc.dram_tensor("eta", [1, 1], F32, kind="ExternalInput").ap()
    m64_d = nc.dram_tensor("m64", [2, 128, 128], F32, kind="ExternalInput").ap()
    id_d = nc.dram_tensor("ident", [128, 128], F32, kind="ExternalInput").ap()
    y_d = nc.dram_tensor("y", [S_len, B, 128], BF16, kind="ExternalOutput").ap()

    stage = [nc.alloc_sbuf_tensor(f"stage{i}", [128, 1024], F32) for i in range(2)]
    xc_tm = nc.alloc_sbuf_tensor("xc_tm", [128, ntr * 1024], BF16)
    xc_fm = nc.alloc_sbuf_tensor("xc_fm", [128, nd * rows], BF16)
    S_sb = nc.alloc_sbuf_tensor("S_sb", [128, ntr * rows], BF16)
    E_sb = nc.alloc_sbuf_tensor("E_sb", [128, ntr * 128], BF16)
    base_sb = nc.alloc_sbuf_tensor("base_sb", [128, ntr * 128], F32)
    v_sb = nc.alloc_sbuf_tensor("v_sb", [128, ntr * 128], BF16)
    y_sb = nc.alloc_sbuf_tensor("y_sb", [128, ntr * 128], BF16)
    Wst = nc.alloc_sbuf_tensor("Wst", [128, nd * 128], F32)
    Wbf = nc.alloc_sbuf_tensor("Wbf", [128, nd * 128], BF16)
    wvt = nc.alloc_sbuf_tensor("wvt_sb", [128, nd * 128], BF16)
    wvtn = nc.alloc_sbuf_tensor("wvtn_sb", [128, nd * 128], BF16)
    bkv = nc.alloc_sbuf_tensor("bkv_sb", [1, 128], BF16)
    bkvn = nc.alloc_sbuf_tensor("bkvn_sb", [1, 128], BF16)
    identbf = nc.alloc_sbuf_tensor("identbf", [128, 128], BF16)
    ones1 = nc.alloc_sbuf_tensor("ones1", [1, 128], BF16)
    ones32 = nc.alloc_sbuf_tensor("ones32", [1, 128], F32)
    etac = nc.alloc_sbuf_tensor("etac", [128, 4], F32)
    m64 = nc.alloc_sbuf_tensor("m64sb", [128, 256], F32)
    ld64 = nc.alloc_sbuf_tensor("ld64", [128, 128], BF16)
    P64 = nc.alloc_sbuf_tensor("P64", [128, 128 * 14], F32)
    stats = nc.alloc_sbuf_tensor("stats", [128, 64], F32)
    spacer = nc.alloc_sbuf_tensor("spacer", [128, 512], F32)
    epsc = nc.alloc_sbuf_tensor("epsc", [128, 1], F32)

    pool = [nc.alloc_psum_tensor(f"pool{i}", [128, 512], F32) for i in range(7)]
    poolB = nc.alloc_psum_tensor("poolB", [128, 1024], BF16)

    pe = ctx.chan("pe"); ve = ctx.chan("ve"); se = ctx.chan("se")
    gst = ctx.chan("gst")
    ch_st = [ctx.chan(f"c_st{i}") for i in range(2)]
    ch_c = ctx.chan("c_c")
    ch_cg = ctx.chan("c_cg")

    ctx.emit("vector", lambda v: v.memset(ones1[:, :], 1.0), ve)
    ctx.emit("vector", lambda v: v.memset(ones32[:, :], 1.0), ve)
    ctx.emit("vector", lambda v: v.memset(epsc[:, :], 1e-5), ve)
    ones_ve = ve.n
    ctx.emit("sync", lambda s: s.dma_start(
        Wst[:, :].rearrange("p (c e) -> p c e", e=128),
        w0_d.rearrange("c p e -> p c e")), ch_c, 16)
    ctx.emit("sync", lambda s: s.dma_start(
        m64[:, :].rearrange("p (a e) -> p a e", e=128),
        m64_d.rearrange("a p e -> p a e")), ch_c, 16)
    ctx.emit("sync", lambda s: s.dma_start(etac[0:1, 3:4], eta_d), ch_c, 16)
    ctx.emit("gpsimd", lambda g: g.dma_start(
        wvt[:, :].rearrange("p (c e) -> p c e", e=128),
        wv_d.rearrange("c p e -> p c e")), ch_cg, 16)
    ctx.emit("gpsimd", lambda g: g.dma_start(bkv[:, :], bkv_d), ch_cg, 16)
    ctx.emit("gpsimd", lambda g: g.dma_start(identbf[:, :], id_d), ch_cg, 16)
    ctx.wait("tensor", ch_c)
    ctx.wait("tensor", ve, ones_ve)
    ctx.emit("tensor", lambda te: te.matmul(pool[6][:, 0:1], ones32[0:1, :],
                                            etac[0:1, 3:4], start=True, stop=True), pe)
    ctx.wait("vector", pe)
    cfac = 2.0 / (B * D)
    def mkc(v):
        v.tensor_scalar(etac[:, 0:1], pool[6][:, 0:1], -cfac, None, ALU.mult)
        v.tensor_scalar(etac[:, 1:2], pool[6][:, 0:1], cfac, None, ALU.mult)
        for _ in range(5):
            v.engine_nop()
        v.tensor_mul(etac[:, 1:2], etac[:, 1:2], etac[:, 1:2])
        for _ in range(5):
            v.engine_nop()
        return v.tensor_mul(etac[:, 2:3], etac[:, 1:2], etac[:, 1:2])
    ctx.emit("vector", mkc, ve)
    cready_ve = ve.n
    ctx.wait("vector", ch_cg)
    ctx.emit("vector", lambda v: v.tensor_scalar(
        wvtn[:, :], wvt[:, :], -1.0, None, ALU.mult), ve)
    ctx.emit("vector", lambda v: v.tensor_scalar(
        bkvn[:, :], bkv[:, :], -1.0, None, ALU.mult), ve)
    wbf_ve = ctx.emit("vector", lambda v: v.tensor_copy(Wbf[:, :], Wst[:, :]), ve)

    stage_free = [ve.n, ve.n]
    pool_free = [ve.n] * 7
    poolB_free = ve.n
    E_ready_ve = ve.n
    wupd_pe = pe.n
    y_st = {}

    if phase < 1:
        ctx.emit("gpsimd", lambda g: g.dma_start(
            y_d[0:1].rearrange("s b e -> (s b) e"), y_sb[0:16, 0:128]), gst, 16)
        ctx.wait("gpsimd", gst)
        ctx.emit("vector", lambda v: v.memset(y_sb[:, :], 0.0), ve)
        ctx.build()
        return nc
    for ch in range(nch):
        for tr in range(ntr):
            si = (ch * ntr + tr) % 2
            sb = tr * 8
            ctx.wait("sync", ve, stage_free[si])
            st_tgt = ctx.emit("sync", lambda s, ch=ch, tr=tr, si=si: s.dma_start(
                stage[si][:, :],
                xs_d[ch * C:(ch + 1) * C].rearrange("s b d -> (s b) d")
                [tr * 128:(tr + 1) * 128, :]), ch_st[si], 16)
            ctx.wait("vector", ch_st[si], st_tgt)
            ctx.emit("vector", lambda v, si=si, sb=sb: v.tensor_reduce(
                stats[:, sb:sb + 1], stage[si][:, :], AX.X, ALU.add, negate=True), ve)
            ctx.wait("scalar", ve)
            ctx.wait("scalar", ch_st[si], st_tgt)
            if ch > 0 and tr == 0:
                ctx.wait("scalar", pe, wupd_pe)
            def mksq(tr, si, sb):
                def fn(sc):
                    sc.activation(xc_tm[:, tr * 1024:(tr + 1) * 1024], stage[si][:, :],
                                  AF.Square, accum_out=stats[:, sb + 1:sb + 2])
                    sc.activation(stats[:, sb + 2:sb + 3], stats[:, sb:sb + 1],
                                  AF.Copy, scale=1.0 / D)
                    return sc.activation(stats[:, sb + 3:sb + 4], stats[:, sb:sb + 1],
                                         AF.Square, scale=1.0 / D)
                return fn
            ctx.emit("scalar", mksq(tr, si, sb), se)
            ctx.wait("vector", se)
            ctx.emit("vector", lambda v, sb=sb: v.scalar_tensor_tensor(
                stats[:, sb + 4:sb + 5], stats[:, sb + 1:sb + 2], 1.0 / D,
                stats[:, sb + 3:sb + 4], ALU.mult, ALU.subtract), ve)
            ctx.wait("scalar", ve)
            def mkrstd(si, sb):
                def fn(sc):
                    sc.activation(stats[:, sb + 5:sb + 6], stats[:, sb + 4:sb + 5],
                                  AF.Ln, bias=epsc[:, 0:1])
                    sc.activation(spacer[:, :], stage[si][:, 0:512], AF.Copy)
                    return sc.activation(stats[:, sb + 5:sb + 6], stats[:, sb + 5:sb + 6],
                                         AF.Exp, scale=-0.5)
                return fn
            ctx.emit("scalar", mkrstd(si, sb), se)
            ctx.wait("vector", se)
            ctx.emit("vector", lambda v, tr=tr, si=si, sb=sb: v.tensor_scalar(
                xc_tm[:, tr * 1024:(tr + 1) * 1024], stage[si][:, :],
                stats[:, sb + 2:sb + 3], stats[:, sb + 5:sb + 6],
                ALU.add, ALU.mult), ve)
            stage_free[si] = ve.n
            apply_ve = ve.n
            if phase < 2:
                continue
            ctx.wait("tensor", ve, apply_ve)
            ctx.wait("tensor", ve, poolB_free)
            ctx.wait("tensor", ch_cg)
            def mktr(tr):
                def fn(te):
                    for cc in range(8):
                        inst = te.transpose(
                            poolB[:, cc * 128:(cc + 1) * 128],
                            xc_tm[:, tr * 1024 + cc * 128:tr * 1024 + (cc + 1) * 128],
                            identbf[:, :])
                    return inst
                return fn
            ctx.emit("tensor", mktr(tr), pe)
            ctx.wait("vector", pe)
            def mkevt(tr):
                def fn(v):
                    inst = None
                    for cc in range(8):
                        inst = v.tensor_copy(
                            xc_fm[:, cc * rows + tr * 128:cc * rows + (tr + 1) * 128],
                            poolB[:, cc * 128:(cc + 1) * 128])
                    return inst
                return fn
            poolB_free = ctx.emit("vector", mkevt(tr), ve)
            fm_ve = ve.n
            if phase < 3:
                continue
            ctx.wait("tensor", ve, fm_ve)
            ctx.wait("tensor", ve, pool_free[2])
            ctx.wait("tensor", ve, wbf_ve)
            def mkvb(tr):
                def fn(te):
                    te.matmul(pool[2][:, 0:128], ones1[0:1, :], bkv[0:1, :],
                              start=True, stop=False)
                    for c in range(nd):
                        te.matmul(pool[2][:, 0:128],
                                  xc_fm[:, c * rows + tr * 128:c * rows + (tr + 1) * 128],
                                  wvt[:, c * 128:(c + 1) * 128],
                                  start=False, stop=(c == nd - 1))
                    te.matmul(pool[2][:, 128:256], ones1[0:1, :], bkvn[0:1, :],
                              start=True, stop=False)
                    for c in range(nd):
                        te.matmul(pool[2][:, 128:256],
                                  xc_fm[:, c * rows + tr * 128:c * rows + (tr + 1) * 128],
                                  wvtn[:, c * 128:(c + 1) * 128],
                                  start=False, stop=False)
                    for c in range(nd):
                        inst = te.matmul(
                            pool[2][:, 128:256],
                            xc_fm[:, c * rows + tr * 128:c * rows + (tr + 1) * 128],
                            Wbf[:, c * 128:(c + 1) * 128],
                            start=False, stop=(c == nd - 1))
                    return inst
                return fn
            ctx.emit("tensor", mkvb(tr), pe)
            ctx.wait("vector", pe)
            def mkvbe(tr):
                def fn(v):
                    v.tensor_copy(v_sb[:, tr * 128:(tr + 1) * 128], pool[2][:, 0:128])
                    return v.tensor_copy(base_sb[:, tr * 128:(tr + 1) * 128],
                                         pool[2][:, 128:256])
                return fn
            pool_free[2] = ctx.emit("vector", mkvbe(tr), ve)
        for tr in range(ntr if phase >= 4 else 0):
            for cg in range(0, rows, 512):
                pi = 3 + ((tr * 2 + cg // 512) % 2)
                ctx.wait("tensor", ve, pool_free[pi])
                def mks(tr, cg, pi):
                    def fn(te):
                        for c in range(nd):
                            inst = te.matmul(
                                pool[pi][:, 0:512],
                                xc_fm[:, c * rows + tr * 128:c * rows + (tr + 1) * 128],
                                xc_fm[:, c * rows + cg:c * rows + cg + 512],
                                start=(c == 0), stop=(c == nd - 1))
                        return inst
                    return fn
                ctx.emit("tensor", mks(tr, cg, pi), pe)
                ctx.wait("vector", pe)
                pool_free[pi] = ctx.emit(
                    "vector", lambda v, tr=tr, cg=cg, pi=pi:
                    v.tensor_copy(S_sb[:, tr * rows + cg:tr * rows + cg + 512],
                                  pool[pi][:, 0:512]), ve)
        S_ve = ve.n
        for st in range(nst if phase >= 5 else 0):
            r0 = st * 128
            big_pe = -1
            if st > 0:
                ctx.wait("tensor", ve, S_ve)
                ctx.wait("tensor", ve, pool_free[5])
                ctx.wait("tensor", ve, E_ready_ve)
                def mkbig(st, r0):
                    def fn(te):
                        for tj in range(st):
                            inst = te.matmul(
                                pool[5][:, 0:128],
                                S_sb[:, tj * rows + r0:tj * rows + r0 + 128],
                                E_sb[:, tj * 128:(tj + 1) * 128],
                                start=(tj == 0), stop=(tj == st - 1))
                        return inst
                    return fn
                ctx.emit("tensor", mkbig(st, r0), pe)
                big_pe = pe.n
            if big_pe >= 0:
                ctx.wait("vector", pe, big_pe)
            ctx.wait("vector", ve, cready_ve)
            if st == 0:
                ctx.wait("vector", gst, y_st.get((ch - 1, ntr - 1), -1))
            def mkP(st, r0):
                def fn(v):
                    bsl = base_sb[:, st * 128:(st + 1) * 128]
                    ssl = S_sb[:, st * rows + r0:st * rows + r0 + 128]
                    if st == 0:
                        v.tensor_copy(P64[:, 0:128], bsl)
                        v.tensor_copy(y_sb[:, 0:128], bsl)
                    else:
                        v.scalar_tensor_tensor(
                            P64[:, 0:128], pool[5][:, 0:128], etac[:, 0:1], bsl,
                            ALU.mult, ALU.add)
                        v.scalar_tensor_tensor(
                            y_sb[:, 0:128], pool[5][:, 0:128], etac[:, 0:1],
                            bsl, ALU.mult, ALU.add)
                    return v.tensor_mul(ld64[:, :], ssl, m64[:, 128:256])
                return fn
            pool_free[5] = ctx.emit("vector", mkP(st, r0), ve)
            ld_ve = ve.n
            for it in range(7):
                ctx.wait("tensor", ve, ld_ve if it == 0 else ve.n)
                ctx.wait("tensor", ve, pool_free[6])
                ctx.emit("tensor", lambda te: te.matmul(
                    pool[6][:, 0:128], ld64[:, :], y_sb[:, 0:128],
                    start=True, stop=True), pe)
                ctx.wait("vector", pe)
                def mkue(it):
                    def fn(v):
                        v.tensor_copy(P64[:, (it + 1) * 128:(it + 2) * 128],
                                      pool[6][:, 0:128])
                        return v.tensor_copy(y_sb[:, 0:128], pool[6][:, 0:128])
                    return fn
                pool_free[6] = ctx.emit("vector", mkue(it), ve)
            def mkE(st):
                def fn(v):
                    for _ in range(4):
                        v.engine_nop()
                    for j in range(4):
                        v.scalar_tensor_tensor(
                            P64[:, (8 + j) * 128:(9 + j) * 128],
                            P64[:, (2 * j + 1) * 128:(2 * j + 2) * 128],
                            etac[:, 0:1], P64[:, 2 * j * 128:(2 * j + 1) * 128],
                            ALU.mult, ALU.add)
                    for _ in range(3):
                        v.engine_nop()
                    v.scalar_tensor_tensor(
                        P64[:, 12 * 128:13 * 128], P64[:, 9 * 128:10 * 128],
                        etac[:, 1:2], P64[:, 8 * 128:9 * 128], ALU.mult, ALU.add)
                    v.scalar_tensor_tensor(
                        P64[:, 13 * 128:14 * 128], P64[:, 11 * 128:12 * 128],
                        etac[:, 1:2], P64[:, 10 * 128:11 * 128], ALU.mult, ALU.add)
                    for _ in range(4):
                        v.engine_nop()
                    return v.scalar_tensor_tensor(
                        E_sb[:, st * 128:(st + 1) * 128],
                        P64[:, 13 * 128:14 * 128], etac[:, 2:3],
                        P64[:, 12 * 128:13 * 128], ALU.mult, ALU.add)
                return fn
            ctx.emit("vector", mkE(st), ve)
            E_ready_ve = ve.n
        for tr in range(ntr if phase >= 6 else 0):
            def mky(tr):
                def fn(v):
                    return v.tensor_add(y_sb[:, tr * 128:(tr + 1) * 128],
                                        E_sb[:, tr * 128:(tr + 1) * 128],
                                        v_sb[:, tr * 128:(tr + 1) * 128])
                return fn
            ctx.emit("vector", mky(tr), ve)
            ctx.wait("gpsimd", ve)
            y_st[(ch, tr)] = ctx.emit("gpsimd", lambda g, ch=ch, tr=tr: g.dma_start(
                y_d[ch * C:(ch + 1) * C].rearrange("s b e -> (s b) e")
                [tr * 128:(tr + 1) * 128, :],
                y_sb[:, tr * 128:(tr + 1) * 128]), gst, 16)
        for c in range(nd if phase >= 6 else 0):
            pi = 3 + c % 2
            ctx.wait("tensor", ve, pool_free[pi])
            ctx.wait("tensor", ve, E_ready_ve)
            def mkwu(c, pi):
                def fn(te):
                    for tr in range(ntr):
                        inst = te.matmul(
                            pool[pi][:, 0:128],
                            xc_tm[:, tr * 1024 + c * 128:tr * 1024 + (c + 1) * 128],
                            E_sb[:, tr * 128:(tr + 1) * 128],
                            start=(tr == 0), stop=(tr == ntr - 1))
                    return inst
                return fn
            ctx.emit("tensor", mkwu(c, pi), pe)
            ctx.wait("vector", pe)
            def mkwue(c, pi):
                def fn(v):
                    return v.scalar_tensor_tensor(
                        Wst[:, c * 128:(c + 1) * 128], pool[pi][:, 0:128],
                        etac[:, 0:1], Wst[:, c * 128:(c + 1) * 128],
                        ALU.mult, ALU.add)
                return fn
            pool_free[pi] = ctx.emit("vector", mkwue(c, pi), ve)
        wupd_pe = pe.n
        if ch < nch - 1 and phase >= 6:
            def mkwbf(v):
                for _ in range(5):
                    v.engine_nop()
                return v.tensor_copy(Wbf[:, :], Wst[:, :])
            wbf_ve = ctx.emit("vector", mkwbf, ve)
    if phase < 6:
        ctx.emit("vector", lambda v: v.memset(y_sb[:, :], 0.0), ve)
        ctx.wait("gpsimd", ve)
        ctx.emit("gpsimd", lambda g: g.dma_start(
            y_d[0:1].rearrange("s b e -> (s b) e"), y_sb[0:16, 0:128]), gst, 16)
    ctx.wait("gpsimd", gst)
    ctx.build()
    return nc



"""L2_ffn: 2-layer local-window transformer (window 512, rotary-xpos,
l2norm qk ×scale, shift_token, LN·lg+lb + gelu-erf GLU FFN). Own launch.

Per inner layer l: x1 = x + attn(shift(x)); x2 = x1 + ffn(LN(shift(x1))).
x starts as xn (from xnfm_d); after l=1: out = xaccp + x2.
T tokens per core = n_seq * SL. th = 512-token chunk (4 tiles).
fm layouts: (NTB, 8, 128, 1024) bf16.
"""


def host_rotary(SL, window=512):
    half = np.arange(0, 64, 2, dtype=np.float64)
    inv_freq = 1.0 / 10000.0 ** (half / 64.0)
    t = np.arange(SL, dtype=np.float64)
    fr = t[:, None] * inv_freq[None, :]
    fr = np.concatenate([fr, fr], axis=-1)
    base = (half + 0.4 * 64) / (1.4 * 64)
    power = (t - SL // 2) / float(window // 2)
    sc = base[None, :] ** power[:, None]
    sc = np.concatenate([sc, sc], axis=-1)
    cos, sin = np.cos(fr), np.sin(fr)
    til = lambda a: np.tile(a, (1, 8)).astype(np.float32)
    return np.stack([til(cos * sc * 8.0), til(sin * sc * 8.0),
                     til(cos / sc), til(sin / sc)])


def host_pack_l2ffn(wqkv, qs, ksc, wo, lg, lb, w1, b1, w2, b2):
    D, HID, HP = 1024, 2730, 3072
    w1a = np.zeros((D, HP), np.float32); w1a[:, :HID] = w1[:, :HID]
    w1g = np.zeros((D, HP), np.float32); w1g[:, :HID] = w1[:, HID:]
    b1a = np.zeros((1, HP), np.float32); b1a[0, :HID] = b1[:HID]
    b1g = np.zeros((1, HP), np.float32); b1g[0, :HID] = b1[HID:]
    w2p = np.zeros((HP, D), np.float32); w2p[:HID] = w2
    return {
        "wqkv": np.ascontiguousarray(wqkv.reshape(8, 128, 1536)),
        "qs": np.tile(qs, 8).reshape(1, 512).astype(np.float32),
        "ks": np.tile(ksc, 8).reshape(1, 512).astype(np.float32),
        "wos": np.ascontiguousarray(
            wo.reshape(8, 64, 1024).transpose(1, 0, 2).reshape(64, 8192)),
        "lgb": np.stack([lg, lb]).reshape(2, 1024).astype(np.float32),
        "w1a": np.ascontiguousarray(w1a.reshape(8, 128, 6, 512).transpose(2, 0, 1, 3)),
        "w1g": np.ascontiguousarray(w1g.reshape(8, 128, 6, 512).transpose(2, 0, 1, 3)),
        "w2": np.ascontiguousarray(w2p.reshape(6, 4, 128, 1024)),
        "b1a": b1a, "b1g": b1g,
        "b2": b2.reshape(1, 1024).astype(np.float32),
    }


def make_blocks_local(qt):
    blocks = []
    klo = qt * 128 - 512
    if klo >= 0:
        blocks.append((klo, 128, "diagT"))
        k0 = klo + 128
    else:
        k0 = 0
    while k0 < qt * 128:
        blocks.append((k0, 128, None))
        k0 += 128
    blocks.append((qt * 128, 128, "diag"))
    return max(klo, 0), qt * 128 + 128, blocks


class NS:
    pass


def build_l2ffn(T=4096, SL=2048):
    NT, NTB, NTH = T // 128, T // 1024, T // 512
    D, nd = 1024, 8
    nc = bass.Bass("TRN2", target_bir_lowering=False)
    ctx = Ctx(nc)

    xnfm_d = nc.dram_tensor("xnfm", [NTB, nd, 128, 1024], BF16,
                            kind="ExternalInput").ap()
    xaccp_d = nc.dram_tensor("xaccp", [NT, 128, D], F32, kind="ExternalInput").ap()
    rot_d = nc.dram_tensor("rot", [4, SL, 512], F32, kind="ExternalInput").ap()
    id_d = nc.dram_tensor("ident", [128, 128], F32, kind="ExternalInput").ap()
    mk_d = nc.dram_tensor("masks_p", [128, 256], BF16, kind="ExternalInput").ap()
    wd = []
    for l in range(2):
        w = NS()
        for nm, shp in (("wqkv", [8, 128, 1536]), ("qs", [1, 512]), ("ks", [1, 512]),
                        ("wos", [64, 8192]), ("lgb", [2, 1024]),
                        ("w1a", [6, 8, 128, 512]), ("w1g", [6, 8, 128, 512]),
                        ("w2", [6, 4, 128, 1024]), ("b1a", [1, 3072]),
                        ("b1g", [1, 3072]), ("b2", [1, 1024])):
            setattr(w, nm, nc.dram_tensor(f"{nm}_{l}", shp, F32,
                                          kind="ExternalInput").ap())
        wd.append(w)
    out_d = nc.dram_tensor("out", [NT, 128, D], F32, kind="ExternalOutput").ap()
    x1fm_d = nc.dram_tensor("x1fm", [NTB, nd, 128, 1024], BF16, kind="Internal").ap()
    x1tm_d = nc.dram_tensor("x1tm", [NT, 128, D], BF16, kind="Internal").ap()
    x2fm_d = nc.dram_tensor("x2fm", [NTB, nd, 128, 1024], BF16, kind="Internal").ap()

    arena = nc.alloc_sbuf_tensor("arena", [128, 37376], BF16)
    Qr = arena[0:64, 0:4096]
    Kr = arena[0:64, 4096:12288]
    Vr = arena[:, 12288:16384]
    AO = arena[0:64, 16384:20480]
    rxn = arena[:, 20480:24576]
    x1th = arena[:, 24576:28672]
    qkbf = arena[:, 28672:29696]
    h2nfm = arena[:, 4096:8192]
    rx1tm = arena[:, 8192:12288]
    hp_fm = arena[:, 12288:12800]
    wsl = [arena[:, 12800 + i * 12288:12800 + (i + 1) * 12288] for i in range(2)]

    hfm = [nc.alloc_sbuf_tensor("hfm0", [128, 4096], BF16)]
    stage = [nc.alloc_sbuf_tensor(f"stg{i}", [128, 1024], F32) for i in range(2)]
    rotb = nc.alloc_sbuf_tensor("rotb", [128, 2048], BF16)
    qsb = nc.alloc_sbuf_tensor("qsb", [128, 1024], BF16)
    wqkv_sb = nc.alloc_sbuf_tensor("wqkv_sb", [128, 12288], BF16)
    wos_sb = nc.alloc_sbuf_tensor("wos_sb", [64, 8192], BF16)
    lgb = nc.alloc_sbuf_tensor("lgb", [128, 2048], BF16)
    rows_b = nc.alloc_sbuf_tensor("rows_b", [1, 8192], BF16)
    rowbuf = nc.alloc_sbuf_tensor("rowbuf", [1, 4096], BF16)
    x2acc = nc.alloc_sbuf_tensor("x2acc", [128, 4096], F32)
    h2t = nc.alloc_sbuf_tensor("h2t", [128, 1024], BF16)
    h2n = nc.alloc_sbuf_tensor("h2n", [128, 1024], BF16)
    gbuf = nc.alloc_sbuf_tensor("gbuf", [128, 512], F32)
    hp_tm = nc.alloc_sbuf_tensor("hp_tm", [128, 512], BF16)
    x2bf = h2t
    xaccp_sb = nc.alloc_sbuf_tensor("xaccp_sb", [128, 1024], F32)
    stg_fm = [nc.alloc_sbuf_tensor(f"stgfm{i}", [128, 1024], BF16) for i in range(2)]
    att_p = nc.alloc_sbuf_tensor("att_p", [128, 640], BF16)
    ptb = nc.alloc_sbuf_tensor("ptb", [128, 640], BF16)
    masks = nc.alloc_sbuf_tensor("masks", [128, 256], BF16)
    identbf = nc.alloc_sbuf_tensor("identbf", [128, 128], BF16)
    ones1 = nc.alloc_sbuf_tensor("ones1", [1, 128], BF16)
    ones32 = nc.alloc_sbuf_tensor("ones32", [1, 128], F32)
    epsc = nc.alloc_sbuf_tensor("epsc", [128, 1], F32)
    tiny = nc.alloc_sbuf_tensor("tiny", [128, 1], F32)
    stats = nc.alloc_sbuf_tensor("stats", [128, 64], F32)
    spacer = stage[1][:, 0:512]

    pool = [nc.alloc_psum_tensor(f"pool{i}", [128, 512], F32) for i in range(7)]
    poolB = nc.alloc_psum_tensor("poolB", [128, 1024], BF16)

    pe = ctx.chan("pe"); ve = ctx.chan("ve"); se = ctx.chan("se")
    gst = ctx.chan("gst")
    ch_h = [ctx.chan(f"c_h{i}") for i in range(2)]
    ch_rx = ctx.chan("c_rx")
    ch_rx1 = ctx.chan("c_rx1")
    ch_xa = ctx.chan("c_xa")
    ch_rot = ctx.chan("c_rot")
    ch_w = [ctx.chan(f"c_w{i}") for i in range(2)]
    ch_c = ctx.chan("c_c")
    ch_cg = ctx.chan("c_cg")
    ch_st = ctx.chan("c_st")
    ch_lw = ctx.chan("c_lw")

    ctx.emit("vector", lambda v: v.memset(ones1[:, :], 1.0), ve)
    ctx.emit("vector", lambda v: v.memset(ones32[:, :], 1.0), ve)
    ctx.emit("vector", lambda v: v.memset(epsc[:, :], 1e-5), ve)
    ctx.emit("vector", lambda v: v.memset(tiny[:, :], 1e-30), ve)
    ones_ve = ve.n
    ctx.emit("gpsimd", lambda g: g.dma_start(identbf[:, :], id_d), ch_cg, 16)
    ctx.emit("sync", lambda s: s.dma_start(masks[:, :], mk_d), ch_c, 16)

    pool_free = [ve.n] * 7
    poolB_free = ve.n
    hfm_pe = [pe.n]
    hfm_mz = [-1, -1]
    stage_ve = ve.n
    layer_pe = pe.n
    rot_ve = -1
    wsl_pe = [pe.n, pe.n]
    rxn_ve = ve.n
    rx1_ve = ve.n
    xaccp_ve = ve.n
    gst_guard = -1
    stg_fm_tgt = [-1, -1]

    def emit_shift_load(buf, fm_d, th, chan, pe_guard):
        g0 = th * 512
        tb, off = g0 // 1024, g0 % 1024
        seq_start = (g0 % SL) == 0
        ctx.wait("sync", pe, pe_guard)
        ctx.emit("sync", lambda s: s.dma_start(
            buf[:, 0:2048].rearrange("p (c t) -> p c t", c=4),
            fm_d[tb, 0:4, :, off:off + 512].rearrange("c p t -> p c t")), chan, 16)
        mz = -1
        if seq_start:
            ctx.emit("sync", lambda s: s.dma_start(
                buf[:, 2048:4096].rearrange("p (c t) -> p c t", c=4)[:, :, 1:512],
                fm_d[tb, 4:8, :, off:off + 511].rearrange("c p t -> p c t")), chan, 16)
            ctx.wait("vector", pe, pe_guard)
            def mkz(v):
                inst = None
                for c in range(4):
                    inst = v.memset(buf[:, 2048 + c * 512:2048 + c * 512 + 1], 0.0)
                return inst
            ctx.emit("vector", mkz, ve)
            mz = ve.n
        elif off == 0:
            for c in range(4):
                ctx.emit("sync", lambda s, c=c: s.dma_start(
                    buf[:, 2048 + c * 512:2048 + c * 512 + 1],
                    fm_d[tb - 1, 4 + c, :, 1023:1024],
                    allow_slow_non_contiguous=True), chan, 16)
            ctx.emit("sync", lambda s: s.dma_start(
                buf[:, 2048:4096].rearrange("p (c t) -> p c t", c=4)[:, :, 1:512],
                fm_d[tb, 4:8, :, 0:511].rearrange("c p t -> p c t")), chan, 16)
        else:
            ctx.emit("sync", lambda s: s.dma_start(
                buf[:, 2048:4096].rearrange("p (c t) -> p c t", c=4),
                fm_d[tb, 4:8, :, off - 1:off + 511].rearrange("c p t -> p c t")), chan, 16)
        return chan.n, mz

    for l in range(2):
        w = wd[l]
        src_fm = xnfm_d if l == 0 else x2fm_d
        ctx.wait("gpsimd", pe, layer_pe)
        ctx.emit("gpsimd", lambda g, w=w: g.dma_start(
            wqkv_sb[:, :].rearrange("p (c j) -> p c j", j=1536),
            w.wqkv.rearrange("c p j -> p c j")), ch_lw, 16)
        ctx.emit("gpsimd", lambda g, w=w: g.dma_start(wos_sb[:, :], w.wos), ch_lw, 16)
        ctx.emit("gpsimd", lambda g, w=w: g.dma_start(rows_b[0:1, 0:3072], w.b1a), ch_lw, 16)
        ctx.emit("gpsimd", lambda g, w=w: g.dma_start(rows_b[0:1, 3072:6144], w.b1g), ch_lw, 16)
        ctx.emit("gpsimd", lambda g, w=w: g.dma_start(rows_b[0:1, 6144:7168], w.b2), ch_lw, 16)
        ctx.emit("gpsimd", lambda g, w=w: g.dma_start(rowbuf[0:1, 0:1024], w.lgb[0:1]), ch_lw, 16)
        ctx.emit("gpsimd", lambda g, w=w: g.dma_start(rowbuf[0:1, 1024:2048], w.lgb[1:2]), ch_lw, 16)
        ctx.emit("gpsimd", lambda g, w=w: g.dma_start(rowbuf[0:1, 2048:2560], w.qs), ch_lw, 16)
        ctx.emit("gpsimd", lambda g, w=w: g.dma_start(rowbuf[0:1, 2560:3072], w.ks), ch_lw, 16)
        lw_gp = ch_lw.n
        ctx.wait("tensor", ch_lw, lw_gp)
        ctx.wait("tensor", ch_c)
        ctx.wait("tensor", ve, ones_ve)
        for half, src0, src1, dst0, dst1 in (
                (0, rowbuf[0:1, 0:512], rowbuf[0:1, 512:1024], lgb[:, 0:512], lgb[:, 512:1024]),
                (1, rowbuf[0:1, 1024:1536], rowbuf[0:1, 1536:2048], lgb[:, 1024:1536], lgb[:, 1536:2048]),
                (2, rowbuf[0:1, 2048:2560], rowbuf[0:1, 2560:3072], qsb[:, 0:512], qsb[:, 512:1024])):
            ctx.wait("tensor", ve, pool_free[0])
            ctx.wait("tensor", ve, pool_free[1])
            def mkm(src0=src0, src1=src1):
                def fn(te):
                    te.matmul(pool[0][:, 0:512], ones1[0:1, :], src0,
                              start=True, stop=True)
                    return te.matmul(pool[1][:, 0:512], ones1[0:1, :], src1,
                                     start=True, stop=True)
                return fn
            ctx.emit("tensor", mkm(), pe)
            ctx.wait("vector", pe)
            ctx.emit("vector", lambda v, dst0=dst0: v.tensor_copy(dst0, pool[0][:, 0:512]), ve)
            pool_free[0] = ve.n
            ctx.emit("vector", lambda v, dst1=dst1: v.tensor_copy(dst1, pool[1][:, 0:512]), ve)
            pool_free[1] = ve.n
        lw_ve = ve.n

        rxn_ve = ve.n
        for th in range(NTH):
            s = (th * 512) // SL
            th_in_seq = (th * 512 % SL) // 512
            hb = 0
            ctx.wait("sync", ch_st)
            htgt, hmz = emit_shift_load(hfm[hb], src_fm, th, ch_h[hb], hfm_pe[hb])
            g0 = th * 512
            tb, off = g0 // 1024, g0 % 1024
            ctx.wait("sync", ve, rxn_ve)
            ctx.wait("sync", pe, layer_pe)
            rxt = ctx.emit("sync", lambda s2, tb=tb, off=off, src_fm=src_fm: s2.dma_start(
                rxn[:, :].rearrange("p (c t) -> p c t", c=8),
                src_fm[tb, :, :, off:off + 512].rearrange("c p t -> p c t")), ch_rx, 16)
            for tt in range(4):
                qt = th_in_seq * 4 + tt
                gt = th * 4 + tt
                slot8 = gt % 8
                rb = 0
                ctx.wait("gpsimd", ve, rot_ve)
                rott = ctx.emit("gpsimd", lambda g, qt=qt: g.dma_start(
                    rotb[:, 0:2048].rearrange("p (a t) -> p a t", a=4),
                    rot_d[:, qt * 128:(qt + 1) * 128, :].rearrange("a t j -> t a j")),
                    ch_rot, 16)
                ctx.wait("tensor", ch_h[hb], htgt)
                if hmz >= 0:
                    ctx.wait("tensor", ve, hmz)
                ctx.wait("tensor", ch_lw, lw_gp)
                ctx.wait("tensor", ve, pool_free[2])
                ctx.wait("tensor", ve, pool_free[3])
                ctx.wait("tensor", ve, pool_free[5])
                def mkqkv(hb, tt):
                    def fn(te):
                        for part, pi in ((0, 2), (1, 3), (2, 5)):
                            for c in range(nd):
                                inst = te.matmul(
                                    pool[pi][:, 0:512],
                                    hfm[hb][:, c * 512 + tt * 128:c * 512 + (tt + 1) * 128],
                                    wqkv_sb[:, c * 1536 + part * 512:c * 1536 + (part + 1) * 512],
                                    start=(c == 0), stop=(c == nd - 1))
                        return inst
                    return fn
                ctx.emit("tensor", mkqkv(hb, tt), pe)
                hfm_pe[hb] = pe.n
                ctx.wait("vector", pe)
                pool_free[5] = ctx.emit("vector", lambda v, slot8=slot8: v.tensor_copy(
                    Vr[:, slot8 * 512:(slot8 + 1) * 512], pool[5][:, 0:512]), ve)
                ctx.wait("vector", ve, stage_ve)
                ctx.emit("vector", lambda v: v.tensor_copy(
                    stage[0][:, 0:512], pool[2][:, 0:512]), ve)
                pool_free[2] = ve.n
                ctx.emit("vector", lambda v: v.tensor_copy(
                    stage[0][:, 512:1024], pool[3][:, 0:512]), ve)
                pool_free[3] = ve.n
                qk32_ve = ve.n
                ctx.wait("scalar", ve, qk32_ve)
                def mksq2(sc):
                    sc.activation(stage[1][:, 0:512], stage[0][:, 0:512], AF.Square)
                    return sc.activation(stage[1][:, 512:1024], stage[0][:, 512:1024],
                                         AF.Square)
                ctx.emit("scalar", mksq2, se)
                ctx.wait("vector", se)
                def mkred(v):
                    for h in range(16):
                        inst = v.tensor_reduce(stats[:, h:h + 1],
                                               stage[1][:, h * 64:(h + 1) * 64],
                                               AX.X, ALU.add)
                    return inst
                ctx.emit("vector", mkred, ve)
                ctx.wait("scalar", ve)
                def mkln(sc):
                    sc.activation(stats[:, 16:32], stats[:, 0:16], AF.Ln,
                                  bias=tiny[:, 0:1])
                    sc.activation(spacer[:, 0:512], stage[0][:, 0:512], AF.Copy)
                    return sc.activation(stats[:, 16:32], stats[:, 16:32], AF.Exp,
                                         scale=-0.5)
                ctx.emit("scalar", mkln, se)
                ctx.wait("vector", se)
                ctx.wait("vector", ve, lw_ve)
                def mkapp(v):
                    for h in range(8):
                        v.tensor_scalar(stage[0][:, h * 64:(h + 1) * 64],
                                        stage[0][:, h * 64:(h + 1) * 64],
                                        stats[:, 16 + h:17 + h], None, ALU.mult)
                    for h in range(8):
                        v.tensor_scalar(stage[0][:, 512 + h * 64:512 + (h + 1) * 64],
                                        stage[0][:, 512 + h * 64:512 + (h + 1) * 64],
                                        stats[:, 24 + h:25 + h], None, ALU.mult)
                    v.tensor_mul(stage[0][:, 0:512], stage[0][:, 0:512], qsb[:, 0:512])
                    return v.tensor_mul(stage[0][:, 512:1024], stage[0][:, 512:1024],
                                        qsb[:, 512:1024])
                ctx.emit("vector", mkapp, ve)
                ctx.wait("vector", ch_rot, rott)
                def mkrot(rb):
                    def fn(v):
                        q3 = stage[0][:, 0:512].rearrange("p (h a) -> p h a", a=64)
                        k3 = stage[0][:, 512:1024].rearrange("p (h a) -> p h a", a=64)
                        r3 = stage[1][:, 0:512].rearrange("p (h a) -> p h a", a=64)
                        rk3 = stage[1][:, 512:1024].rearrange("p (h a) -> p h a", a=64)
                        v.tensor_scalar(r3[:, :, 0:32], q3[:, :, 32:64], -1.0,
                                        None, ALU.mult)
                        v.tensor_copy(r3[:, :, 32:64], q3[:, :, 0:32])
                        v.tensor_scalar(rk3[:, :, 0:32], k3[:, :, 32:64], -1.0,
                                        None, ALU.mult)
                        v.tensor_copy(rk3[:, :, 32:64], k3[:, :, 0:32])
                        ro = rotb[:, rb * 2048:(rb + 1) * 2048]
                        v.tensor_mul(stage[0][:, 0:512], stage[0][:, 0:512], ro[:, 0:512])
                        v.tensor_mul(stage[0][:, 512:1024], stage[0][:, 512:1024],
                                     ro[:, 1024:1536])
                        v.tensor_mul(stage[1][:, 0:512], stage[1][:, 0:512],
                                     ro[:, 512:1024])
                        v.tensor_mul(stage[1][:, 512:1024], stage[1][:, 512:1024],
                                     ro[:, 1536:2048])
                        for _ in range(3):
                            v.engine_nop()
                        v.tensor_add(qkbf[:, 0:512], stage[0][:, 0:512],
                                     stage[1][:, 0:512])
                        return v.tensor_add(qkbf[:, 512:1024], stage[0][:, 512:1024],
                                            stage[1][:, 512:1024])
                    return fn
                ctx.emit("vector", mkrot(rb), ve)
                stage_ve = ve.n
                rot_ve = ve.n
                qkbf_ve = ve.n
                ctx.wait("tensor", ve, qkbf_ve)
                ctx.wait("tensor", ve, poolB_free)
                ctx.wait("tensor", ch_cg)
                def mktrq(te):
                    for h in range(8):
                        inst = te.transpose(poolB[0:64, h * 128:(h + 1) * 128],
                                            qkbf[:, h * 64:(h + 1) * 64], identbf[:, :])
                    return inst
                ctx.emit("tensor", mktrq, pe)
                ctx.wait("vector", pe)
                poolB_free = ctx.emit("vector", lambda v, tt=tt: v.tensor_copy(
                    Qr[:, tt * 1024:(tt + 1) * 1024], poolB[0:64, 0:1024]), ve)
                ctx.wait("tensor", ve, poolB_free)
                def mktrk(te):
                    for h in range(8):
                        inst = te.transpose(poolB[0:64, h * 128:(h + 1) * 128],
                                            qkbf[:, 512 + h * 64:512 + (h + 1) * 64],
                                            identbf[:, :])
                    return inst
                ctx.emit("tensor", mktrk, pe)
                ctx.wait("vector", pe)
                poolB_free = ctx.emit("vector", lambda v, slot8=slot8: v.tensor_copy(
                    Kr[:, slot8 * 1024:(slot8 + 1) * 1024], poolB[0:64, 0:1024]), ve)
            ring_ve = ve.n
            for tt in range(4):
                qt = th_in_seq * 4 + tt
                ext0, ext1, blocks = make_blocks_local(qt)
                ext = ext1 - ext0
                nb = len(blocks)
                for h in range(8):
                    ctx.wait("tensor", ve, ring_ve)
                    ctx.wait("tensor", ve, pool_free[0])
                    if ext > 512:
                        ctx.wait("tensor", ve, pool_free[1])
                    def mksc(h, tt, blocks, ext0, s):
                        def fn(te):
                            for (k0, wdt, msk) in blocks:
                                r = k0 - ext0
                                kslot = (s * (SL // 128) + k0 // 128) % 8
                                inst = te.matmul(
                                    pool[r // 512][:, r % 512:r % 512 + 128],
                                    Qr[:, tt * 1024 + h * 128:tt * 1024 + (h + 1) * 128],
                                    Kr[:, kslot * 1024 + h * 128:kslot * 1024 + (h + 1) * 128],
                                    start=True, stop=True)
                            return inst
                        return fn
                    ctx.emit("tensor", mksc(h, tt, blocks, ext0, s), pe)
                    ctx.wait("vector", pe)
                    ctx.wait("vector", ch_c)
                    def mkmask(blocks, ext0):
                        def fn(v):
                            inst = v.engine_nop()
                            for (k0, wdt, msk) in blocks:
                                if msk is None:
                                    continue
                                r = k0 - ext0
                                moff = 0 if msk == "diag" else 128
                                inst = v.tensor_tensor(
                                    pool[r // 512][:, r % 512:r % 512 + 128],
                                    pool[r // 512][:, r % 512:r % 512 + 128],
                                    masks[:, moff:moff + 128], ALU.add)
                            for _ in range(4):
                                v.engine_nop()
                            return inst
                        return fn
                    ctx.emit("vector", mkmask(blocks, ext0), ve)
                    nbk = (ext + 511) // 512
                    def mkmp(nbk, ext):
                        def fn(v):
                            for i in range(nbk):
                                w = min(512, ext - i * 512)
                                v.tensor_reduce(stats[:, 40 + i:41 + i],
                                                pool[i][:, 0:w], AX.X, ALU.max)
                            mfin = _tree(v, stats, list(range(40, 40 + nbk)), 48, "max")
                            return v.tensor_copy(stats[:, 54:55], stats[:, mfin:mfin + 1])
                        return fn
                    ctx.emit("vector", mkmp(nbk, ext), ve)
                    mt_ve = ve.n
                    ctx.wait("scalar", ve, mt_ve)
                    def mkexp(nbk, ext):
                        def fn(sc):
                            sc.activation(stats[:, 55:56], stats[:, 54:55], AF.Copy,
                                          scale=-1.0)
                            sc.activation(spacer[:, 0:256], masks[:, 0:256], AF.Copy)
                            inst = None
                            for i in range(nbk):
                                w = min(512, ext - i * 512)
                                inst = sc.activation(
                                    att_p[:, i * 512:i * 512 + w],
                                    pool[i][:, 0:w], AF.Exp,
                                    bias=stats[:, 55:56],
                                    accum_out=stats[:, 56 + i:57 + i])
                            return inst
                        return fn
                    ctx.emit("scalar", mkexp(nbk, ext), se)
                    exp_se = se.n
                    ctx.wait("vector", se, exp_se)
                    def mklt(nbk, ext):
                        def fn(v):
                            lfin = _tree(v, stats, list(range(56, 56 + nbk)), 32, "add")
                            v.reciprocal(stats[:, 38:39], stats[:, lfin:lfin + 1])
                            for _ in range(6):
                                v.engine_nop()
                            return v.tensor_scalar(att_p[:, 0:ext], att_p[:, 0:ext],
                                                   stats[:, 38:39], None, ALU.mult)
                        return fn
                    ctx.emit("vector", mklt(nbk, ext), ve)
                    pnorm_ve = ve.n
                    pool_free[0] = pnorm_ve
                    if ext > 512:
                        pool_free[1] = pnorm_ve
                    nkt = ext // 128
                    ctx.wait("tensor", ve, pnorm_ve)
                    ctx.wait("tensor", ve, poolB_free)
                    ctx.wait("tensor", ve, pool_free[4])
                    def mkpt(nkt):
                        def fn(te):
                            for j in range(nkt):
                                inst = te.transpose(
                                    poolB[:, j * 128:(j + 1) * 128],
                                    att_p[:, j * 128:(j + 1) * 128], identbf[:, :])
                            return inst
                        return fn
                    ctx.emit("tensor", mkpt(nkt), pe)
                    ctx.wait("vector", pe)
                    poolB_free = ctx.emit("vector", lambda v, nkt=nkt: v.tensor_copy(
                        ptb[:, 0:nkt * 128], poolB[:, 0:nkt * 128]), ve)
                    ctx.wait("tensor", ve, poolB_free)
                    def mkpv(nkt, ext0, h, s):
                        def fn(te):
                            for j in range(nkt):
                                kslot = (s * (SL // 128) + ext0 // 128 + j) % 8
                                inst = te.matmul(
                                    pool[4][0:64, 0:128],
                                    Vr[:, kslot * 512 + h * 64:kslot * 512 + (h + 1) * 64],
                                    ptb[:, j * 128:(j + 1) * 128],
                                    start=(j == 0), stop=(j == nkt - 1))
                            return inst
                        return fn
                    ctx.emit("tensor", mkpv(nkt, ext0, h, s), pe)
                    ctx.wait("vector", pe)
                    pool_free[4] = ctx.emit("vector", lambda v, tt=tt, h=h: v.tensor_copy(
                        AO[:, tt * 1024 + h * 128:tt * 1024 + (h + 1) * 128],
                        pool[4][0:64, 0:128]), ve)
            ao_ve = ve.n
            for c8 in range(8):
                pi = 6 if c8 % 2 == 0 else 2
                ctx.wait("tensor", ve, pool_free[pi])
                ctx.wait("tensor", ve, ao_ve)
                def mkop(c8, pi):
                    def fn(te):
                        for ttr in range(4):
                            for h in range(8):
                                inst = te.matmul(
                                    pool[pi][:, ttr * 128:(ttr + 1) * 128],
                                    wos_sb[:, h * 1024 + c8 * 128:h * 1024 + (c8 + 1) * 128],
                                    AO[:, ttr * 1024 + h * 128:ttr * 1024 + (h + 1) * 128],
                                    start=(h == 0), stop=(h == 7))
                        return inst
                    return fn
                ctx.emit("tensor", mkop(c8, pi), pe)
                ctx.wait("vector", pe)
                ctx.wait("vector", ch_rx, rxt)
                if c8 == 0:
                    ctx.wait("vector", ch_st)
                def mkopev(c8, pi):
                    def fn(v):
                        return v.tensor_add(x1th[:, c8 * 512:(c8 + 1) * 512],
                                            pool[pi][:, 0:512],
                                            rxn[:, c8 * 512:(c8 + 1) * 512])
                    return fn
                pool_free[pi] = ctx.emit("vector", mkopev(c8, pi), ve)
            x1th_ve = ve.n
            rxn_ve = x1th_ve
            ctx.wait("sync", ve, x1th_ve)
            ctx.emit("sync", lambda s2, tb=tb, off=off: s2.dma_start(
                x1fm_d[tb, :, :, off:off + 512].rearrange("c p t -> p c t"),
                x1th[:, :].rearrange("p (c t) -> p c t", c=8)), ch_st, 16)
            for ttr in range(4):
                bi = ttr % 2
                ctx.wait("tensor", ve, x1th_ve)
                ctx.wait("tensor", ve, poolB_free)
                def mktrx(ttr):
                    def fn(te):
                        for c in range(8):
                            inst = te.transpose(
                                poolB[:, c * 128:(c + 1) * 128],
                                x1th[:, c * 512 + ttr * 128:c * 512 + (ttr + 1) * 128],
                                identbf[:, :])
                        return inst
                    return fn
                ctx.emit("tensor", mktrx(ttr), pe)
                ctx.wait("vector", pe)
                ctx.wait("vector", ch_st, stg_fm_tgt[bi])
                poolB_free = ctx.emit("vector", lambda v, bi=bi: v.tensor_copy(
                    stg_fm[bi][:, :], poolB[:, :]), ve)
                ctx.wait("sync", ve)
                stg_fm_tgt[bi] = ctx.emit("sync", lambda s2, th=th, ttr=ttr, bi=bi:
                    s2.dma_start(x1tm_d[th * 4 + ttr], stg_fm[bi][:, :]), ch_st, 16)

        stb_pe0 = pe.n
        wsl_pe = [stb_pe0, stb_pe0]
        rx1_ve = ve.n
        for th in range(NTH):
            hb = 0
            g0 = th * 512
            tb, off = g0 // 1024, g0 % 1024
            ctx.wait("sync", ch_st)
            htgt, hmz = emit_shift_load(hfm[hb], x1fm_d, th, ch_h[hb], hfm_pe[hb])
            ctx.wait("sync", ve, rx1_ve)
            ctx.wait("sync", pe, stb_pe0)
            rx1t = ctx.emit("sync", lambda s2, th=th: s2.dma_start(
                rx1tm[:, :].rearrange("p (a t) -> p a t", a=4),
                x1tm_d[th * 4:th * 4 + 4].rearrange("a p t -> p a t")), ch_rx1, 16)
            for tt in range(4):
                ctx.wait("tensor", ch_h[hb], htgt)
                if hmz >= 0:
                    ctx.wait("tensor", ve, hmz)
                ctx.wait("tensor", ve, poolB_free)
                def mktr2(hb, tt):
                    def fn(te):
                        for c in range(8):
                            inst = te.transpose(
                                poolB[:, c * 128:(c + 1) * 128],
                                hfm[hb][:, c * 512 + tt * 128:c * 512 + (tt + 1) * 128],
                                identbf[:, :])
                        return inst
                    return fn
                ctx.emit("tensor", mktr2(hb, tt), pe)
                hfm_pe[hb] = pe.n
                ctx.wait("vector", pe)
                poolB_free = ctx.emit("vector", lambda v: v.tensor_copy(
                    h2t[:, :], poolB[:, :]), ve)
                ctx.emit("vector", lambda v: v.tensor_reduce(
                    stats[:, 0:1], h2t[:, :], AX.X, ALU.add, negate=True), ve)
                ctx.wait("scalar", ve)
                def mkst2(sc):
                    sc.activation(spacer[:, 0:512], h2t[:, 0:512], AF.Square,
                                  accum_out=stats[:, 1:2])
                    sc.activation(spacer[:, 0:512], h2t[:, 512:1024], AF.Square,
                                  accum_out=stats[:, 2:3])
                    sc.activation(stats[:, 3:4], stats[:, 0:1], AF.Copy, scale=1.0 / D)
                    return sc.activation(stats[:, 4:5], stats[:, 0:1], AF.Square,
                                         scale=1.0 / D)
                ctx.emit("scalar", mkst2, se)
                ctx.wait("vector", se)
                def mkvar(v):
                    v.tensor_add(stats[:, 5:6], stats[:, 1:2], stats[:, 2:3])
                    for _ in range(4):
                        v.engine_nop()
                    return v.scalar_tensor_tensor(stats[:, 5:6], stats[:, 5:6],
                                                  1.0 / D, stats[:, 4:5],
                                                  ALU.mult, ALU.subtract)
                ctx.emit("vector", mkvar, ve)
                ctx.wait("scalar", ve)
                def mkrstd2(sc):
                    sc.activation(stats[:, 6:7], stats[:, 5:6], AF.Ln, bias=epsc[:, 0:1])
                    sc.activation(spacer[:, 0:512], h2t[:, 0:512], AF.Copy)
                    return sc.activation(stats[:, 6:7], stats[:, 6:7], AF.Exp, scale=-0.5)
                ctx.emit("scalar", mkrstd2, se)
                ctx.wait("vector", se)
                ctx.wait("vector", ve, lw_ve)
                def mkapply2(v):
                    v.tensor_scalar(stage[0][:, :], h2t[:, :], stats[:, 3:4],
                                    stats[:, 6:7], ALU.add, ALU.mult)
                    for _ in range(3):
                        v.engine_nop()
                    v.tensor_mul(stage[0][:, :], stage[0][:, :], lgb[:, 0:1024])
                    for _ in range(3):
                        v.engine_nop()
                    return v.tensor_add(h2n[:, :], stage[0][:, :], lgb[:, 1024:2048])
                ctx.emit("vector", mkapply2, ve)
                h2n_ve = ve.n
                ctx.wait("tensor", ve, h2n_ve)
                ctx.wait("tensor", ve, poolB_free)
                def mktr3(te):
                    for c in range(8):
                        inst = te.transpose(poolB[:, c * 128:(c + 1) * 128],
                                            h2n[:, c * 128:(c + 1) * 128], identbf[:, :])
                    return inst
                ctx.emit("tensor", mktr3, pe)
                ctx.wait("vector", pe)
                def mkevh(tt):
                    def fn(v):
                        for c in range(8):
                            inst = v.tensor_copy(
                                h2nfm[:, c * 512 + tt * 128:c * 512 + (tt + 1) * 128],
                                poolB[:, c * 128:(c + 1) * 128])
                        return inst
                    return fn
                poolB_free = ctx.emit("vector", mkevh(tt), ve)
            h2nfm_ve = ve.n
            for j in range(6):
                ws = j % 2
                ctx.wait("gpsimd", pe, wsl_pe[ws])
                ctx.wait("gpsimd", ch_st)
                ctx.emit("gpsimd", lambda g, j=j, ws=ws, w=w: g.dma_start(
                    wsl[ws][:, 0:4096].rearrange("p (c t) -> p c t", c=8),
                    w.w1a[j].rearrange("c p t -> p c t")), ch_w[ws], 16)
                ctx.emit("gpsimd", lambda g, j=j, ws=ws, w=w: g.dma_start(
                    wsl[ws][:, 4096:8192].rearrange("p (c t) -> p c t", c=8),
                    w.w1g[j].rearrange("c p t -> p c t")), ch_w[ws], 16)
                wt = ctx.emit("gpsimd", lambda g, j=j, ws=ws, w=w: g.dma_start(
                    wsl[ws][:, 8192:12288].rearrange("p (c t) -> p c t", c=4),
                    w.w2[j].rearrange("c p t -> p c t")), ch_w[ws], 16)
                for tt in range(4):
                    ctx.wait("tensor", ch_w[ws], wt)
                    ctx.wait("tensor", ve, h2nfm_ve)
                    ctx.wait("tensor", ve, pool_free[2])
                    ctx.wait("tensor", ve, pool_free[3])
                    ctx.wait("tensor", ch_lw, lw_gp)
                    ctx.wait("tensor", ve, ones_ve)
                    def mkag(j, ws, tt):
                        def fn(te):
                            te.matmul(pool[2][:, 0:512], ones1[0:1, :],
                                      rows_b[0:1, j * 512:(j + 1) * 512],
                                      start=True, stop=False)
                            for c in range(nd):
                                te.matmul(
                                    pool[2][:, 0:512],
                                    h2nfm[:, c * 512 + tt * 128:c * 512 + (tt + 1) * 128],
                                    wsl[ws][:, c * 512:(c + 1) * 512],
                                    start=False, stop=(c == nd - 1))
                            te.matmul(pool[3][:, 0:512], ones1[0:1, :],
                                      rows_b[0:1, 3072 + j * 512:3072 + (j + 1) * 512],
                                      start=True, stop=False)
                            for c in range(nd):
                                inst = te.matmul(
                                    pool[3][:, 0:512],
                                    h2nfm[:, c * 512 + tt * 128:c * 512 + (tt + 1) * 128],
                                    wsl[ws][:, 4096 + c * 512:4096 + (c + 1) * 512],
                                    start=False, stop=(c == nd - 1))
                            return inst
                        return fn
                    ctx.emit("tensor", mkag(j, ws, tt), pe)
                    ctx.wait("scalar", pe)
                    ctx.emit("scalar", lambda sc: sc.activation(
                        gbuf[:, :], pool[3][:, 0:512], AF.Gelu), se)
                    ctx.wait("vector", se)
                    ctx.emit("vector", lambda v: v.tensor_mul(
                        hp_tm[:, :], pool[2][:, 0:512], gbuf[:, :]), ve)
                    pool_free[2] = ve.n
                    pool_free[3] = ve.n
                    hp_ve = ve.n
                    ctx.wait("tensor", ve, hp_ve)
                    ctx.wait("tensor", ve, poolB_free)
                    def mktr4(te):
                        for c in range(4):
                            inst = te.transpose(poolB[:, c * 128:(c + 1) * 128],
                                                hp_tm[:, c * 128:(c + 1) * 128],
                                                identbf[:, :])
                        return inst
                    ctx.emit("tensor", mktr4, pe)
                    ctx.wait("vector", pe)
                    poolB_free = ctx.emit("vector", lambda v: v.tensor_copy(
                        hp_fm[:, 0:512], poolB[:, 0:512]), ve)
                    hpf_ve = ve.n
                    for dh in range(2):
                        pi = 4 + dh
                        ctx.wait("tensor", ve, hpf_ve)
                        ctx.wait("tensor", ve, pool_free[pi])
                        def mkw2(j, ws, dh, pi):
                            def fn(te):
                                if j == 0:
                                    te.matmul(pool[pi][:, 0:512], ones1[0:1, :],
                                              rows_b[0:1, 6144 + dh * 512:6144 + (dh + 1) * 512],
                                              start=True, stop=False)
                                inst = None
                                for c in range(4):
                                    inst = te.matmul(
                                        pool[pi][:, 0:512],
                                        hp_fm[:, c * 128:(c + 1) * 128],
                                        wsl[ws][:, 8192 + c * 1024 + dh * 512:
                                                8192 + c * 1024 + (dh + 1) * 512],
                                        start=(c == 0 and j != 0), stop=(c == 3))
                                return inst
                            return fn
                        ctx.emit("tensor", mkw2(j, ws, dh, pi), pe)
                        wsl_pe[ws] = pe.n
                        ctx.wait("vector", pe)
                        if j == 0:
                            ctx.wait("vector", ch_rx1, rx1t)
                            ctx.wait("vector", gst, gst_guard)
                            def mkx2i(tt, dh, pi):
                                def fn(v):
                                    return v.tensor_add(
                                        x2acc[:, tt * 1024 + dh * 512:tt * 1024 + (dh + 1) * 512],
                                        pool[pi][:, 0:512],
                                        rx1tm[:, tt * 1024 + dh * 512:tt * 1024 + (dh + 1) * 512])
                                return fn
                            pool_free[pi] = ctx.emit("vector", mkx2i(tt, dh, pi), ve)
                        else:
                            def mkx2a(tt, dh, pi):
                                def fn(v):
                                    dst = x2acc[:, tt * 1024 + dh * 512:tt * 1024 + (dh + 1) * 512]
                                    return v.scalar_tensor_tensor(
                                        dst, pool[pi][:, 0:512], 1.0, dst,
                                        ALU.mult, ALU.add)
                                return fn
                            pool_free[pi] = ctx.emit("vector", mkx2a(tt, dh, pi), ve)
            rx1_ve = ve.n
            for tt in range(4):
                gt = th * 4 + tt
                if l == 0:
                    ctx.emit("vector", lambda v, tt=tt: v.tensor_copy(
                        x2bf[:, :], x2acc[:, tt * 1024:(tt + 1) * 1024]), ve)
                    x2b_ve = ve.n
                    ctx.wait("tensor", ve, x2b_ve)
                    ctx.wait("tensor", ve, poolB_free)
                    def mktr5(te):
                        for c in range(8):
                            inst = te.transpose(poolB[:, c * 128:(c + 1) * 128],
                                                x2bf[:, c * 128:(c + 1) * 128],
                                                identbf[:, :])
                        return inst
                    ctx.emit("tensor", mktr5, pe)
                    ctx.wait("vector", pe)
                    bi = tt % 2
                    ctx.wait("vector", ch_st, stg_fm_tgt[bi])
                    poolB_free = ctx.emit("vector", lambda v, bi=bi: v.tensor_copy(
                        stg_fm[bi][:, :], poolB[:, :]), ve)
                    ctx.wait("sync", ve)
                    stg_fm_tgt[bi] = ctx.emit("sync",
                        lambda s2, tb=tb, off=off, tt=tt, bi=bi: s2.dma_start(
                            x2fm_d[tb, :, :, off + tt * 128:off + (tt + 1) * 128]
                            .rearrange("c p t -> p c t"),
                            stg_fm[bi][:, :].rearrange("p (c t) -> p c t", c=8)),
                        ch_st, 16)
                else:
                    ctx.wait("sync", ve, xaccp_ve)
                    xat = ctx.emit("sync", lambda s2, gt=gt: s2.dma_start(
                        xaccp_sb[:, :], xaccp_d[gt]), ch_xa, 16)
                    ctx.wait("vector", ch_xa, xat)
                    def mkout(tt):
                        def fn(v):
                            return v.tensor_add(x2acc[:, tt * 1024:(tt + 1) * 1024],
                                                x2acc[:, tt * 1024:(tt + 1) * 1024],
                                                xaccp_sb[:, :])
                        return fn
                    ctx.emit("vector", mkout(tt), ve)
                    xaccp_ve = ve.n
                    ctx.wait("gpsimd", ve)
                    ctx.emit("gpsimd", lambda g, gt=gt, tt=tt: g.dma_start(
                        out_d[gt], x2acc[:, tt * 1024:(tt + 1) * 1024]), gst, 16)
            gst_guard = gst.n
        layer_pe = pe.n
    ctx.wait("gpsimd", gst)
    ctx.build()
    return nc


import ml_dtypes
from concourse.bass_utils import run_bass_kernel_spmd

D_ = 1024
ND_ = 8
NT_ = 32
T_ = 4096
TBK = 1024
NSEQ = 2
SEQ_TB = 2


def build_l0rest():
    nc = bass.Bass("TRN2", target_bir_lowering=False)
    ctx = Ctx(nc)
    x_d = nc.dram_tensor("x", [NT_, 128, D_], F32, kind="ExternalInput").ap()
    y_d = nc.dram_tensor("y", [NT_, 128, D_], BF16, kind="ExternalInput").ap()
    id_d = nc.dram_tensor("ident", [128, 128], F32, kind="ExternalInput").ap()
    gw_d = nc.dram_tensor("gwp", [128, ND_ * 8], F32, kind="ExternalInput").ap()
    wl_d = nc.dram_tensor("wl_p", [8, 128, 1024], F32, kind="ExternalInput").ap()
    la_d = nc.dram_tensor("la_p", [8, 128, 32], F32, kind="ExternalInput").ap()
    ra_d = nc.dram_tensor("ra_p", [32, 1024], F32, kind="ExternalInput").ap()
    bl_d = nc.dram_tensor("blin_p", [1, D_], F32, kind="ExternalInput").ap()
    out_d = nc.dram_tensor("out", [NT_, 128, D_], F32, kind="ExternalOutput").ap()
    xnfm_d = nc.dram_tensor("xnfm", [T_ // TBK, ND_, 128, TBK], BF16, kind="Internal").ap()

    b = alloc_common(nc, ctx, NT_, D_, 8, 8, TBK)
    b.xnfm = [nc.alloc_sbuf_tensor(f"xnfm{i}", [128, ND_ * TBK], BF16) for i in range(2)]
    b.w1r = [nc.alloc_sbuf_tensor(f"w1r{i}", [128, ND_ * 128], BF16) for i in range(b.NWR)]
    b.wur = [nc.alloc_sbuf_tensor(f"wur{i}", [128, ND_ * 128], BF16) for i in range(b.NWR)]
    b.w2r = [nc.alloc_sbuf_tensor(f"w2r{i}", [128, D_], BF16) for i in range(b.NWR)]
    b.b2r = [nc.alloc_sbuf_tensor("b2r0", [1, D_], BF16)]
    b.ch_xnfm = [ctx.chan(f"c_xnf{i}") for i in range(2)]
    b.ch_w1 = [ctx.chan(f"c_w1_{i}") for i in range(b.NWR)]
    b.ch_wu = [ctx.chan(f"c_wu_{i}") for i in range(b.NWR)]
    b.ch_w2 = [ctx.chan(f"c_w2_{i}") for i in range(b.NWR)]
    b.ch_b2 = [ctx.chan("c_b2_0")]
    gw32 = nc.alloc_sbuf_tensor("gw32", [128, ND_ * 8], F32)

    emit_consts(b, id_d)
    alloc_attn(b, id_d)
    ctx.emit("sync", lambda s: s.dma_start(gw32[:, :], gw_d), b.ch_cst[0], 16)
    prologue(b, x_d, xnfm_d, gw32=gw32, gw_E=8, ln=True)
    softmax_top2(b, 8, renorm=False)
    cfg = {"wl_d": wl_d, "la_d": la_d, "ra_d": ra_d, "blin_d": bl_d,
           "xnfm_d": xnfm_d, "y_d": y_d}
    emit_l0ffn(b, cfg)
    ctx.wait("gpsimd", b.ve)
    for t in range(0, NT_, 8):
        ctx.emit("gpsimd", lambda g, t=t: g.dma_start(
            out_d[t:t + 8].rearrange("a p d -> p a d"),
            b.xacc[:, t * D_:(t + 8) * D_].rearrange("p (a d) -> p a d", d=D_)),
            b.gst, 16)
    ctx.wait("gpsimd", b.gst)
    ctx.build()
    return nc


def build_l1():
    nc = bass.Bass("TRN2", target_bir_lowering=False)
    ctx = Ctx(nc)
    x_d = nc.dram_tensor("x", [NT_, 128, D_], F32, kind="ExternalInput").ap()
    id_d = nc.dram_tensor("ident", [128, 128], F32, kind="ExternalInput").ap()
    qk_d = nc.dram_tensor("qk_p", [16, 128, 1024], F32, kind="ExternalInput").ap()
    v_d = nc.dram_tensor("v_p", [16, 128, 512], F32, kind="ExternalInput").ap()
    wo_d = nc.dram_tensor("wo_p", [8, 2, 64, 1024], F32, kind="ExternalInput").ap()
    bqk_d = nc.dram_tensor("bqk_p", [128, 32], F32, kind="ExternalInput").ap()
    bv_d = nc.dram_tensor("bv_p", [1, 1024], F32, kind="ExternalInput").ap()
    bo_d = nc.dram_tensor("bo_p", [128, 8], F32, kind="ExternalInput").ap()
    mk_d = nc.dram_tensor("masks_p", [128, 256], BF16, kind="ExternalInput").ap()
    gwr_d = nc.dram_tensor("gwr", [128, ND_ * 8], F32, kind="ExternalInput").ap()
    w1_d = nc.dram_tensor("w1L", [8, 32, 128, ND_ * 128], F32, kind="ExternalInput").ap()
    w2_d = nc.dram_tensor("w2R", [8, 32, 128, D_], F32, kind="ExternalInput").ap()
    b1_dd = nc.dram_tensor("b1p", [8, 128, 32], F32, kind="ExternalInput").ap()
    b2_dd = nc.dram_tensor("b2p", [1, 8 * D_], F32, kind="ExternalInput").ap()
    out_d = nc.dram_tensor("out", [NT_, 128, D_], F32, kind="ExternalOutput").ap()
    xnfm_d = nc.dram_tensor("xnfm", [T_ // TBK, ND_, 128, TBK], BF16, kind="Internal").ap()

    b = alloc_common(nc, ctx, NT_, D_, 8, 32, TBK)
    b.NWR = 2
    b.xnfm = [nc.alloc_sbuf_tensor(f"xnfm{i}", [128, ND_ * TBK], BF16) for i in range(2)]
    b.w1r = [nc.alloc_sbuf_tensor(f"w1r{i}", [128, ND_ * 128], BF16) for i in range(b.NWR)]
    b.wur = [nc.alloc_sbuf_tensor(f"wur{i}", [128, ND_ * 128], BF16) for i in range(b.NWR)]
    b.w2r = [nc.alloc_sbuf_tensor(f"w2r{i}", [128, D_], BF16) for i in range(b.NWR)]
    b.b2r = [nc.alloc_sbuf_tensor("b2r0", [1, D_], BF16)]
    b.ch_xnfm = [ctx.chan(f"c_xnf{i}") for i in range(2)]
    b.ch_w1 = [ctx.chan(f"c_w1_{i}") for i in range(b.NWR)]
    b.ch_wu = [ctx.chan(f"c_wu_{i}") for i in range(b.NWR)]
    b.ch_w2 = [ctx.chan(f"c_w2_{i}") for i in range(b.NWR)]
    b.ch_b2 = [ctx.chan("c_b2_0")]
    gw32 = nc.alloc_sbuf_tensor("gw32", [128, ND_ * 8], F32)
    b1sb = nc.alloc_sbuf_tensor("b1sb", [128, 8 * 32], F32)

    emit_consts(b, id_d)
    alloc_attn(b, id_d)
    ctx.emit("sync", lambda s: s.dma_start(gw32[:, :], gwr_d), b.ch_cst[0], 16)
    ctx.emit("gpsimd", lambda g: g.dma_start(
        b1sb[:, :].rearrange("p (e f) -> p e f", f=32),
        b1_dd.rearrange("e p f -> p e f")), b.ch_cstg[0], 16)
    prologue(b, x_d, xnfm_d, gw32=gw32, gw_E=8, ln=True)
    softmax_top2(b, 8, renorm=True)
    acfg = {"mode": "alibi", "qk_d": qk_d, "v_d": v_d, "wo_d": wo_d,
            "bqk_d": bqk_d, "bv_d": bv_d, "bo_d": bo_d, "masks_d": mk_d,
            "xnfm_d": xnfm_d, "n_seq": NSEQ, "seq_tb": SEQ_TB}
    emit_attn(b, acfg)
    mcfg = {"E": 8, "H": 4096, "xnfm_d": xnfm_d, "first_acc": False,
            "act": "relu_bias", "w1_d": w1_d, "w2_d": w2_d, "b2_d": b2_dd,
            "b1sb": b1sb, "ch_b1": b.ch_cstg[0]}
    moe_loop(b, mcfg)
    ctx.wait("gpsimd", b.ve)
    for t in range(0, NT_, 8):
        ctx.emit("gpsimd", lambda g, t=t: g.dma_start(
            out_d[t:t + 8].rearrange("a p d -> p a d"),
            b.xacc[:, t * D_:(t + 8) * D_].rearrange("p (a d) -> p a d", d=D_)),
            b.gst, 16)
    ctx.wait("gpsimd", b.gst)
    ctx.build()
    return nc


def build_l2a():
    nc = bass.Bass("TRN2", target_bir_lowering=False)
    ctx = Ctx(nc)
    x_d = nc.dram_tensor("x", [NT_, 128, D_], F32, kind="ExternalInput").ap()
    id_d = nc.dram_tensor("ident", [128, 128], F32, kind="ExternalInput").ap()
    qk_d = nc.dram_tensor("qk_p", [16, 128, 1024], F32, kind="ExternalInput").ap()
    v_d = nc.dram_tensor("v_p", [16, 128, 512], F32, kind="ExternalInput").ap()
    wof_d = nc.dram_tensor("wof_p", [8, 2, 64, 1024], F32, kind="ExternalInput").ap()
    wob_d = nc.dram_tensor("wob_p", [8, 2, 64, 1024], F32, kind="ExternalInput").ap()
    wf_d = nc.dram_tensor("wf_p", [128, 128], F32, kind="ExternalInput").ap()
    bf_d = nc.dram_tensor("bf_p", [16, 1], F32, kind="ExternalInput").ap()
    mk_d = nc.dram_tensor("masks_p", [128, 256], BF16, kind="ExternalInput").ap()
    out_d = nc.dram_tensor("out", [NT_, 128, D_], F32, kind="ExternalOutput").ap()
    xnfm_d = nc.dram_tensor("xnfm", [T_ // TBK, ND_, 128, TBK], BF16,
                            kind="ExternalOutput").ap()
    gc_d = nc.dram_tensor("gcd", [NSEQ, 16, 2048], F32, kind="Internal").ap()
    u_d = nc.dram_tensor("ud", [NSEQ, 16, 2048], F32, kind="Internal").ap()

    b = alloc_common(nc, ctx, NT_, D_, 8, 8, TBK)
    b.xnfm = [nc.alloc_sbuf_tensor(f"xnfm{i}", [128, ND_ * TBK], BF16) for i in range(2)]
    b.w1r = [nc.alloc_sbuf_tensor(f"w1r{i}", [128, ND_ * 128], BF16) for i in range(b.NWR)]
    b.wur = [nc.alloc_sbuf_tensor(f"wur{i}", [128, ND_ * 128], BF16) for i in range(b.NWR)]
    b.w2r = [nc.alloc_sbuf_tensor(f"w2r{i}", [128, D_], BF16) for i in range(b.NWR)]
    b.b2r = [nc.alloc_sbuf_tensor("b2r0", [1, D_], BF16)]
    b.ch_xnfm = [ctx.chan(f"c_xnf{i}") for i in range(2)]
    b.ch_w1 = [ctx.chan(f"c_w1_{i}") for i in range(b.NWR)]
    b.ch_wu = [ctx.chan(f"c_wu_{i}") for i in range(b.NWR)]
    b.ch_w2 = [ctx.chan(f"c_w2_{i}") for i in range(b.NWR)]
    b.ch_b2 = [ctx.chan("c_b2_0")]

    emit_consts(b, id_d)
    alloc_attn(b, id_d)
    prologue(b, x_d, xnfm_d, gw32=None, ln=True)
    cfg = {"mode": "fgate", "qk_d": qk_d, "v_d": v_d,
           "wo_f_d": wof_d, "wo_b_d": wob_d, "wf_d": wf_d, "bf_d": bf_d,
           "masks_d": mk_d, "gc_dram": gc_d, "u_dram": u_d,
           "xnfm_d": xnfm_d, "n_seq": NSEQ, "seq_tb": SEQ_TB}
    emit_attn(b, cfg)
    ctx.wait("gpsimd", b.ve)
    for t in range(0, NT_, 8):
        ctx.emit("gpsimd", lambda g, t=t: g.dma_start(
            out_d[t:t + 8].rearrange("a p d -> p a d"),
            b.xacc[:, t * D_:(t + 8) * D_].rearrange("p (a d) -> p a d", d=D_)),
            b.gst, 16)
    ctx.wait("gpsimd", b.gst)
    ctx.build()
    return nc


def build_l3():
    nc = bass.Bass("TRN2", target_bir_lowering=False)
    ctx = Ctx(nc)
    x_d = nc.dram_tensor("x", [NT_, 128, D_], F32, kind="ExternalInput").ap()
    id_d = nc.dram_tensor("ident", [128, 128], F32, kind="ExternalInput").ap()
    qk_d = nc.dram_tensor("qk_p", [16, 128, 1024], F32, kind="ExternalInput").ap()
    v_d = nc.dram_tensor("v_p", [16, 128, 512], F32, kind="ExternalInput").ap()
    wo_d = nc.dram_tensor("wo_p", [8, 2, 64, 1024], F32, kind="ExternalInput").ap()
    bqkv_d = nc.dram_tensor("bqkv_p", [64, 48], F32, kind="ExternalInput").ap()
    bo_d = nc.dram_tensor("bo_p", [128, 8], F32, kind="ExternalInput").ap()
    gwr_d = nc.dram_tensor("gwr", [128, ND_ * 8], F32, kind="ExternalInput").ap()
    g_dd = nc.dram_tensor("gL", [8, 32, 128, ND_ * 128], F32, kind="ExternalInput").ap()
    u_dd = nc.dram_tensor("uL", [8, 32, 128, ND_ * 128], F32, kind="ExternalInput").ap()
    d_dd = nc.dram_tensor("dR", [8, 32, 128, D_], F32, kind="ExternalInput").ap()
    out_d = nc.dram_tensor("out", [NT_, 128, D_], F32, kind="ExternalOutput").ap()
    xnfm_d = nc.dram_tensor("xnfm", [T_ // TBK, ND_, 128, TBK], BF16, kind="Internal").ap()

    b = alloc_common(nc, ctx, NT_, D_, 8, 32, TBK)
    b.NWR = 2
    b.xnfm = [nc.alloc_sbuf_tensor(f"xnfm{i}", [128, ND_ * TBK], BF16) for i in range(2)]
    b.w1r = [nc.alloc_sbuf_tensor(f"w1r{i}", [128, ND_ * 128], BF16) for i in range(b.NWR)]
    b.wur = [nc.alloc_sbuf_tensor(f"wur{i}", [128, ND_ * 128], BF16) for i in range(b.NWR)]
    b.w2r = [nc.alloc_sbuf_tensor(f"w2r{i}", [128, D_], BF16) for i in range(b.NWR)]
    b.ch_xnfm = [ctx.chan(f"c_xnf{i}") for i in range(2)]
    b.ch_w1 = [ctx.chan(f"c_w1_{i}") for i in range(b.NWR)]
    b.ch_wu = [ctx.chan(f"c_wu_{i}") for i in range(b.NWR)]
    b.ch_w2 = [ctx.chan(f"c_w2_{i}") for i in range(b.NWR)]
    gw32 = nc.alloc_sbuf_tensor("gw32", [128, ND_ * 8], F32)

    emit_consts(b, id_d)
    ctx.emit("sync", lambda s: s.dma_start(gw32[:, :], gwr_d), b.ch_cst[0], 16)
    prologue(b, x_d, xnfm_d, gw32=gw32, gw_E=8, ln=True)
    softmax_top2(b, 8, renorm=True)
    acfg = {"qk_d": qk_d, "v_d": v_d, "wo_d": wo_d, "bqkv_d": bqkv_d, "bo_d": bo_d,
            "xnfm_d": xnfm_d, "n_seq": NSEQ, "seq_tb": SEQ_TB}
    emit_l3attn(b, acfg)
    mcfg = {"E": 8, "H": 4096, "xnfm_d": xnfm_d, "first_acc": False,
            "act": "swiglu", "g_d": g_dd, "u_d": u_dd, "w2_d": d_dd}
    moe_loop(b, mcfg)
    ctx.wait("gpsimd", b.ve)
    for t in range(0, NT_, 8):
        ctx.emit("gpsimd", lambda g, t=t: g.dma_start(
            out_d[t:t + 8].rearrange("a p d -> p a d"),
            b.xacc[:, t * D_:(t + 8) * D_].rearrange("p (a d) -> p a d", d=D_)),
            b.gst, 16)
    ctx.wait("gpsimd", b.gst)
    ctx.build()
    return nc


_PROGS = {}


def _get_prog(name):
    if name not in _PROGS:
        _PROGS[name] = {"A": lambda: build_l0attn(S_len=2048, B=16, D=1024),
                        "B": build_l0rest, "C": build_l1, "D1": build_l2a,
                        "D2": lambda: build_l2ffn(T=T_, SL=2048),
                        "E": build_l3}[name]()
    return _PROGS[name]


def kernel(**inputs):
    gi = lambda k: np.asarray(inputs[k])
    x = gi("L0_attn_x")
    ident = np.eye(128, dtype=np.float32)
    cores = list(range(8))
    bf16 = ml_dtypes.bfloat16

    W0 = gi("L0_attn_W0"); Wkv = gi("L0_attn_Wkv"); bkv = gi("L0_attn_bkv")
    eta = np.asarray(inputs["L0_attn_eta"], dtype=np.float32).reshape(1, 1)
    xs_sbd = np.ascontiguousarray(x.transpose(1, 0, 2))
    m64 = host_masks128()
    imA = []
    for i in cores:
        e0 = i * 128
        im = {"xs": xs_sbd, "eta": eta, "m64": m64, "ident": ident}
        im.update(host_pack_l0attn(
            W0[e0:e0 + 128, :],
            np.ascontiguousarray(Wkv[1024 + e0:1024 + e0 + 128, :].T),
            bkv[1024 + e0:1024 + e0 + 128]))
        imA.append(im)
    rA = run_bass_kernel_spmd(_get_prog("A"), imA, core_ids=cores)
    y_full = np.concatenate([np.asarray(rA.results[i]["y"]) for i in cores], axis=2)

    gwp = np.ascontiguousarray(
        gi("L0_ffn_gate_w").T.reshape(ND_, 128, 8).transpose(1, 0, 2)).reshape(128, ND_ * 8)
    l0f = host_pack_l0ffn(gi("L0_ffn_W_lin"), gi("L0_ffn_b_lin"),
                          gi("L0_ffn_left"), gi("L0_ffn_right"))
    imB = []
    for i in cores:
        xc = x[2 * i:2 * i + 2].reshape(T_, D_)
        yc = np.ascontiguousarray(
            y_full[:, 2 * i:2 * i + 2, :].transpose(1, 0, 2)).reshape(T_, D_)
        im = {"x": tm_view(xc), "y": tm_view(yc), "ident": ident, "gwp": gwp}
        im.update(l0f)
        imB.append(im)
    rB = run_bass_kernel_spmd(_get_prog("B"), imB, core_ids=cores)

    w_in = gi("L1_attn_w_in"); b_in = gi("L1_attn_b_in")
    w_out = gi("L1_attn_w_out"); b_out = gi("L1_attn_b_out")
    wq = np.ascontiguousarray(w_in[:, 0:1024]).reshape(D_, 16, 64)
    wk = np.ascontiguousarray(w_in[:, 1024:2048]).reshape(D_, 16, 64)
    wv = np.ascontiguousarray(w_in[:, 2048:3072]).reshape(D_, 16, 64)
    bqk_p = np.zeros((128, 32), np.float32)
    bqk_p[64:128, 0:16] = b_in[0:1024].reshape(16, 64).T
    bqk_p[64:128, 16:32] = b_in[1024:2048].reshape(16, 64).T
    gwr1 = np.ascontiguousarray(
        gi("L1_ffn_gate_w").T.reshape(ND_, 128, 8).transpose(1, 0, 2)).reshape(128, ND_ * 8)
    w1 = gi("L1_ffn_w1"); b1 = gi("L1_ffn_b1"); w2 = gi("L1_ffn_w2"); b2 = gi("L1_ffn_b2")
    shared_C = {
        "ident": ident,
        "qk_p": host_pack_headproj(wq, wk), "v_p": host_pack_vslab(wv),
        "wo_p": host_pack_wo(w_out), "bqk_p": bqk_p,
        "bv_p": np.ascontiguousarray(b_in[2048:3072].reshape(1, 1024)),
        "bo_p": np.ascontiguousarray(b_out.reshape(8, 128).T),
        "masks_p": host_masks().astype(bf16),
        "gwr": gwr1,
        "w1L": np.stack([pack_lhsT(w1[e].T) for e in range(8)]),
        "w2R": np.stack([pack_rhs(w2[e].T) for e in range(8)]),
        "b1p": np.ascontiguousarray(b1.reshape(8, 32, 128).transpose(0, 2, 1)),
        "b2p": np.ascontiguousarray(b2.reshape(1, 8 * D_)),
    }
    imC = [dict(shared_C, x=np.asarray(rB.results[i]["out"])) for i in cores]
    rC = run_bass_kernel_spmd(_get_prog("C"), imC, core_ids=cores)

    WoT = np.ascontiguousarray(gi("L2_attn_Wo").T)
    shared_D1 = {
        "ident": ident,
        "qk_p": host_pack_headproj(
            np.ascontiguousarray(gi("L2_attn_Wq").T).reshape(D_, 16, 64),
            np.ascontiguousarray(gi("L2_attn_Wk").T).reshape(D_, 16, 64)),
        "v_p": host_pack_vslab(
            np.ascontiguousarray(gi("L2_attn_Wv").T).reshape(D_, 16, 64)),
        "wof_p": host_pack_wo(WoT[0:1024]),
        "wob_p": host_pack_wo(WoT[1024:2048]),
        "wf_p": np.ascontiguousarray(
            gi("L2_attn_Wf").T.reshape(8, 128, 16).transpose(1, 0, 2).reshape(128, 128)),
        "bf_p": np.ascontiguousarray(gi("L2_attn_bf").reshape(16, 1)),
        "masks_p": host_masks().astype(bf16),
    }
    imD1 = [dict(shared_D1, x=np.asarray(rC.results[i]["out"])) for i in cores]
    rD1 = run_bass_kernel_spmd(_get_prog("D1"), imD1, core_ids=cores)

    wqkv2 = gi("L2_ffn_wqkv"); qs2 = gi("L2_ffn_q_scale"); ks2 = gi("L2_ffn_k_scale")
    wo2 = gi("L2_ffn_wo"); lg2 = gi("L2_ffn_ln_g"); lb2 = gi("L2_ffn_ln_b")
    w12 = gi("L2_ffn_w1"); b12 = gi("L2_ffn_b1")
    w22 = gi("L2_ffn_w2"); b22 = gi("L2_ffn_b2")
    shared_D2 = {"rot": host_rotary(2048), "ident": ident,
                 "masks_p": host_masks().astype(bf16)}
    for l in range(2):
        pk = host_pack_l2ffn(wqkv2[l], qs2[l], ks2[l], wo2[l], lg2[l], lb2[l],
                             w12[l], b12[l], w22[l], b22[l])
        for k2, v2 in pk.items():
            shared_D2[f"{k2}_{l}"] = v2
    imD2 = [dict(shared_D2, xnfm=np.asarray(rD1.results[i]["xnfm"]),
                 xaccp=np.asarray(rD1.results[i]["out"])) for i in cores]
    rD2 = run_bass_kernel_spmd(_get_prog("D2"), imD2, core_ids=cores)

    gwr3 = np.ascontiguousarray(
        gi("L3_ffn_router_w").T.reshape(ND_, 128, 8).transpose(1, 0, 2)).reshape(128, ND_ * 8)
    gw3 = gi("L3_ffn_gate_w"); uw3 = gi("L3_ffn_up_w"); dw3 = gi("L3_ffn_down_w")
    shared_E = {"ident": ident, "gwr": gwr3}
    shared_E.update(host_pack_l3attn(
        gi("L3_attn_wq"), gi("L3_attn_bq"), gi("L3_attn_wk"), gi("L3_attn_bk"),
        gi("L3_attn_wv"), gi("L3_attn_bv"), gi("L3_attn_wo"), gi("L3_attn_bo")))
    shared_E["gL"] = np.stack([pack_lhsT(gw3[e].T) for e in range(8)])
    shared_E["uL"] = np.stack([pack_lhsT(uw3[e].T) for e in range(8)])
    shared_E["dR"] = np.stack([pack_rhs(dw3[e].T) for e in range(8)])
    imE = [dict(shared_E, x=np.asarray(rD2.results[i]["out"])) for i in cores]
    rE = run_bass_kernel_spmd(_get_prog("E"), imE, core_ids=cores)

    out = np.empty((16, 2048, 1024), np.float32)
    for i in cores:
        out[2 * i:2 * i + 2] = np.asarray(
            rE.results[i]["out"], dtype=np.float32).reshape(2, 2048, 1024)
    return out
